# version 05
# baseline (speedup 1.0000x reference)
import numpy as np
import ml_dtypes
import concourse.bass as bass
import concourse.mybir as mybir
from concourse.bass_utils import run_bass_kernel_spmd
from concourse.alu_op_type import AluOpType
from contextlib import ExitStack

F32 = mybir.dt.float32
BF16 = mybir.dt.bfloat16
I32 = mybir.dt.int32
AF = mybir.ActivationFunctionType
AX = mybir.AxisListType

LN2 = float(np.log(2.0))
N_TOK = 4096
D = 1024
NTILES = 32
C = 64
M = 33
KLOW = 4
LHI = 29
KS = 7
NCHUNK = 64
NPAIR = 32
HID = 4096
G_CH = 8
NGRP = NCHUNK // G_CH
GW = G_CH * LHI

ENGS = ("sync", "gpsimd", "tensor", "vector", "scalar")


def make_consts():
    c = np.arange(C)
    m = np.arange(M)
    ang = 2 * np.pi * np.outer(c, m) / C
    Fc = np.cos(ang).astype(np.float32)
    Fs = (-np.sin(ang)).astype(np.float32)
    F64 = np.zeros((128, 66), np.float32)
    F64[0:64, 0:33] = Fc
    F64[0:64, 33:66] = Fs
    F64[64:128] = F64[0:64]
    Flow = np.zeros((128, 8), np.float32)
    Flow[0:64, 0:4] = Fc[:, 0:4]
    Flow[0:64, 4:8] = Fs[:, 0:4]
    Flow[64:128] = Flow[0:64]

    mm_ = np.arange(M)[:, None]
    cc_ = np.arange(C)[None, :]
    s2 = np.where((mm_ >= 1) & (mm_ <= 31), 2.0, 1.0)
    wr = (np.cos(2 * np.pi * mm_ * cc_ / C) / C * s2).astype(np.float32)
    wi = (-np.sin(2 * np.pi * mm_ * cc_ / C) / C * s2).astype(np.float32)
    wi[0] = 0.0
    wi[32] = 0.0
    Wr2 = np.zeros((66, 128), np.float32)
    Wi2 = np.zeros((66, 128), np.float32)
    Wr2[0:33, 0:64] = wr
    Wr2[33:66, 64:128] = wr
    Wi2[0:33, 0:64] = wi
    Wi2[33:66, 64:128] = wi

    cid = np.arange(256) // 4
    mask8 = np.where(cid[:, None] >= cid[None, :], 0.0, -8.0e9).astype(np.float32)

    bf = ml_dtypes.bfloat16
    return {
        "F64b": F64.astype(bf),
        "FlowB": Flow.astype(bf),
        "Wr2": Wr2.astype(bf),
        "Wi2": Wi2.astype(bf),
        "I128b": np.eye(128, dtype=bf),
        "ones2": np.ones((2, 128), dtype=bf),
        "maskc": mask8,
    }


class B:
    def __init__(self):
        self.ops = {e: [] for e in ENGS}
        self.c = {}
        self.sem = {}

    def add_sem(self, name, handle):
        self.sem[name] = handle
        self.c[name] = 0

    def inc(self, eng, make, sem, n=1):
        h = self.sem[sem]
        self.ops[eng].append(lambda e, make=make, h=h, n=n: make(e).then_inc(h, n))
        self.c[sem] += n
        return self.c[sem]

    def plain(self, eng, make):
        self.ops[eng].append(make)

    def seal(self, eng, sem, n=1):
        h = self.sem[sem]
        last = self.ops[eng].pop()
        self.ops[eng].append(lambda e, last=last, h=h, n=n:
                             last(e).then_inc(h, n))
        self.c[sem] += n
        return self.c[sem]

    def wait(self, eng, sem, val=None):
        h = self.sem[sem]
        v = self.c[sem] if val is None else val
        if v > 0:
            self.ops[eng].append(lambda e, h=h, v=v: e.wait_ge(h, v))
        return v

    def full_barrier(self):
        vals = dict(self.c)
        for e in ENGS:
            for s, v in vals.items():
                self.wait(e, s, v)


def build_program(phases=4, p2sub=9):
    nc = bass.Bass("TRN2", target_bir_lowering=False,
                   dynamic_dma_scratch_size=16384)

    def din(name, shape, dtype):
        return nc.dram_tensor(name, list(shape), dtype, kind="ExternalInput").ap()

    x_h = din("x", (N_TOK, D), F32)
    qwT_h = din("qwT", (D, D), F32)
    kwT_h = din("kwT", (D, D), F32)
    vwT_h = din("vwT", (D, D), F32)
    owT_h = din("owT", (D, D), F32)
    gwT_h = din("gwT", (D, D), F32)
    pwrT_h = din("pwrT", (D, D), F32)
    pwiT_h = din("pwiT", (D, D), F32)
    dwr_h = din("dwr", (D, KS), F32)
    dwi_h = din("dwi", (D, KS), F32)
    alpha_h = din("alpha128", (128, 1), F32)
    bfc_h = din("bfc", (1, HID), F32)
    bpj_h = din("bproj", (1, D), F32)
    wfc_h = din("wfc", (HID, D), F32)
    wpj_h = din("wproj", (D, HID), F32)
    F64b_h = din("F64b", (128, 66), BF16)
    FlowB_h = din("FlowB", (128, 8), BF16)
    Wr2_h = din("Wr2", (66, 128), BF16)
    Wi2_h = din("Wi2", (66, 128), BF16)
    I128b_h = din("I128b", (128, 128), BF16)
    ones2_h = din("ones2", (2, 128), BF16)
    maskc_h = din("maskc", (256, 256), F32)

    out_h = nc.dram_tensor("out", [N_TOK, D], F32, kind="ExternalOutput").ap()
    xnb_s = nc.dram_tensor("xnb_s", [N_TOK, D], BF16, kind="Internal").ap()
    g_s = nc.dram_tensor("g_s", [N_TOK, D], F32, kind="Internal").ap()

    esg = ExitStack()

    def sb(name, shape, dtype, stack):
        return stack.enter_context(nc.sbuf_tensor(name, list(shape), dtype))

    cst = sb("cst", (128, 8), F32, esg)
    I128b = sb("I128b_sb", (128, 128), BF16, esg)
    F64b = sb("F64b_sb", (128, 66), BF16, esg)
    FlowB = sb("FlowB_sb", (128, 8), BF16, esg)
    Wr2 = sb("Wr2_sb", (66, 128), BF16, esg)
    Wi2 = sb("Wi2_sb", (66, 128), BF16, esg)
    ones2 = sb("ones2_sb", (2, 128), BF16, esg)
    maskt = sb("maskt", (128, 512), F32, esg)
    alpha_t = sb("alpha_t", (128, 1), F32, esg)
    alpha_sg = sb("alpha_sg", (128, 1), F32, esg)
    bspl_fc = sb("bspl_fc", (2, HID), BF16, esg)
    bspl_pj = sb("bspl_pj", (2, D), BF16, esg)
    dwr_t = sb("dwr_t", (128, 8 * KS), F32, esg)
    dwi_t = sb("dwi_t", (128, 8 * KS), F32, esg)

    psA = esg.enter_context(nc.psum_tensor("psA", [128, 2048], F32))
    psB = esg.enter_context(nc.psum_tensor("psB", [128, 2048], F32))

    b = B()
    for s in ("cst", "gq", "T", "V", "S", "wst0", "wst1", "x0", "x1",
              "xnbst", "gst", "p2x0", "p2x1", "p4l0", "p4l1", "p4s0", "p4s1", "bld", "bspl"):
        b.add_sem(s, esg.enter_context(nc.semaphore("sem_" + s)))

    def bankA(k, w=512):
        return psA.ap()[:, 512 * k:512 * k + w]

    def bankB(k, w=512):
        return psB.ap()[:, 512 * k:512 * k + w]


    def _finish():
        for s in b.c:
            if s not in ("T", "V", "S"):
                b.wait("sync", s)
        with nc.Block() as block:
            @block.sync
            def _(eng):
                for fn in b.ops["sync"]:
                    fn(eng)

            @block.gpsimd
            def _(eng):
                for fn in b.ops["gpsimd"]:
                    fn(eng)

            @block.tensor
            def _(eng):
                for fn in b.ops["tensor"]:
                    fn(eng)

            @block.vector
            def _(eng):
                for fn in b.ops["vector"]:
                    fn(eng)

            @block.scalar
            def _(eng):
                for fn in b.ops["scalar"]:
                    fn(eng)
        return nc

    def ld(dst, src, sem="cst"):
        return b.inc("sync", lambda e, d=dst, s=src: e.dma_start(out=d, in_=s),
                     sem, 16)

    def ld8(dst_sb, src_h):
        for k in range(8):
            b.inc("gpsimd", lambda e, d=dst_sb.ap()[:, 1024 * k:1024 * k + 1024],
                  s=src_h[128 * k:128 * k + 128, :]: e.dma_start(out=d, in_=s),
                  "gq", 16)
        return b.c["gq"]

    ld(I128b.ap(), I128b_h)
    ld(F64b.ap(), F64b_h)
    ld(FlowB.ap(), FlowB_h)
    ld(Wr2.ap(), Wr2_h)
    ld(Wi2.ap(), Wi2_h)
    ld(ones2.ap(), ones2_h)
    ld(maskt.ap()[0:128, 0:256], maskc_h[0:128, :])
    ld(maskt.ap()[0:128, 256:512], maskc_h[128:256, :])
    ld(alpha_t.ap(), alpha_h)
    for k in range(8):
        ld(dwr_t.ap()[:, KS * k:KS * k + KS], dwr_h[128 * k:128 * k + 128, :])
        ld(dwi_t.ap()[:, KS * k:KS * k + KS], dwi_h[128 * k:128 * k + 128, :])
    consts_loaded = b.c["cst"]

    CST = {"lnscale": 0, "eps": 1, "nln2": 2, "p3ln2": 3, "pln2": 4, "n3ln2": 5,
           "smx": 6, "invln2": 7}
    cvals = [1.0 / 1024.0, 1e-5, -LN2, 3 * LN2, LN2, -3 * LN2, 0.125, 1.0 / LN2]
    for col in range(8):
        b.plain("vector", lambda e, c=col, v=cvals[col]:
                e.memset(cst.ap()[:, c:c + 1], v))
    b.seal("vector", "V")

    def ccol(name):
        i = CST[name]
        return cst.ap()[:, i:i + 1]

    b.wait("scalar", "cst", consts_loaded)
    b.wait("scalar", "V")
    b.inc("scalar", lambda e: e.activation(alpha_sg.ap(), alpha_t.ap(), AF.Sigmoid),
          "S")
    es01 = ExitStack()
    WfcqT = sb("WfcqT", (128, 8 * HID), BF16, es01)
    WpjqT = sb("WpjqT", (128, 32 * D), BF16, es01)

    tb = sb("tb", (128, 1024), F32, es01)
    ib = sb("ib", (128, 1024), I32, es01)
    amx = sb("amx", (128, 64), F32, es01)
    yq = sb("yq", (128, 64), F32, es01)
    e32q = sb("e32q", (128, 64), I32, es01)
    f0q = sb("f0q", (128, 64), F32, es01)
    gtq = sb("gtq", (128, 64), F32, es01)
    eqt = sb("eqt", (128, 64), F32, es01)
    y2q = sb("y2q", (128, 64), F32, es01)
    rsq = sb("rsq", (128, 64), F32, es01)
    scq = sb("scq", (128, 64), F32, es01)
    es_stg = ExitStack()
    wstg = sb("wstg", (128, 2048), F32, es_stg)
    wqb = sb("wqb", (128, 1024), BF16, es_stg)
    bfc_f = sb("bfc_f", (1, HID), F32, es_stg)
    bpj_f = sb("bpj_f", (1, D), F32, es_stg)
    bfc_hi = sb("bfc_hi", (1, HID), BF16, es_stg)
    bfc_lo = sb("bfc_lo", (1, HID), BF16, es_stg)
    bpj_hi = sb("bpj_hi", (1, D), BF16, es_stg)
    bpj_lo = sb("bpj_lo", (1, D), BF16, es_stg)

    bld = ld(bfc_f.ap(), bfc_h, "bld")
    bld = ld(bpj_f.ap(), bpj_h, "bld")
    b.wait("vector", "bld", bld)
    b.plain("vector", lambda e: e.tensor_copy(bfc_hi.ap(), bfc_f.ap()))
    b.plain("vector", lambda e: e.drain())
    b.plain("vector", lambda e: e.tensor_tensor(bfc_lo.ap(), bfc_f.ap(),
                                                bfc_hi.ap(), op=AluOpType.subtract))
    b.plain("vector", lambda e: e.tensor_copy(bpj_hi.ap(), bpj_f.ap()))
    b.plain("vector", lambda e: e.drain())
    b.inc("vector", lambda e: e.tensor_tensor(bpj_lo.ap(), bpj_f.ap(), bpj_hi.ap(),
                                              op=AluOpType.subtract), "V")
    b.wait("sync", "V")
    for srcv, dstv, r in ((bfc_hi, bspl_fc, 0), (bfc_lo, bspl_fc, 1),
                          (bpj_hi, bspl_pj, 0), (bpj_lo, bspl_pj, 1)):
        b.inc("sync", lambda e, s=srcv, d=dstv, r=r:
              e.dma_start(out=d.ap()[r:r + 1, :], in_=s.ap()[0:1, :]), "bspl", 16)

    def emit_quant(src_ap, nblk, dst_ap, acol):
        W = 32 * nblk
        a = amx.ap()[:, acol:acol + nblk]
        y = yq.ap()[:, acol:acol + nblk]
        y2 = y2q.ap()[:, acol:acol + nblk]
        e32 = e32q.ap()[:, acol:acol + nblk]
        f0 = f0q.ap()[:, acol:acol + nblk]
        gt = gtq.ap()[:, acol:acol + nblk]
        ee = eqt.ap()[:, acol:acol + nblk]
        rs = rsq.ap()[:, acol:acol + nblk]
        sc = scq.ap()[:, acol:acol + nblk]
        src3 = src_ap.rearrange("p (nb k) -> p nb k", k=32)
        t3 = tb.ap()[:, 0:W].rearrange("p (nb k) -> p nb k", k=32)
        i3 = ib.ap()[:, 0:W].rearrange("p (nb k) -> p nb k", k=32)
        dst3 = dst_ap.rearrange("p (nb k) -> p nb k", k=32)
        rs3 = rs.unsqueeze(2).broadcast_to([128, nblk, 32])
        sc3 = sc.unsqueeze(2).broadcast_to([128, nblk, 32])

        b.plain("vector", lambda e: e.tensor_reduce(
            a, src3, axis=AX.X, op=AluOpType.max, apply_absolute_value=True))
        vA = b.seal("vector", "V")
        b.wait("scalar", "V", vA)
        b.plain("scalar", lambda e: e.activation(y, a, AF.Ln))
        sA = b.seal("scalar", "S")
        b.wait("vector", "S", sA)
        b.plain("vector", lambda e: e.tensor_scalar_mul(y2, y, 1.0 / LN2))
        b.plain("vector", lambda e: e.drain())
        b.plain("vector", lambda e: e.tensor_scalar_add(e32, y2, 1024.0))
        b.plain("vector", lambda e: e.drain())
        b.plain("vector", lambda e: e.tensor_scalar_add(f0, e32, -1024.0))
        b.plain("vector", lambda e: e.drain())
        b.plain("vector", lambda e: e.tensor_tensor(gt, f0, y2, op=AluOpType.is_gt))
        b.plain("vector", lambda e: e.drain())
        b.plain("vector", lambda e: e.tensor_sub(ee, f0, gt))
        vB = b.seal("vector", "V")
        b.wait("scalar", "V", vB)
        b.plain("scalar", lambda e: e.activation(rs, ee, AF.Exp, scale=ccol("nln2"),
                                                 bias=ccol("p3ln2")))
        b.plain("scalar", lambda e: e.activation(sc, ee, AF.Exp, scale=ccol("pln2"),
                                                 bias=ccol("n3ln2")))
        sB = b.seal("scalar", "S")
        b.wait("vector", "S", sB)
        b.plain("vector", lambda e: e.tensor_tensor(t3, src3, rs3, op=AluOpType.mult))
        b.plain("vector", lambda e: e.drain())
        b.plain("vector", lambda e: e.tensor_copy(ib.ap()[:, 0:W], tb.ap()[:, 0:W]))
        b.plain("vector", lambda e: e.drain())
        b.plain("vector", lambda e: e.tensor_tensor(dst3, i3, sc3, op=AluOpType.mult))
        return b.seal("vector", "V")

    def emit_wtile(src_bf_ap, nblocks, dst_views, banks=(0, 1)):
        places = []
        for blk in range(nblocks):
            bank = banks[(blk // 4) % len(banks)]
            slot = blk % 4
            places.append((bank, slot))
        for blk in range(nblocks):
            bank, slot = places[blk]
            src_blk = src_bf_ap[:, 128 * blk:128 * blk + 128]
            o = psB.ap()[:, 512 * bank + 128 * slot:512 * bank + 128 * slot + 128]
            b.plain("tensor", lambda e, o=o, l=src_blk:
                    e.matmul(o, l, I128b.ap(), start=True, stop=True))
        tT = b.seal("tensor", "T")
        b.wait("scalar", "T", tT)
        for blk in range(nblocks):
            bank, slot = places[blk]
            s = psB.ap()[:, 512 * bank + 128 * slot:512 * bank + 128 * slot + 128]
            b.plain("scalar", lambda e, o=dst_views[blk], s=s: e.copy(o, s))
        return b.seal("scalar", "S")

    last_sQ = 0
    for i in range(NTILES):
        slot = (i % 2) * 1024
        b.wait("sync", "S", last_sQ)
        ldv = ld(wstg.ap()[:, slot:slot + 1024], wfc_h[128 * i:128 * i + 128, :],
                 "wst%d" % (i % 2))
        b.wait("vector", "wst%d" % (i % 2), ldv)
        emit_quant(wstg.ap()[:, slot:slot + 1024], 32, wqb.ap(), 0)
        b.wait("tensor", "V")
        if last_sQ:
            b.wait("tensor", "S", last_sQ)
        dsts = [WfcqT.ap()[:, HID * j + 128 * i:HID * j + 128 * i + 128]
                for j in range(8)]
        last_sQ = emit_wtile(wqb.ap(), 8, dsts)
    for i in range(8):
        for qv in range(4):
            slot = ((i * 4 + qv) % 2) * 1024
            b.wait("sync", "S", last_sQ)
            par = (i * 4 + qv) % 2
            ldv = ld(wstg.ap()[:, slot:slot + 1024],
                     wpj_h[128 * i:128 * i + 128, 1024 * qv:1024 * qv + 1024],
                     "wst%d" % par)
            b.wait("vector", "wst%d" % par, ldv)
            emit_quant(wstg.ap()[:, slot:slot + 1024], 32, wqb.ap(), 0)
            b.wait("tensor", "V")
            b.wait("tensor", "S", last_sQ)
            dsts = [WpjqT.ap()[:, D * (8 * qv + j) + 128 * i:
                               D * (8 * qv + j) + 128 * i + 128] for j in range(8)]
            last_sQ = emit_wtile(wqb.ap(), 8, dsts)

    es_stg.close()
    if phases <= 0:
        b.full_barrier()
        return _finish()

    xt = sb("xt", (128, 2048), F32, es01)
    xn_t = sb("xn_t", (128, 1024), F32, es01)
    xnb_t = sb("xnb_t", (128, 1024), BF16, es01)
    xq_t = sb("xq_t", (128, 1024), BF16, es01)
    xqT_t = sb("xqT_t", (128, 1024), BF16, es01)
    hb_t = sb("hb_t", (128, 1024), F32, es01)
    hq_t = sb("hq_t", (128, 1024), BF16, es01)
    hqT_t = sb("hqT_t", (128, 1024), BF16, es01)
    g_t = sb("g_t", (128, 1024), F32, es01)
    lnst = sb("lnst", (128, 8), F32, es01)

    b.full_barrier()

    chk = {}
    for i in range(NTILES):
        xslot = (i % 2) * 1024
        if i >= 2:
            b.wait("sync", "V", chk[(i - 2, "xn")])
        lx = ld(xt.ap()[:, xslot:xslot + 1024], x_h[128 * i:128 * i + 128, :],
                "x%d" % (i % 2))
        xap = xt.ap()[:, xslot:xslot + 1024]

        b.wait("vector", "x%d" % (i % 2), lx)
        b.plain("vector", lambda e, x=xap: e.reduce_sum(lnst.ap()[:, 0:1], x,
                                                        axis=AX.X))
        b.plain("vector", lambda e: e.drain())
        b.plain("vector", lambda e: e.tensor_scalar_mul(
            lnst.ap()[:, 1:2], lnst.ap()[:, 0:1], -1.0 / 1024.0))
        vln = b.seal("vector", "V")
        b.wait("scalar", "V", vln)
        b.plain("scalar", lambda e, x=xap: e.activation(
            tb.ap(), x, AF.Square, bias=lnst.ap()[:, 1:2],
            accum_out=lnst.ap()[:, 2:3]))
        b.seal("scalar", "S")
        b.wait("scalar", "S")
        b.plain("scalar", lambda e: e.activation(
            lnst.ap()[:, 3:4], lnst.ap()[:, 2:3], AF.Sqrt,
            scale=ccol("lnscale"), bias=ccol("eps")))
        sv = b.seal("scalar", "S")
        b.wait("vector", "S", sv)
        b.plain("vector", lambda e: e.reciprocal(lnst.ap()[:, 4:5],
                                                 lnst.ap()[:, 3:4]))
        b.plain("vector", lambda e: e.drain())
        b.plain("vector", lambda e: e.tensor_tensor(
            lnst.ap()[:, 5:6], lnst.ap()[:, 1:2], lnst.ap()[:, 4:5],
            op=AluOpType.mult))
        b.plain("vector", lambda e: e.drain())
        b.plain("vector", lambda e, x=xap: e.tensor_scalar(
            xn_t.ap(), x, lnst.ap()[:, 4:5], lnst.ap()[:, 5:6],
            op0=AluOpType.mult, op1=AluOpType.add))
        chk[(i, "xn")] = b.seal("vector", "V")

        b.plain("vector", lambda e: e.drain())
        if i >= 1:
            b.wait("vector", "xnbst", chk[(i - 1, "xnb_store")])
        vnb = b.inc("vector", lambda e: e.tensor_copy(xnb_t.ap(), xn_t.ap()), "V")
        b.wait("sync", "V", vnb)
        chk[(i, "xnb_store")] = b.inc("sync", lambda e, i=i: e.dma_start(
            out=xnb_s[128 * i:128 * i + 128, :], in_=xnb_t.ap()), "xnbst", 16)

        emit_quant(xn_t.ap(), 32, xq_t.ap(), 0)
        b.wait("tensor", "V")
        if i >= 1:
            b.wait("tensor", "S", chk[(i - 1, "hqT3")])
        dsts = [xqT_t.ap()[:, 128 * j:128 * j + 128] for j in range(8)]
        chk[(i, "xqT")] = emit_wtile(xq_t.ap(), 8, dsts)
        b.wait("tensor", "S", chk[(i, "xqT")])

        for w in range(4):
            hpar = w % 2
            pb = 2 * hpar
            if w >= 2:
                b.wait("tensor", "V", chk[(i, w - 2, "hq")])
            elif i >= 1:
                b.wait("tensor", "V", chk[(i - 1, w + 2, "hq")])
            if w == 0 and i >= 1:
                b.wait("tensor", "S", chk[(i - 1, "silu")])
            for K in range(8):
                lhs = xqT_t.ap()[:, 128 * K:128 * K + 128]
                for c2 in range(2):
                    rhs = WfcqT.ap()[:, HID * K + 1024 * w + 512 * c2:
                                     HID * K + 1024 * w + 512 * c2 + 512]
                    b.plain("tensor", lambda e, o=bankA(pb + c2), l=lhs, r=rhs,
                            st=(K == 0): e.matmul(o, l, r, start=st, stop=False))
            for c2 in range(2):
                rhs = bspl_fc.ap()[:, 1024 * w + 512 * c2:1024 * w + 512 * c2 + 512]
                b.plain("tensor", lambda e, o=bankA(pb + c2), r=rhs:
                        e.matmul(o, ones2.ap(), r, start=False, stop=True))
            tqh = b.seal("tensor", "T")
            hs = 0
            b.wait("scalar", "T", tqh)
            if w >= 1:
                b.wait("scalar", "V", chk[(i, w - 1, "hq")])
            elif i >= 1:
                b.wait("scalar", "V", chk[(i - 1, 3, "hq")])
            for c2 in range(2):
                b.plain("scalar", lambda e,
                        o=hb_t.ap()[:, hs + 512 * c2:hs + 512 * c2 + 512],
                        s=bankA(pb + c2): e.activation(o, s, AF.Gelu))
            sgl = b.seal("scalar", "S")
            b.wait("vector", "S", sgl)
            chk[(i, w, "hq")] = emit_quant(hb_t.ap()[:, hs:hs + 1024], 32,
                                           hq_t.ap(), 32)
            b.wait("tensor", "V", chk[(i, w, "hq")])
            hts = 0
            dsts = [hqT_t.ap()[:, hts + 128 * j:hts + 128 * j + 128]
                    for j in range(8)]
            skey = "hqT%d" % w
            chk[(i, skey)] = emit_wtile(hq_t.ap(), 8, dsts)
            b.wait("tensor", "S", chk[(i, skey)])
            for K in range(8):
                lhs = hqT_t.ap()[:, hts + 128 * K:hts + 128 * K + 128]
                for c2 in range(2):
                    rhs = WpjqT.ap()[:, D * (8 * w + K) + 512 * c2:
                                     D * (8 * w + K) + 512 * c2 + 512]
                    st = (w == 0 and K == 0)
                    b.plain("tensor", lambda e, o=bankB(2 + c2), l=lhs, r=rhs,
                            st=st: e.matmul(o, l, r, start=st, stop=False))
        for c2 in range(2):
            rhs = bspl_pj.ap()[:, 512 * c2:512 * c2 + 512]
            b.plain("tensor", lambda e, o=bankB(2 + c2), r=rhs:
                    e.matmul(o, ones2.ap(), r, start=False, stop=True))
        tf = b.seal("tensor", "T")
        b.wait("scalar", "T", tf)
        if i >= 1:
            b.wait("scalar", "gst", chk[(i - 1, "g_store")])
        for c2 in range(2):
            b.plain("scalar", lambda e, o=g_t.ap()[:, 512 * c2:512 * c2 + 512],
                    s=bankB(2 + c2): e.activation(o, s, AF.Silu))
        chk[(i, "silu")] = b.seal("scalar", "S")
        b.wait("sync", "S", chk[(i, "silu")])
        chk[(i, "g_store")] = b.inc("sync", lambda e, i=i: e.dma_start(
            out=g_s[128 * i:128 * i + 128, :], in_=g_t.ap()), "gst", 16)

    b.full_barrier()
    es01.close()
    if phases <= 1:
        return _finish()

    esM = ExitStack()
    SpecTr = sb("SpecTr", (128, 8 * NCHUNK * M), BF16, esM)
    SpecTi = sb("SpecTi", (128, 8 * NCHUNK * M), BF16, esM)

    es23 = ExitStack()
    XlTr = sb("XlTr", (128, 8 * 256), F32, es23)
    XlTi = sb("XlTi", (128, 8 * 256), F32, es23)

    es2 = ExitStack()
    GmT = sb("GmT", (128, 8 * 64), BF16, es2)
    G_T = sb("G_T", (128, 8 * 64), F32, es2)
    cpr = sb("cpr", (128, 8 * G_CH * 35), BF16, es2)
    cpi = sb("cpi", (128, 8 * G_CH * 35), BF16, es2)
    Hbt = sb("Hbt", (128, 8 * GW), BF16, es2)
    dwdg = sb("dwdg", (128, 2 * 7 * 8 * 128), BF16, es2)
    pwrT = sb("pwrT_sb", (128, 8 * D), BF16, es2)
    pwiT = sb("pwiT_sb", (128, 8 * D), BF16, es2)
    xnbp = sb("xnbp", (128, 2048), BF16, es2)

    def dwdg_ap(comp, j, dblk):
        off = 128 * (dblk + 8 * (j + 7 * comp))
        return dwdg.ap()[:, off:off + 128]

    for comp, dwt in ((0, dwr_t), (1, dwi_t)):
        for j in range(KS):
            for dblk in range(8):
                col = dwt.ap()[:, KS * dblk + j:KS * dblk + j + 1]
                b.plain("vector", lambda e, o=dwdg_ap(comp, j, dblk), c=col:
                        e.tensor_scalar(o, I128b.ap(), c, None,
                                        op0=AluOpType.mult))
    b.seal("vector", "V")

    gw_ld = ld8(pwrT, gwT_h)
    if p2sub <= 0:
        b.full_barrier()
        return _finish()

    pa_done = {}
    for p in range(NPAIR):
        xslot = (p % 2) * 1024
        if p >= 2:
            b.wait("sync", "V", pa_done[p - 2])
        lxp = ld(xnbp.ap()[:, xslot:xslot + 1024],
                 xnb_s[128 * p:128 * p + 128, :], "p2x%d" % (p % 2))
        b.wait("tensor", "p2x%d" % (p % 2), lxp)
        if p >= 1:
            b.wait("tensor", "V", pa_done[p - 1])
        for half in range(2):
            ch = 2 * p + half
            base = 64 * half
            pb = (ch % 2) * 2
            for dblk in range(8):
                lhs = xnbp.ap()[base:base + 64,
                                xslot + 128 * dblk:xslot + 128 * dblk + 128]
                rhs = F64b.ap()[base:base + 64, :]
                bk = pb + dblk // 4
                o = psA.ap()[:, 512 * bk + 128 * (dblk % 4):
                             512 * bk + 128 * (dblk % 4) + 66]
                b.plain("tensor", lambda e, o=o, l=lhs, r=rhs:
                        e.matmul(o, l, r, start=True, stop=True))
        tpa = b.seal("tensor", "T")
        b.wait("vector", "T", tpa)
        for half in range(2):
            ch = 2 * p + half
            pb = (ch % 2) * 2
            dr = XlTr.ap().rearrange("p (db t) -> p db t", t=256)
            di = XlTi.ap().rearrange("p (db t) -> p db t", t=256)
            dg = GmT.ap().rearrange("p (db c) -> p db c", c=64)
            for hb2 in range(2):
                src = psA.ap()[:, 512 * (pb + hb2):512 * (pb + hb2) + 512]
                src3 = src.rearrange("p (s w) -> p s w", w=128)
                b.plain("vector", lambda e,
                        o=dr[:, 4 * hb2:4 * hb2 + 4, 4 * ch:4 * ch + 4],
                        s=src3[:, :, 0:4]: e.tensor_copy(o, s))
                b.plain("vector", lambda e,
                        o=di[:, 4 * hb2:4 * hb2 + 4, 4 * ch:4 * ch + 4],
                        s=src3[:, :, 33:37]: e.tensor_copy(o, s))
                b.plain("vector", lambda e,
                        o=dg[:, 4 * hb2:4 * hb2 + 4, ch:ch + 1],
                        s=src3[:, :, 0:1]: e.tensor_scalar_mul(o, s, 1.0 / 64.0))
        pa_done[p] = b.seal("vector", "V")

    if p2sub <= 1:
        b.full_barrier()
        return _finish()

    b.wait("tensor", "V")
    b.wait("tensor", "gq", gw_ld)
    for mtl in range(8):
        for K in range(8):
            lhs = pwrT.ap()[:, 1024 * K + 128 * mtl:1024 * K + 128 * mtl + 128]
            rhs = GmT.ap()[:, 64 * K:64 * K + 64]
            o = psB.ap()[:, 64 * mtl:64 * mtl + 64]
            b.plain("tensor", lambda e, o=o, l=lhs, r=rhs, st=(K == 0),
                    sp=(K == 7): e.matmul(o, l, r, start=st, stop=sp))
    tg = b.seal("tensor", "T")
    b.wait("scalar", "T", tg)
    sg2 = b.inc("scalar", lambda e: e.activation(G_T.ap(), psB.ap()[:, 0:512],
                                                 AF.Sigmoid), "S")
    pw_ld = 0
    if p2sub >= 3:
        b.wait("gpsimd", "S", sg2)
        ld8(pwrT, pwrT_h)
        pw_ld = ld8(pwiT, pwiT_h)

    if p2sub <= 2:
        b.full_barrier()
        return _finish()

    for buf in (cpr, cpi):
        pad = buf.ap().rearrange("p (q w) -> p q w", w=35)[:, :, 0:6]
        b.plain("vector", lambda e, o=pad: e.memset(o, 0.0))
    vz = b.seal("vector", "V")

    b.wait("tensor", "V", vz)
    b.wait("tensor", "gq", pw_ld)
    prev_evac = pa_done[NPAIR - 1]
    grp_last = {}
    for g in range(NGRP):
        pair_evac = {}
        for pp in range(4):
            p = 4 * g + pp
            xslot = (p % 2) * 1024
            free_v = pair_evac.get(pp - 2, prev_evac)
            b.wait("sync", "V", free_v)
            lxp = ld(xnbp.ap()[:, xslot:xslot + 1024],
                     xnb_s[128 * p:128 * p + 128, :], "p2x%d" % (p % 2))
            b.wait("tensor", "p2x%d" % (p % 2), lxp)
            b.wait("tensor", "V", pair_evac.get(pp - 1, prev_evac))
            for half in range(2):
                ch = 2 * p + half
                gc = ch - G_CH * g
                base = 64 * half
                pb = (ch % 2) * 2
                for dblk in range(8):
                    lhs = xnbp.ap()[base:base + 64,
                                    xslot + 128 * dblk:xslot + 128 * dblk + 128]
                    rhs = F64b.ap()[base:base + 64, :]
                    bk = pb + dblk // 4
                    o = psA.ap()[:, 512 * bk + 128 * (dblk % 4):
                                 512 * bk + 128 * (dblk % 4) + 66]
                    b.plain("tensor", lambda e, o=o, l=lhs, r=rhs:
                            e.matmul(o, l, r, start=True, stop=True))
                trf = b.seal("tensor", "T")
                b.wait("vector", "T", trf)
                for hb2 in range(2):
                    src = psA.ap()[:, 512 * (pb + hb2):512 * (pb + hb2) + 512]
                    src3 = src.rearrange("p (s w) -> p s w", w=128)
                    for comp, buf in ((0, cpr), (1, cpi)):
                        co = 4 if comp == 0 else 37
                        dst = buf.ap().rearrange(
                            "p (db g2 w) -> p db g2 w", w=35, g2=G_CH)[
                            :, 4 * hb2:4 * hb2 + 4, gc:gc + 1, 6:35]
                        s4 = src3[:, :, co:co + LHI].unsqueeze(2)
                        b.plain("vector", lambda e, o=dst, s=s4:
                                e.tensor_copy(o, s))
            pair_evac[pp] = b.seal("vector", "V")
        prev_evac = pair_evac[3]

        for comp in (range(2) if p2sub >= 4 else range(0)):
            cpad = cpr if comp == 0 else cpi
            pwT = pwrT if comp == 0 else pwiT
            spec = SpecTr if comp == 0 else SpecTi
            b.wait("tensor", "V", prev_evac)
            for dblk in range(8):
                pb2 = dblk % 2
                for j in range(KS):
                    lhs = dwdg_ap(comp, j, dblk)
                    base_col = G_CH * 35 * dblk
                    rhs = cpad.ap()[:, base_col:base_col + G_CH * 35].rearrange(
                        "p (g2 w) -> p g2 w", w=35)[:, :, j:j + LHI]
                    o = bankB(pb2)[:, 0:GW]
                    b.plain("tensor", lambda e, o=o, l=lhs, r=rhs, st=(j == 0),
                            sp=(j == KS - 1): e.matmul(o, l, r, start=st, stop=sp))
                tcv = b.seal("tensor", "T")
                b.wait("scalar", "T", tcv)
                b.plain("scalar", lambda e, o=Hbt.ap()[:, GW * dblk:GW * (dblk + 1)],
                        s=bankB(pb2)[:, 0:GW]: e.activation(o, s, AF.Gelu))
                sck = b.seal("scalar", "S")
                b.wait("tensor", "S", sck)
            for od in (range(8) if p2sub >= 5 else range(0)):
                pb2 = 2 + od % 2
                for K in range(8):
                    lhs = pwT.ap()[:, 1024 * K + 128 * od:1024 * K + 128 * od + 128]
                    rhs = Hbt.ap()[:, GW * K:GW * (K + 1)]
                    o = bankB(pb2)[:, 0:GW]
                    b.plain("tensor", lambda e, o=o, l=lhs, r=rhs, st=(K == 0),
                            sp=(K == 7): e.matmul(o, l, r, start=st, stop=sp))
                tpw = b.seal("tensor", "T")
                b.wait("vector", "T", tpw)
                gsl = G_T.ap()[:, 64 * od + G_CH * g:64 * od + G_CH * (g + 1)] \
                    .unsqueeze(2).broadcast_to([128, G_CH, LHI])
                doff = NCHUNK * M * od + M * G_CH * g
                dst = spec.ap()[:, doff:doff + M * G_CH].rearrange(
                    "p (ch m) -> p ch m", m=M)[:, :, 4:33]
                src = bankB(pb2)[:, 0:GW].rearrange("p (ch m) -> p ch m", m=LHI)
                b.plain("vector", lambda e, o=dst, s=src, gg=gsl:
                        e.tensor_tensor(o, s, gg, op=AluOpType.mult))
                vpw = b.seal("vector", "V")
                b.wait("tensor", "V", vpw)

    b.full_barrier()
    es2.close()
    if phases <= 2:
        return _finish()

    es3 = ExitStack()
    qwT = sb("qwT_sb", (128, 8 * D), BF16, es3)
    kwT = sb("kwT_sb", (128, 8 * D), BF16, es3)
    vwT = sb("vwT_sb", (128, 8 * D), BF16, es3)
    owT = sb("owT_sb", (128, 8 * D), BF16, es3)
    XlTbr = sb("XlTbr", (128, 2048), BF16, es3)
    XlTbi = sb("XlTbi", (128, 2048), BF16, es3)
    QTb = sb("QTb", (128, 2048), BF16, es3)
    KTb = sb("KTb", (128, 2048), BF16, es3)
    Vtok = sb("Vtok", (128, 2048), BF16, es3)
    OTb = sb("OTb", (128, 2048), BF16, es3)
    Ssb = sb("Ssb", (128, 512), F32, es3)
    Pf = sb("Pf", (128, 512), F32, es3)
    Pbf = sb("Pbf", (128, 512), BF16, es3)
    PbT = sb("PbT", (128, 512), BF16, es3)
    smst = sb("smst", (128, 8), F32, es3)
    d1b = sb("d1b", (128, 256), F32, es3)

    ld8(qwT, qwT_h)
    half_ld = ld8(kwT, kwT_h)
    b.wait("gpsimd", "gq", half_ld)
    ld8(vwT, vwT_h)
    aw_ld = ld8(owT, owT_h)
    b.wait("vector", "gq", aw_ld)
    b.plain("vector", lambda e: e.tensor_copy(XlTbr.ap(), XlTr.ap()))
    b.inc("vector", lambda e: e.tensor_copy(XlTbi.ap(), XlTi.ap()), "V")
    b.full_barrier()

    for comp in range(2):
        XlTb = XlTbr if comp == 0 else XlTbi
        XlTf = XlTr if comp == 0 else XlTi
        spec = SpecTr if comp == 0 else SpecTi

        for wT, dstb in ((qwT, QTb), (kwT, KTb)):
            for mtl in range(8):
                pbk = mtl % 4
                slot = 256 * (mtl // 4)
                for K in range(8):
                    lhs = wT.ap()[:, 1024 * K + 128 * mtl:1024 * K + 128 * mtl + 128]
                    rhs = XlTb.ap()[:, 256 * K:256 * K + 256]
                    o = bankA(pbk)[:, slot:slot + 256]
                    b.plain("tensor", lambda e, o=o, l=lhs, r=rhs, st=(K == 0),
                            sp=(K == 7): e.matmul(o, l, r, start=st, stop=sp))
            tq = b.seal("tensor", "T")
            b.wait("scalar", "T", tq)
            for mtl in range(8):
                pbk = mtl % 4
                slot = 256 * (mtl // 4)
                b.plain("scalar", lambda e,
                        o=dstb.ap()[:, 256 * mtl:256 * mtl + 256],
                        s=bankA(pbk)[:, slot:slot + 256]: e.copy(o, s))
            sq = b.seal("scalar", "S")
            b.wait("tensor", "S", sq)
        for mt in range(2):
            for K in range(8):
                lhs = XlTb.ap()[:, 256 * K + 128 * mt:256 * K + 128 * mt + 128]
                for c2 in range(2):
                    rhs = vwT.ap()[:, 1024 * K + 512 * c2:1024 * K + 512 * c2 + 512]
                    o = bankA(2 * mt + c2)
                    b.plain("tensor", lambda e, o=o, l=lhs, r=rhs, st=(K == 0),
                            sp=(K == 7): e.matmul(o, l, r, start=st, stop=sp))
        tv = b.seal("tensor", "T")
        b.wait("scalar", "T", tv)
        for k4 in range(4):
            b.plain("scalar", lambda e, o=Vtok.ap()[:, 512 * k4:512 * k4 + 512],
                    s=bankA(k4): e.copy(o, s))
        sv2 = b.seal("scalar", "S")
        b.wait("tensor", "S", sv2)

        prev_v = 0
        for h in range(16):
            base = 64 * (h % 2)
            tl = h // 2
            if prev_v:
                b.wait("tensor", "V", prev_v)
            for qm in range(2):
                lhs = QTb.ap()[base:base + 64,
                               256 * tl + 128 * qm:256 * tl + 128 * qm + 128]
                rhs = KTb.ap()[base:base + 64, 256 * tl:256 * tl + 256]
                o = bankB(qm)[:, 0:256]
                b.plain("tensor", lambda e, o=o, l=lhs, r=rhs:
                        e.matmul(o, l, r, start=True, stop=True))
            ts2 = b.seal("tensor", "T")
            b.wait("vector", "T", ts2)
            for qm in range(2):
                b.plain("vector", lambda e, o=Ssb.ap()[:, 256 * qm:256 * qm + 256],
                        s=bankB(qm)[:, 0:256],
                        m=maskt.ap()[:, 256 * qm:256 * qm + 256]:
                        e.tensor_tensor(o, s, m, op=AluOpType.add))
            b.plain("vector", lambda e: e.drain())
            for qm in range(2):
                b.plain("vector", lambda e, o=smst.ap()[:, qm:qm + 1],
                        s=Ssb.ap()[:, 256 * qm:256 * qm + 256]:
                        e.reduce_max(o, s, axis=AX.X))
            b.plain("vector", lambda e: e.drain())
            for qm in range(2):
                b.plain("vector", lambda e, o=smst.ap()[:, 2 + qm:3 + qm],
                        s=smst.ap()[:, qm:qm + 1]: e.tensor_scalar_mul(o, s, -0.125))
            vs1 = b.seal("vector", "V")
            prev_v = vs1
            b.wait("scalar", "V", vs1)
            for qm in range(2):
                b.plain("scalar", lambda e, o=Pf.ap()[:, 256 * qm:256 * qm + 256],
                        s=Ssb.ap()[:, 256 * qm:256 * qm + 256],
                        bi=smst.ap()[:, 2 + qm:3 + qm],
                        ac=smst.ap()[:, 4 + qm:5 + qm]:
                        e.activation(o, s, AF.Exp, scale=ccol("smx"), bias=bi,
                                     accum_out=ac))
            se1 = b.seal("scalar", "S")
            b.wait("vector", "S", se1)
            b.plain("vector", lambda e: e.reciprocal(smst.ap()[:, 6:8],
                                                     smst.ap()[:, 4:6]))
            vs2 = b.seal("vector", "V")
            b.wait("scalar", "V", vs2)
            for qm in range(2):
                b.plain("scalar", lambda e, o=Pbf.ap()[:, 256 * qm:256 * qm + 256],
                        s=Pf.ap()[:, 256 * qm:256 * qm + 256],
                        sc2=smst.ap()[:, 6 + qm:7 + qm]:
                        e.activation(o, s, AF.Copy, scale=sc2))
            se2 = b.seal("scalar", "S")
            b.wait("tensor", "S", se2)
            for blk in range(4):
                qm, kb = blk % 2, blk // 2
                lhs = Pbf.ap()[:, 256 * qm + 128 * kb:256 * qm + 128 * kb + 128]
                o = psB.ap()[:, 1024 + 128 * (2 * kb + qm):
                             1024 + 128 * (2 * kb + qm) + 128]
                b.plain("tensor", lambda e, o=o, l=lhs:
                        e.matmul(o, l, I128b.ap(), start=True, stop=True))
            tpt = b.seal("tensor", "T")
            b.wait("scalar", "T", tpt)
            b.plain("scalar", lambda e: e.copy(PbT.ap(), psB.ap()[:, 1024:1536]))
            spt = b.seal("scalar", "S")
            b.wait("tensor", "S", spt)
            for kt in range(2):
                lhs = Vtok.ap()[:, 1024 * kt + 64 * h:1024 * kt + 64 * h + 64]
                rhs = PbT.ap()[:, 256 * kt:256 * kt + 256]
                o = psB.ap()[base:base + 64, 1536:1536 + 256]
                b.plain("tensor", lambda e, o=o, l=lhs, r=rhs, st=(kt == 0),
                        sp=(kt == 1), tp=(0, base):
                        e.matmul(o, l, r, start=st, stop=sp, tile_position=tp))
            to = b.seal("tensor", "T")
            b.wait("scalar", "T", to)
            b.plain("scalar", lambda e, o=OTb.ap()[base:base + 64,
                                                   256 * tl:256 * tl + 256],
                    s=psB.ap()[base:base + 64, 1536:1536 + 256]: e.copy(o, s))
            so = b.seal("scalar", "S")
            b.wait("tensor", "S", so)

        for mtl in range(8):
            pbk = mtl % 2
            for K in range(8):
                lhs = owT.ap()[:, 1024 * K + 128 * mtl:1024 * K + 128 * mtl + 128]
                rhs = OTb.ap()[:, 256 * K:256 * K + 256]
                o = bankA(pbk)[:, 0:256]
                b.plain("tensor", lambda e, o=o, l=lhs, r=rhs, st=(K == 0),
                        sp=(K == 7): e.matmul(o, l, r, start=st, stop=sp))
            tso = b.seal("tensor", "T")
            b.wait("vector", "T", tso)
            b.plain("vector", lambda e, s=bankA(pbk)[:, 0:256],
                    xl=XlTf.ap()[:, 256 * mtl:256 * mtl + 256]:
                    e.tensor_tensor(d1b.ap(), s, xl, op=AluOpType.subtract))
            b.plain("vector", lambda e: e.drain())
            doff = NCHUNK * M * mtl
            dst = spec.ap()[:, doff:doff + NCHUNK * M].rearrange(
                "p (ch m) -> p ch m", m=M)[:, :, 0:4]
            b.plain("vector", lambda e, o=dst,
                    s=d1b.ap().rearrange("p (ch m) -> p ch m", m=4):
                    e.tensor_scalar(o, s, alpha_sg.ap()[:, 0:1], None,
                                    op0=AluOpType.mult))
            vso = b.seal("vector", "V")
            b.wait("tensor", "V", vso)

    b.full_barrier()
    es3.close()
    es23.close()
    if phases <= 3:
        return _finish()

    es4 = ExitStack()
    slotR = sb("slotR", (66, 1024), BF16, es4)
    slotI = sb("slotI", (66, 1024), BF16, es4)
    xnb4 = sb("xnb4", (128, 2048), BF16, es4)
    xp4 = sb("xp4", (128, 2048), F32, es4)
    gp4 = sb("gp4", (128, 2048), F32, es4)
    ub4 = sb("ub4", (128, 1024), F32, es4)
    ob4 = sb("ob4", (128, 2048), F32, es4)

    p4 = {}
    for p in range(NPAIR):
        sl = (p % 2) * 1024
        psem = "p4l%d" % (p % 2)
        if p >= 2:
            b.wait("sync", "V", p4[(p - 2, "u")])
        l1 = ld(xnb4.ap()[:, sl:sl + 1024], xnb_s[128 * p:128 * p + 128, :], psem)
        ld(xp4.ap()[:, sl:sl + 1024], x_h[128 * p:128 * p + 128, :], psem)
        l3 = ld(gp4.ap()[:, sl:sl + 1024], g_s[128 * p:128 * p + 128, :], psem)
        if p >= 1:
            b.wait("tensor", "S", p4[(p - 1, "slots")])
        for comp in range(2):
            spec = SpecTr if comp == 0 else SpecTi
            for dblk in range(8):
                lhs = spec.ap()[:, NCHUNK * M * dblk + 66 * p:
                                NCHUNK * M * dblk + 66 * p + 66]
                bank = 2 * comp + dblk // 4
                o = psA.ap()[0:66, 512 * bank + 128 * (dblk % 4):
                             512 * bank + 128 * (dblk % 4) + 128]
                b.plain("tensor", lambda e, o=o, l=lhs:
                        e.matmul(o, l, I128b.ap(), start=True, stop=True))
        ttr = b.seal("tensor", "T")
        b.wait("scalar", "T", ttr)
        for comp in range(2):
            dstb = slotR if comp == 0 else slotI
            for hb2 in range(2):
                bank = 2 * comp + hb2
                b.plain("scalar", lambda e,
                        o=dstb.ap()[0:66, 512 * hb2:512 * hb2 + 512],
                        s=psA.ap()[0:66, 512 * bank:512 * bank + 512]:
                        e.copy(o, s))
        p4[(p, "slots")] = b.seal("scalar", "S")
        b.wait("tensor", "S", p4[(p, "slots")])
        b.wait("tensor", psem, l3)
        if p >= 1:
            b.wait("tensor", "V", p4[(p - 1, "u")])
        for c2 in range(2):
            rhs = xnb4.ap()[:, sl + 512 * c2:sl + 512 * c2 + 512]
            b.plain("tensor", lambda e, o=bankB(c2), r=rhs:
                    e.matmul(o, I128b.ap(), r, start=True, stop=False))
        for c2 in range(2):
            rhs = slotR.ap()[0:66, 512 * c2:512 * c2 + 512]
            b.plain("tensor", lambda e, o=bankB(c2), r=rhs:
                    e.matmul(o, Wr2.ap(), r, start=False, stop=False))
        for c2 in range(2):
            rhs = slotI.ap()[0:66, 512 * c2:512 * c2 + 512]
            b.plain("tensor", lambda e, o=bankB(c2), r=rhs:
                    e.matmul(o, Wi2.ap(), r, start=False, stop=True))
        ty = b.seal("tensor", "T")
        b.wait("vector", "T", ty)
        b.wait("vector", psem, l3)
        if p >= 2:
            b.wait("vector", "p4s%d" % (p % 2), p4[(p - 2, "done")])
        b.plain("vector", lambda e, s=sl: e.tensor_tensor(
            ub4.ap(), psB.ap()[:, 0:1024], gp4.ap()[:, s:s + 1024],
            op=AluOpType.mult))
        b.plain("vector", lambda e: e.drain())
        b.plain("vector", lambda e, s=sl: e.tensor_tensor(
            ob4.ap()[:, s:s + 1024], ub4.ap(), xp4.ap()[:, s:s + 1024],
            op=AluOpType.add))
        p4[(p, "u")] = b.seal("vector", "V")
        b.wait("sync", "V", p4[(p, "u")])
        p4[(p, "done")] = b.inc("sync", lambda e, s=sl, p=p: e.dma_start(
            out=out_h[128 * p:128 * p + 128, :], in_=ob4.ap()[:, s:s + 1024]),
            "p4s%d" % (p % 2), 16)

    es4.close()
    esM.close()
    return _finish()


def kernel(L0_attn_hidden_states, L0_attn_q_w, L0_attn_k_w, L0_attn_v_w,
           L0_attn_o_w, L0_attn_dw_r, L0_attn_pw_r, L0_attn_dw_i, L0_attn_pw_i,
           L0_attn_gate_w, L0_attn_alpha_raw, L0_ffn_w_fc, L0_ffn_b_fc,
           L0_ffn_w_proj, L0_ffn_b_proj):
    x = np.ascontiguousarray(np.asarray(L0_attn_hidden_states, dtype=np.float32))
    qw = np.asarray(L0_attn_q_w, np.float32)
    kw = np.asarray(L0_attn_k_w, np.float32)
    vw = np.asarray(L0_attn_v_w, np.float32)
    ow = np.asarray(L0_attn_o_w, np.float32)
    gw = np.asarray(L0_attn_gate_w, np.float32)
    pwr = np.asarray(L0_attn_pw_r, np.float32)
    pwi = np.asarray(L0_attn_pw_i, np.float32)
    dwr = np.ascontiguousarray(np.asarray(L0_attn_dw_r, np.float32)[:, 0, :])
    dwi = np.ascontiguousarray(np.asarray(L0_attn_dw_i, np.float32)[:, 0, :])
    alpha = np.asarray(L0_attn_alpha_raw, np.float32)
    wfc = np.ascontiguousarray(np.asarray(L0_ffn_w_fc, np.float32))
    bfc = np.asarray(L0_ffn_b_fc, np.float32)
    wpj = np.ascontiguousarray(np.asarray(L0_ffn_w_proj, np.float32))
    bpj = np.asarray(L0_ffn_b_proj, np.float32)

    kw_rep = np.repeat(kw.reshape(4, 64, D), 4, axis=0).reshape(D, D)
    vw_rep = np.repeat(vw.reshape(4, 64, D), 4, axis=0).reshape(D, D)
    consts = make_consts()
    shared = {
        "qwT": np.ascontiguousarray(qw.T),
        "kwT": np.ascontiguousarray(kw_rep.T),
        "vwT": np.ascontiguousarray(vw_rep.T),
        "owT": np.ascontiguousarray(ow.T),
        "gwT": np.ascontiguousarray(gw.T),
        "pwrT": np.ascontiguousarray(pwr.T),
        "pwiT": np.ascontiguousarray(pwi.T),
        "dwr": dwr, "dwi": dwi,
        "alpha128": np.ascontiguousarray(np.repeat(alpha.reshape(1, 1), 128, 0)),
        "bfc": np.ascontiguousarray(bfc.reshape(1, HID)),
        "bproj": np.ascontiguousarray(bpj.reshape(1, D)),
        "wfc": wfc, "wproj": wpj,
        **consts,
    }

    nc = build_program()
    in_maps = [dict(shared, x=x[bb]) for bb in range(8)]
    res = run_bass_kernel_spmd(nc, in_maps, core_ids=list(range(8)))
    return np.stack([res.results[i]["out"] for i in range(8)], axis=0)


# version 06
# speedup vs baseline: 1.5247x; 1.5247x over previous
import numpy as np
import ml_dtypes
import concourse.bass as bass
import concourse.mybir as mybir
from concourse.bass_utils import run_bass_kernel_spmd
from concourse.alu_op_type import AluOpType
from contextlib import ExitStack

F32 = mybir.dt.float32
BF16 = mybir.dt.bfloat16
I32 = mybir.dt.int32
AF = mybir.ActivationFunctionType
AX = mybir.AxisListType

LN2 = float(np.log(2.0))
N_TOK = 4096
D = 1024
NTILES = 32
C = 64
M = 33
KLOW = 4
LHI = 29
KS = 7
NCHUNK = 64
NPAIR = 32
HID = 4096
G_CH = 8
NGRP = NCHUNK // G_CH
GW = G_CH * LHI

ENGS = ("sync", "gpsimd", "tensor", "vector", "scalar")


def make_consts():
    c = np.arange(C)
    m = np.arange(M)
    ang = 2 * np.pi * np.outer(c, m) / C
    Fc = np.cos(ang).astype(np.float32)
    Fs = (-np.sin(ang)).astype(np.float32)
    F64 = np.zeros((128, 66), np.float32)
    F64[0:64, 0:33] = Fc
    F64[0:64, 33:66] = Fs
    F64[64:128] = F64[0:64]
    Flow = np.zeros((128, 8), np.float32)
    Flow[0:64, 0:4] = Fc[:, 0:4]
    Flow[0:64, 4:8] = Fs[:, 0:4]
    Flow[64:128] = Flow[0:64]

    mm_ = np.arange(M)[:, None]
    cc_ = np.arange(C)[None, :]
    s2 = np.where((mm_ >= 1) & (mm_ <= 31), 2.0, 1.0)
    wr = (np.cos(2 * np.pi * mm_ * cc_ / C) / C * s2).astype(np.float32)
    wi = (-np.sin(2 * np.pi * mm_ * cc_ / C) / C * s2).astype(np.float32)
    wi[0] = 0.0
    wi[32] = 0.0
    Wr2 = np.zeros((66, 128), np.float32)
    Wi2 = np.zeros((66, 128), np.float32)
    Wr2[0:33, 0:64] = wr
    Wr2[33:66, 64:128] = wr
    Wi2[0:33, 0:64] = wi
    Wi2[33:66, 64:128] = wi

    cid = np.arange(256) // 4
    mask8 = np.where(cid[:, None] >= cid[None, :], 0.0, -8.0e9).astype(np.float32)

    bf = ml_dtypes.bfloat16
    return {
        "F64b": F64.astype(bf),
        "FlowB": Flow.astype(bf),
        "Wr2": Wr2.astype(bf),
        "Wi2": Wi2.astype(bf),
        "I128b": np.eye(128, dtype=bf),
        "ones2": np.ones((2, 128), dtype=bf),
        "maskc": mask8,
    }


class B:
    def __init__(self):
        self.ops = {e: [] for e in ENGS}
        self.c = {}
        self.sem = {}

    def add_sem(self, name, handle):
        self.sem[name] = handle
        self.c[name] = 0

    def inc(self, eng, make, sem, n=1):
        h = self.sem[sem]
        self.ops[eng].append(lambda e, make=make, h=h, n=n: make(e).then_inc(h, n))
        self.c[sem] += n
        return self.c[sem]

    def plain(self, eng, make):
        self.ops[eng].append(make)

    def seal(self, eng, sem, n=1):
        h = self.sem[sem]
        last = self.ops[eng].pop()
        self.ops[eng].append(lambda e, last=last, h=h, n=n:
                             last(e).then_inc(h, n))
        self.c[sem] += n
        return self.c[sem]

    def wait(self, eng, sem, val=None):
        h = self.sem[sem]
        v = self.c[sem] if val is None else val
        if v > 0:
            self.ops[eng].append(lambda e, h=h, v=v: e.wait_ge(h, v))
        return v

    def full_barrier(self):
        vals = dict(self.c)
        for e in ENGS:
            for s, v in vals.items():
                self.wait(e, s, v)


def build_program(phases=4, p2sub=9):
    nc = bass.Bass("TRN2", target_bir_lowering=False,
                   dynamic_dma_scratch_size=16384)

    def din(name, shape, dtype):
        return nc.dram_tensor(name, list(shape), dtype, kind="ExternalInput").ap()

    x_h = din("x", (N_TOK, D), F32)
    qwT_h = din("qwT", (D, D), F32)
    kwT_h = din("kwT", (D, D), F32)
    vwT_h = din("vwT", (D, D), F32)
    owT_h = din("owT", (D, D), F32)
    gwT_h = din("gwT", (D, D), F32)
    pwrT_h = din("pwrT", (D, D), F32)
    pwiT_h = din("pwiT", (D, D), F32)
    dwr_h = din("dwr", (D, KS), F32)
    dwi_h = din("dwi", (D, KS), F32)
    alpha_h = din("alpha128", (128, 1), F32)
    bfc_h = din("bfc", (1, HID), F32)
    bpj_h = din("bproj", (1, D), F32)
    wfc_h = din("wfc", (HID, D), F32)
    wpj_h = din("wproj", (D, HID), F32)
    F64b_h = din("F64b", (128, 66), BF16)
    FlowB_h = din("FlowB", (128, 8), BF16)
    Wr2_h = din("Wr2", (66, 128), BF16)
    Wi2_h = din("Wi2", (66, 128), BF16)
    I128b_h = din("I128b", (128, 128), BF16)
    ones2_h = din("ones2", (2, 128), BF16)
    maskc_h = din("maskc", (256, 256), F32)

    out_h = nc.dram_tensor("out", [N_TOK, D], F32, kind="ExternalOutput").ap()
    xnb_s = nc.dram_tensor("xnb_s", [N_TOK, D], BF16, kind="Internal").ap()
    g_s = nc.dram_tensor("g_s", [N_TOK, D], F32, kind="Internal").ap()

    esg = ExitStack()

    def sb(name, shape, dtype, stack):
        return stack.enter_context(nc.sbuf_tensor(name, list(shape), dtype))

    cst = sb("cst", (128, 16), F32, esg)
    I128b = sb("I128b_sb", (128, 128), BF16, esg)
    F64b = sb("F64b_sb", (128, 66), BF16, esg)
    FlowB = sb("FlowB_sb", (128, 8), BF16, esg)
    Wr2 = sb("Wr2_sb", (66, 128), BF16, esg)
    Wi2 = sb("Wi2_sb", (66, 128), BF16, esg)
    ones2 = sb("ones2_sb", (2, 128), BF16, esg)
    maskt = sb("maskt", (128, 512), F32, esg)
    alpha_t = sb("alpha_t", (128, 1), F32, esg)
    alpha_sg = sb("alpha_sg", (128, 1), F32, esg)
    bspl_fc = sb("bspl_fc", (2, HID), BF16, esg)
    bspl_pj = sb("bspl_pj", (2, D), BF16, esg)
    dwr_t = sb("dwr_t", (128, 8 * KS), F32, esg)
    dwi_t = sb("dwi_t", (128, 8 * KS), F32, esg)

    psA = esg.enter_context(nc.psum_tensor("psA", [128, 2048], F32))
    psB = esg.enter_context(nc.psum_tensor("psB", [128, 2048], F32))

    b = B()
    for s in ("cst", "gq", "T", "V", "S", "wst0", "wst1", "x0", "x1",
              "xnbst", "gst", "p2x0", "p2x1", "p4l0", "p4l1", "p4s0", "p4s1", "bld", "bspl"):
        b.add_sem(s, esg.enter_context(nc.semaphore("sem_" + s)))

    def bankA(k, w=512):
        return psA.ap()[:, 512 * k:512 * k + w]

    def bankB(k, w=512):
        return psB.ap()[:, 512 * k:512 * k + w]


    def _finish():
        for s in b.c:
            if s not in ("T", "V", "S"):
                b.wait("sync", s)
        with nc.Block() as block:
            @block.sync
            def _(eng):
                for fn in b.ops["sync"]:
                    fn(eng)

            @block.gpsimd
            def _(eng):
                for fn in b.ops["gpsimd"]:
                    fn(eng)

            @block.tensor
            def _(eng):
                for fn in b.ops["tensor"]:
                    fn(eng)

            @block.vector
            def _(eng):
                for fn in b.ops["vector"]:
                    fn(eng)

            @block.scalar
            def _(eng):
                for fn in b.ops["scalar"]:
                    fn(eng)
        return nc

    def ld(dst, src, sem="cst"):
        return b.inc("sync", lambda e, d=dst, s=src: e.dma_start(out=d, in_=s),
                     sem, 16)

    def ld8(dst_sb, src_h):
        for k in range(8):
            b.inc("gpsimd", lambda e, d=dst_sb.ap()[:, 1024 * k:1024 * k + 1024],
                  s=src_h[128 * k:128 * k + 128, :]: e.dma_start(out=d, in_=s),
                  "gq", 16)
        return b.c["gq"]

    ld(I128b.ap(), I128b_h)
    ld(F64b.ap(), F64b_h)
    ld(FlowB.ap(), FlowB_h)
    ld(Wr2.ap(), Wr2_h)
    ld(Wi2.ap(), Wi2_h)
    ld(ones2.ap(), ones2_h)
    ld(maskt.ap()[0:128, 0:256], maskc_h[0:128, :])
    ld(maskt.ap()[0:128, 256:512], maskc_h[128:256, :])
    ld(alpha_t.ap(), alpha_h)
    for k in range(8):
        ld(dwr_t.ap()[:, KS * k:KS * k + KS], dwr_h[128 * k:128 * k + 128, :])
        ld(dwi_t.ap()[:, KS * k:KS * k + KS], dwi_h[128 * k:128 * k + 128, :])
    consts_loaded = b.c["cst"]

    CST = {"lnscale": 0, "eps": 1, "nln2": 2, "p3ln2": 3, "pln2": 4, "n3ln2": 5,
           "smx": 6, "invln2": 7, "mhalf": 8}
    cvals = [1.0 / 1024.0, 1e-5, -LN2, 3 * LN2, LN2, -3 * LN2, 0.125, 1.0 / LN2,
             -0.5]
    for col in range(9):
        b.plain("vector", lambda e, c=col, v=cvals[col]:
                e.memset(cst.ap()[:, c:c + 1], v))
    b.seal("vector", "V")

    def ccol(name):
        i = CST[name]
        return cst.ap()[:, i:i + 1]

    b.wait("scalar", "cst", consts_loaded)
    b.wait("scalar", "V")
    b.inc("scalar", lambda e: e.activation(alpha_sg.ap(), alpha_t.ap(), AF.Sigmoid),
          "S")
    es01 = ExitStack()
    WfcqT = sb("WfcqT", (128, 8 * HID), BF16, es01)
    WpjqT = sb("WpjqT", (128, 32 * D), BF16, es01)

    tb = sb("tb", (128, 2048), F32, es01)
    ib = sb("ib", (128, 2048), I32, es01)
    amx = sb("amx", (128, 96), F32, es01)
    yq = sb("yq", (128, 96), F32, es01)
    e32q = sb("e32q", (128, 96), I32, es01)
    rsq = sb("rsq", (128, 96), F32, es01)
    scq = sb("scq", (128, 96), F32, es01)
    es_stg = ExitStack()
    wstg = sb("wstg", (128, 2048), F32, es_stg)
    wqb = sb("wqb", (128, 2048), BF16, es_stg)
    bst_f = sb("bst_f", (1, 1024), F32, es_stg)
    bst_hi = sb("bst_hi", (1, 1024), BF16, es_stg)
    bst_lo = sb("bst_lo", (1, 1024), BF16, es_stg)

    for bi, (src_hbm, dstt, n_ch) in enumerate(((bfc_h, bspl_fc, 4),
                                                (bpj_h, bspl_pj, 1))):
        for cchunk in range(n_ch):
            bldv = ld(bst_f.ap(), src_hbm[0:1, 1024 * cchunk:1024 * cchunk + 1024],
                      "bld")
            b.wait("vector", "bld", bldv)
            b.plain("vector", lambda e: e.tensor_copy(bst_hi.ap(), bst_f.ap()))
            b.plain("vector", lambda e: e.drain())
            b.plain("vector", lambda e: e.tensor_tensor(
                bst_lo.ap(), bst_f.ap(), bst_hi.ap(), op=AluOpType.subtract))
            vsp = b.seal("vector", "V")
            b.wait("sync", "V", vsp)
            for r, srcv in ((0, bst_hi), (1, bst_lo)):
                b.inc("sync", lambda e, s=srcv, d=dstt, r=r, c=cchunk:
                      e.dma_start(out=d.ap()[r:r + 1, 1024 * c:1024 * c + 1024],
                                  in_=s.ap()[0:1, :]), "bspl", 16)
            b.wait("vector", "bspl")

    def q_views(src_ap, nblk, dst_ap, acol, tcol):
        W = 32 * nblk
        rs = rsq.ap()[:, acol:acol + nblk]
        sc = scq.ap()[:, acol:acol + nblk]
        return {
            "a": amx.ap()[:, acol:acol + nblk],
            "y": yq.ap()[:, acol:acol + nblk],
            "e32": e32q.ap()[:, acol:acol + nblk],
            "rs": rs, "sc": sc,
            "src3": src_ap.rearrange("p (nb k) -> p nb k", k=32),
            "tw": tb.ap()[:, tcol:tcol + W],
            "iw": ib.ap()[:, tcol:tcol + W],
            "t3": tb.ap()[:, tcol:tcol + W].rearrange("p (nb k) -> p nb k", k=32),
            "i3": ib.ap()[:, tcol:tcol + W].rearrange("p (nb k) -> p nb k", k=32),
            "dst3": dst_ap.rearrange("p (nb k) -> p nb k", k=32),
            "rs3": rs.unsqueeze(2).broadcast_to([128, nblk, 32]),
            "sc3": sc.unsqueeze(2).broadcast_to([128, nblk, 32]),
        }

    def q_amax(qv):
        b.plain("vector", lambda e: e.tensor_reduce(
            qv["a"], qv["src3"], axis=AX.X, op=AluOpType.max,
            apply_absolute_value=True))
        qv["vA"] = b.seal("vector", "V")

    def q_scales(qv):
        b.wait("scalar", "V", qv["vA"])
        b.plain("scalar", lambda e: e.activation(qv["y"], qv["a"], AF.Ln))
        b.seal("scalar", "S")
        b.wait("scalar", "S")
        b.plain("scalar", lambda e: e.activation(
            qv["e32"], qv["y"], AF.Identity, scale=ccol("invln2"),
            bias=ccol("mhalf")))
        b.seal("scalar", "S")
        b.wait("scalar", "S")
        b.plain("scalar", lambda e: e.activation(
            qv["rs"], qv["e32"], AF.Exp, scale=ccol("nln2"), bias=ccol("p3ln2")))
        b.plain("scalar", lambda e: e.activation(
            qv["sc"], qv["e32"], AF.Exp, scale=ccol("pln2"), bias=ccol("n3ln2")))
        qv["sB"] = b.seal("scalar", "S")

    def q_apply(qv):
        b.wait("vector", "S", qv["sB"])
        b.plain("vector", lambda e: e.tensor_tensor(
            qv["t3"], qv["src3"], qv["rs3"], op=AluOpType.mult))
        b.plain("vector", lambda e: e.drain())
        b.plain("vector", lambda e: e.tensor_copy(qv["iw"], qv["tw"]))
        b.plain("vector", lambda e: e.drain())
        b.plain("vector", lambda e: e.tensor_tensor(
            qv["dst3"], qv["i3"], qv["sc3"], op=AluOpType.mult))
        qv["done"] = b.seal("vector", "V")

    def t_transpose8(src_bf_ap):
        for blk in range(8):
            bank = blk // 4
            o = psB.ap()[:, 512 * bank + 128 * (blk % 4):
                         512 * bank + 128 * (blk % 4) + 128]
            sblk = src_bf_ap[:, 128 * blk:128 * blk + 128]
            b.plain("tensor", lambda e, o=o, l=sblk:
                    e.matmul(o, l, I128b.ap(), start=True, stop=True))
        return b.seal("tensor", "T")

    def s_evac8(dst_views, tT):
        b.wait("scalar", "T", tT)
        for blk in range(8):
            bank = blk // 4
            s = psB.ap()[:, 512 * bank + 128 * (blk % 4):
                         512 * bank + 128 * (blk % 4) + 128]
            b.plain("scalar", lambda e, o=dst_views[blk], s=s: e.copy(o, s))
        return b.seal("scalar", "S")

    def p0_src(u):
        if u < 32:
            return wfc_h[128 * u:128 * u + 128, :]
        t8, qv_ = (u - 32) // 4, (u - 32) % 4
        return wpj_h[128 * t8:128 * t8 + 128, 1024 * qv_:1024 * qv_ + 1024]

    def p0_dsts(u):
        if u < 32:
            return [WfcqT.ap()[:, HID * j + 128 * u:HID * j + 128 * u + 128]
                    for j in range(8)]
        t8, qv_ = (u - 32) // 4, (u - 32) % 4
        return [WpjqT.ap()[:, D * (8 * qv_ + j) + 128 * t8:
                           D * (8 * qv_ + j) + 128 * t8 + 128] for j in range(8)]

    NU = 64
    uq = {}
    uld = {}
    uld[0] = ld(wstg.ap()[:, 0:1024], p0_src(0), "wst0")
    uld[1] = ld(wstg.ap()[:, 1024:2048], p0_src(1), "wst1")
    for u in range(NU):
        par = u % 2
        qv = q_views(wstg.ap()[:, 1024 * par:1024 * par + 1024], 32,
                     wqb.ap()[:, 1024 * par:1024 * par + 1024], 32 * par, 0)
        uq[u] = qv
        b.wait("vector", "wst%d" % par, uld[u])
        if u >= 2:
            b.wait("vector", "T", uq[u - 2]["tr"])
        q_amax(qv)
        q_scales(qv)
        if u >= 1:
            pq = uq[u - 1]
            q_apply(pq)
            b.wait("tensor", "V", pq["done"])
            if u >= 2:
                b.wait("tensor", "S", uq[u - 2]["te"])
            pq["tr"] = t_transpose8(wqb.ap()[:, 1024 * ((u - 1) % 2):
                                             1024 * ((u - 1) % 2) + 1024])
            pq["te"] = s_evac8(p0_dsts(u - 1), pq["tr"])
            if u + 1 < NU:
                b.wait("sync", "V", pq["done"])
                uld[u + 1] = ld(wstg.ap()[:, 1024 * ((u + 1) % 2):
                                          1024 * ((u + 1) % 2) + 1024],
                                p0_src(u + 1), "wst%d" % ((u + 1) % 2))
    pq = uq[NU - 1]
    q_apply(pq)
    b.wait("tensor", "V", pq["done"])
    b.wait("tensor", "S", uq[NU - 2]["te"])
    pq["tr"] = t_transpose8(wqb.ap()[:, 1024 * ((NU - 1) % 2):
                                     1024 * ((NU - 1) % 2) + 1024])
    pq["te"] = s_evac8(p0_dsts(NU - 1), pq["tr"])

    es_stg.close()
    if phases <= 0:
        b.full_barrier()
        return _finish()

    xt = sb("xt", (128, 2048), F32, es01)
    xn_t = sb("xn_t", (128, 2048), F32, es01)
    xnb_t = sb("xnb_t", (128, 1024), BF16, es01)
    xq_t = sb("xq_t", (128, 2048), BF16, es01)
    xqT_t = sb("xqT_t", (128, 2048), BF16, es01)
    hb_t = sb("hb_t", (128, 2048), F32, es01)
    hq_t = sb("hq_t", (128, 1024), BF16, es01)
    hqT_t = sb("hqT_t", (128, 1024), BF16, es01)
    g_t = sb("g_t", (128, 1024), F32, es01)
    lnst = sb("lnst", (128, 16), F32, es01)

    b.full_barrier()

    chk = {}

    def emit_prep_v1(j):
        sl = (j % 2) * 1024
        xap = xt.ap()[:, sl:sl + 1024]
        lo = 8 * (j % 2)
        b.wait("vector", "x%d" % (j % 2), chk[(j, "xl")])
        b.plain("vector", lambda e, x=xap: e.reduce_sum(
            lnst.ap()[:, lo:lo + 1], x, axis=AX.X))
        b.plain("vector", lambda e: e.drain())
        b.plain("vector", lambda e: e.tensor_scalar_mul(
            lnst.ap()[:, lo + 1:lo + 2], lnst.ap()[:, lo:lo + 1], -1.0 / 1024.0))
        chk[(j, "ln1")] = b.seal("vector", "V")

    def emit_prep_s(j):
        sl = (j % 2) * 1024
        xap = xt.ap()[:, sl:sl + 1024]
        lo = 8 * (j % 2)
        b.wait("scalar", "V", chk[(j, "ln1")])
        b.plain("scalar", lambda e, x=xap, s=sl: e.activation(
            xn_t.ap()[:, s:s + 1024], x, AF.Square,
            bias=lnst.ap()[:, lo + 1:lo + 2], accum_out=lnst.ap()[:, lo + 2:lo + 3]))
        b.seal("scalar", "S")
        b.wait("scalar", "S")
        b.plain("scalar", lambda e: e.activation(
            lnst.ap()[:, lo + 3:lo + 4], lnst.ap()[:, lo + 2:lo + 3], AF.Ln,
            scale=ccol("lnscale"), bias=ccol("eps")))
        b.seal("scalar", "S")
        b.wait("scalar", "S")
        b.plain("scalar", lambda e: e.activation(
            lnst.ap()[:, lo + 4:lo + 5], lnst.ap()[:, lo + 3:lo + 4], AF.Exp,
            scale=ccol("mhalf")))
        chk[(j, "rstd")] = b.seal("scalar", "S")

    def emit_prep_v2(j):
        sl = (j % 2) * 1024
        xap = xt.ap()[:, sl:sl + 1024]
        lo = 8 * (j % 2)
        b.wait("vector", "S", chk[(j, "rstd")])
        b.plain("vector", lambda e: e.tensor_tensor(
            lnst.ap()[:, lo + 5:lo + 6], lnst.ap()[:, lo + 1:lo + 2],
            lnst.ap()[:, lo + 4:lo + 5], op=AluOpType.mult))
        b.plain("vector", lambda e: e.drain())
        b.plain("vector", lambda e, x=xap, s=sl: e.tensor_scalar(
            xn_t.ap()[:, s:s + 1024], x, lnst.ap()[:, lo + 4:lo + 5],
            lnst.ap()[:, lo + 5:lo + 6], op0=AluOpType.mult, op1=AluOpType.add))
        b.plain("vector", lambda e: e.drain())
        if (j - 1, "xnbst") in chk:
            b.wait("vector", "xnbst", chk[(j - 1, "xnbst")])
        b.plain("vector", lambda e, s=sl: e.tensor_copy(
            xnb_t.ap(), xn_t.ap()[:, s:s + 1024]))
        chk[(j, "xn")] = b.seal("vector", "V")

    chk[(0, "xl")] = ld(xt.ap()[:, 0:1024], x_h[0:128, :], "x0")
    chk[(1, "xl")] = ld(xt.ap()[:, 1024:2048], x_h[128:256, :], "x1")
    emit_prep_v1(0)
    emit_prep_s(0)
    emit_prep_v2(0)
    b.wait("sync", "V", chk[(0, "xn")])
    chk[(0, "xnbst")] = b.inc("sync", lambda e: e.dma_start(
        out=xnb_s[0:128, :], in_=xnb_t.ap()), "xnbst", 16)
    qx0 = q_views(xn_t.ap()[:, 0:1024], 32, xq_t.ap()[:, 0:1024], 0, 1024)
    chk[(0, "qx")] = qx0
    q_amax(qx0)
    q_scales(qx0)
    q_apply(qx0)
    b.wait("tensor", "V", qx0["done"])
    b.wait("tensor", "S", uq[NU - 1]["te"])
    chk[(0, "xqT")] = t_transpose8(xq_t.ap()[:, 0:1024])
    chk[(0, "xqTe")] = s_evac8(
        [xqT_t.ap()[:, 128 * j:128 * j + 128] for j in range(8)], chk[(0, "xqT")])

    for i in range(NTILES):
        nx = i + 1

        def hmm(w):
            pb = 2 * (w % 2)
            if w >= 2:
                b.wait("tensor", "S", chk[(i, w - 2, "g")])
            elif (i - 1, w + 2, "g") in chk:
                b.wait("tensor", "S", chk[(i - 1, w + 2, "g")])
            if w == 0:
                b.wait("tensor", "S", chk[(i, "xqTe")])
            for K in range(8):
                lhs = xqT_t.ap()[:, ((i % 2) * 1024) + 128 * K:
                                 ((i % 2) * 1024) + 128 * K + 128]
                for c2 in range(2):
                    rhs = WfcqT.ap()[:, HID * K + 1024 * w + 512 * c2:
                                     HID * K + 1024 * w + 512 * c2 + 512]
                    b.plain("tensor", lambda e, o=bankA(pb + c2), l=lhs, r=rhs,
                            st=(K == 0): e.matmul(o, l, r, start=st, stop=False))
            for c2 in range(2):
                rhs = bspl_fc.ap()[:, 1024 * w + 512 * c2:1024 * w + 512 * c2 + 512]
                b.plain("tensor", lambda e, o=bankA(pb + c2), r=rhs:
                        e.matmul(o, ones2.ap(), r, start=False, stop=True))
            chk[(i, w, "hm")] = b.seal("tensor", "T")

        def gelu(w):
            pb = 2 * (w % 2)
            hs = (w % 2) * 1024
            b.wait("scalar", "T", chk[(i, w, "hm")])
            if (i, w - 2, "ap") in chk:
                b.wait("scalar", "V", chk[(i, w - 2, "ap")])
            elif (i - 1, w + 2, "ap") in chk:
                b.wait("scalar", "V", chk[(i - 1, w + 2, "ap")])
            for c2 in range(2):
                b.plain("scalar", lambda e,
                        o=hb_t.ap()[:, hs + 512 * c2:hs + 512 * c2 + 512],
                        s=bankA(pb + c2): e.activation(o, s, AF.Gelu))
            chk[(i, w, "g")] = b.seal("scalar", "S")

        def mk_qh(w):
            hs = (w % 2) * 1024
            return q_views(hb_t.ap()[:, hs:hs + 1024], 32, hq_t.ap(),
                           32 + 32 * (w % 2), 0)

        def t_tr(w):
            b.wait("tensor", "V", chk[(i, w, "ap")])
            chk[(i, w, "tr")] = t_transpose8(hq_t.ap())

        def s_te(w):
            chk[(i, w, "te")] = s_evac8(
                [hqT_t.ap()[:, 128 * j:128 * j + 128] for j in range(8)],
                chk[(i, w, "tr")])

        def t_f(w):
            b.wait("tensor", "S", chk[(i, w, "te")])
            if w == 0 and (i - 1, "sil") in chk:
                b.wait("tensor", "S", chk[(i - 1, "sil")])
            for K in range(8):
                lhs = hqT_t.ap()[:, 128 * K:128 * K + 128]
                for c2 in range(2):
                    rhs = WpjqT.ap()[:, D * (8 * w + K) + 512 * c2:
                                     D * (8 * w + K) + 512 * c2 + 512]
                    st = (w == 0 and K == 0)
                    b.plain("tensor", lambda e, o=bankB(2 + c2), l=lhs, r=rhs,
                            st=st: e.matmul(o, l, r, start=st, stop=False))
            if w == 3:
                for c2 in range(2):
                    rhs = bspl_pj.ap()[:, 512 * c2:512 * c2 + 512]
                    b.plain("tensor", lambda e, o=bankB(2 + c2), r=rhs:
                            e.matmul(o, ones2.ap(), r, start=False, stop=True))
            chk[(i, w, "f")] = b.seal("tensor", "T")

        def v_amax(w, qh):
            b.wait("vector", "S", chk[(i, w, "g")])
            q_amax(qh)
            chk[(i, w, "am")] = qh["vA"]

        def v_apply(w, qh):
            if (i, w - 1, "tr") in chk:
                b.wait("vector", "T", chk[(i, w - 1, "tr")])
            q_apply(qh)
            chk[(i, w, "ap")] = qh["done"]

        qh = {}
        hmm(0)
        hmm(1)
        if nx < NTILES and nx + 1 < NTILES:
            b.wait("sync", "V", chk[(i, "xn")])
            chk[(nx + 1, "xl")] = ld(
                xt.ap()[:, ((nx + 1) % 2) * 1024:((nx + 1) % 2) * 1024 + 1024],
                x_h[128 * (nx + 1):128 * (nx + 1) + 128, :],
                "x%d" % ((nx + 1) % 2))
        gelu(0)
        if nx < NTILES:
            emit_prep_v1(nx)
        qh[0] = mk_qh(0)
        v_amax(0, qh[0])
        q_scales(qh[0])
        if nx < NTILES:
            emit_prep_s(nx)
        hmm(2)
        gelu(1)
        if nx < NTILES:
            emit_prep_v2(nx)
            b.wait("sync", "V", chk[(nx, "xn")])
            chk[(nx, "xnbst")] = b.inc("sync", lambda e, j=nx: e.dma_start(
                out=xnb_s[128 * j:128 * j + 128, :], in_=xnb_t.ap()), "xnbst", 16)
        qh[1] = mk_qh(1)
        v_amax(1, qh[1])
        q_scales(qh[1])
        v_apply(0, qh[0])
        t_tr(0)
        s_te(0)
        t_f(0)
        hmm(3)
        gelu(2)
        qh[2] = mk_qh(2)
        v_amax(2, qh[2])
        q_scales(qh[2])
        v_apply(1, qh[1])
        t_tr(1)
        s_te(1)
        t_f(1)
        gelu(3)
        qh[3] = mk_qh(3)
        v_amax(3, qh[3])
        q_scales(qh[3])
        v_apply(2, qh[2])
        t_tr(2)
        s_te(2)
        t_f(2)
        v_apply(3, qh[3])
        t_tr(3)
        s_te(3)
        t_f(3)
        if nx < NTILES:
            qx = q_views(xn_t.ap()[:, ((nx % 2) * 1024):((nx % 2) * 1024) + 1024],
                         32, xq_t.ap()[:, ((nx % 2) * 1024):((nx % 2) * 1024) + 1024],
                         0, 1024)
            q_amax(qx)
            q_scales(qx)
            if (i - 1, "xqT") in chk:
                b.wait("vector", "T", chk[(i - 1, "xqT")])
            q_apply(qx)
            b.wait("tensor", "V", qx["done"])
            b.wait("tensor", "S", chk[(i, 3, "te")])
            chk[(nx, "xqT")] = t_transpose8(
                xq_t.ap()[:, ((nx % 2) * 1024):((nx % 2) * 1024) + 1024])
            chk[(nx, "xqTe")] = s_evac8(
                [xqT_t.ap()[:, ((nx % 2) * 1024) + 128 * j:
                            ((nx % 2) * 1024) + 128 * j + 128] for j in range(8)],
                chk[(nx, "xqT")])
        b.wait("scalar", "T", chk[(i, 3, "f")])
        if (i - 1, "gst") in chk:
            b.wait("scalar", "gst", chk[(i - 1, "gst")])
        for c2 in range(2):
            b.plain("scalar", lambda e, o=g_t.ap()[:, 512 * c2:512 * c2 + 512],
                    s=bankB(2 + c2): e.activation(o, s, AF.Silu))
        chk[(i, "sil")] = b.seal("scalar", "S")
        b.wait("sync", "S", chk[(i, "sil")])
        chk[(i, "gst")] = b.inc("sync", lambda e, j=i: e.dma_start(
            out=g_s[128 * j:128 * j + 128, :], in_=g_t.ap()), "gst", 16)

    b.full_barrier()
    es01.close()
    if phases <= 1:
        return _finish()

    esM = ExitStack()
    SpecTr = sb("SpecTr", (128, 8 * NCHUNK * M), BF16, esM)
    SpecTi = sb("SpecTi", (128, 8 * NCHUNK * M), BF16, esM)

    es23 = ExitStack()
    XlTr = sb("XlTr", (128, 8 * 256), F32, es23)
    XlTi = sb("XlTi", (128, 8 * 256), F32, es23)

    es2 = ExitStack()
    GmT = sb("GmT", (128, 8 * 64), BF16, es2)
    G_T = sb("G_T", (128, 8 * 64), F32, es2)
    cpr = sb("cpr", (128, 8 * G_CH * 35), BF16, es2)
    cpi = sb("cpi", (128, 8 * G_CH * 35), BF16, es2)
    Hbt = sb("Hbt", (128, 8 * GW), BF16, es2)
    dwdg = sb("dwdg", (128, 2 * 7 * 8 * 128), BF16, es2)
    pwrT = sb("pwrT_sb", (128, 8 * D), BF16, es2)
    pwiT = sb("pwiT_sb", (128, 8 * D), BF16, es2)
    xnbp = sb("xnbp", (128, 2048), BF16, es2)

    def dwdg_ap(comp, j, dblk):
        off = 128 * (dblk + 8 * (j + 7 * comp))
        return dwdg.ap()[:, off:off + 128]

    for comp, dwt in ((0, dwr_t), (1, dwi_t)):
        for j in range(KS):
            for dblk in range(8):
                col = dwt.ap()[:, KS * dblk + j:KS * dblk + j + 1]
                b.plain("vector", lambda e, o=dwdg_ap(comp, j, dblk), c=col:
                        e.tensor_scalar(o, I128b.ap(), c, None,
                                        op0=AluOpType.mult))
    b.seal("vector", "V")

    gw_ld = ld8(pwrT, gwT_h)
    if p2sub <= 0:
        b.full_barrier()
        return _finish()

    pa_done = {}
    for p in range(NPAIR):
        xslot = (p % 2) * 1024
        if p >= 2:
            b.wait("sync", "V", pa_done[p - 2])
        lxp = ld(xnbp.ap()[:, xslot:xslot + 1024],
                 xnb_s[128 * p:128 * p + 128, :], "p2x%d" % (p % 2))
        b.wait("tensor", "p2x%d" % (p % 2), lxp)
        if p >= 1:
            b.wait("tensor", "V", pa_done[p - 1])
        for half in range(2):
            ch = 2 * p + half
            base = 64 * half
            pb = (ch % 2) * 2
            for dblk in range(8):
                lhs = xnbp.ap()[base:base + 64,
                                xslot + 128 * dblk:xslot + 128 * dblk + 128]
                rhs = F64b.ap()[base:base + 64, :]
                bk = pb + dblk // 4
                o = psA.ap()[:, 512 * bk + 128 * (dblk % 4):
                             512 * bk + 128 * (dblk % 4) + 66]
                b.plain("tensor", lambda e, o=o, l=lhs, r=rhs:
                        e.matmul(o, l, r, start=True, stop=True))
        tpa = b.seal("tensor", "T")
        b.wait("vector", "T", tpa)
        for half in range(2):
            ch = 2 * p + half
            pb = (ch % 2) * 2
            dr = XlTr.ap().rearrange("p (db t) -> p db t", t=256)
            di = XlTi.ap().rearrange("p (db t) -> p db t", t=256)
            dg = GmT.ap().rearrange("p (db c) -> p db c", c=64)
            for hb2 in range(2):
                src = psA.ap()[:, 512 * (pb + hb2):512 * (pb + hb2) + 512]
                src3 = src.rearrange("p (s w) -> p s w", w=128)
                b.plain("vector", lambda e,
                        o=dr[:, 4 * hb2:4 * hb2 + 4, 4 * ch:4 * ch + 4],
                        s=src3[:, :, 0:4]: e.tensor_copy(o, s))
                b.plain("vector", lambda e,
                        o=di[:, 4 * hb2:4 * hb2 + 4, 4 * ch:4 * ch + 4],
                        s=src3[:, :, 33:37]: e.tensor_copy(o, s))
                b.plain("vector", lambda e,
                        o=dg[:, 4 * hb2:4 * hb2 + 4, ch:ch + 1],
                        s=src3[:, :, 0:1]: e.tensor_scalar_mul(o, s, 1.0 / 64.0))
        pa_done[p] = b.seal("vector", "V")

    if p2sub <= 1:
        b.full_barrier()
        return _finish()

    b.wait("tensor", "V")
    b.wait("tensor", "gq", gw_ld)
    for mtl in range(8):
        for K in range(8):
            lhs = pwrT.ap()[:, 1024 * K + 128 * mtl:1024 * K + 128 * mtl + 128]
            rhs = GmT.ap()[:, 64 * K:64 * K + 64]
            o = psB.ap()[:, 64 * mtl:64 * mtl + 64]
            b.plain("tensor", lambda e, o=o, l=lhs, r=rhs, st=(K == 0),
                    sp=(K == 7): e.matmul(o, l, r, start=st, stop=sp))
    tg = b.seal("tensor", "T")
    b.wait("scalar", "T", tg)
    sg2 = b.inc("scalar", lambda e: e.activation(G_T.ap(), psB.ap()[:, 0:512],
                                                 AF.Sigmoid), "S")
    pw_ld = 0
    if p2sub >= 3:
        b.wait("gpsimd", "S", sg2)
        ld8(pwrT, pwrT_h)
        pw_ld = ld8(pwiT, pwiT_h)

    if p2sub <= 2:
        b.full_barrier()
        return _finish()

    for buf in (cpr, cpi):
        pad = buf.ap().rearrange("p (q w) -> p q w", w=35)[:, :, 0:6]
        b.plain("vector", lambda e, o=pad: e.memset(o, 0.0))
    vz = b.seal("vector", "V")

    b.wait("tensor", "V", vz)
    b.wait("tensor", "gq", pw_ld)
    prev_evac = pa_done[NPAIR - 1]
    grp_last = {}
    for g in range(NGRP):
        pair_evac = {}
        for pp in range(4):
            p = 4 * g + pp
            xslot = (p % 2) * 1024
            free_v = pair_evac.get(pp - 2, prev_evac)
            b.wait("sync", "V", free_v)
            lxp = ld(xnbp.ap()[:, xslot:xslot + 1024],
                     xnb_s[128 * p:128 * p + 128, :], "p2x%d" % (p % 2))
            b.wait("tensor", "p2x%d" % (p % 2), lxp)
            b.wait("tensor", "V", pair_evac.get(pp - 1, prev_evac))
            for half in range(2):
                ch = 2 * p + half
                gc = ch - G_CH * g
                base = 64 * half
                pb = (ch % 2) * 2
                for dblk in range(8):
                    lhs = xnbp.ap()[base:base + 64,
                                    xslot + 128 * dblk:xslot + 128 * dblk + 128]
                    rhs = F64b.ap()[base:base + 64, :]
                    bk = pb + dblk // 4
                    o = psA.ap()[:, 512 * bk + 128 * (dblk % 4):
                                 512 * bk + 128 * (dblk % 4) + 66]
                    b.plain("tensor", lambda e, o=o, l=lhs, r=rhs:
                            e.matmul(o, l, r, start=True, stop=True))
                trf = b.seal("tensor", "T")
                b.wait("vector", "T", trf)
                for hb2 in range(2):
                    src = psA.ap()[:, 512 * (pb + hb2):512 * (pb + hb2) + 512]
                    src3 = src.rearrange("p (s w) -> p s w", w=128)
                    for comp, buf in ((0, cpr), (1, cpi)):
                        co = 4 if comp == 0 else 37
                        dst = buf.ap().rearrange(
                            "p (db g2 w) -> p db g2 w", w=35, g2=G_CH)[
                            :, 4 * hb2:4 * hb2 + 4, gc:gc + 1, 6:35]
                        s4 = src3[:, :, co:co + LHI].unsqueeze(2)
                        b.plain("vector", lambda e, o=dst, s=s4:
                                e.tensor_copy(o, s))
            pair_evac[pp] = b.seal("vector", "V")
        prev_evac = pair_evac[3]

        for comp in (range(2) if p2sub >= 4 else range(0)):
            cpad = cpr if comp == 0 else cpi
            pwT = pwrT if comp == 0 else pwiT
            spec = SpecTr if comp == 0 else SpecTi
            b.wait("tensor", "V", prev_evac)
            for dblk in range(8):
                pb2 = dblk % 2
                for j in range(KS):
                    lhs = dwdg_ap(comp, j, dblk)
                    base_col = G_CH * 35 * dblk
                    rhs = cpad.ap()[:, base_col:base_col + G_CH * 35].rearrange(
                        "p (g2 w) -> p g2 w", w=35)[:, :, j:j + LHI]
                    o = bankB(pb2)[:, 0:GW]
                    b.plain("tensor", lambda e, o=o, l=lhs, r=rhs, st=(j == 0),
                            sp=(j == KS - 1): e.matmul(o, l, r, start=st, stop=sp))
                tcv = b.seal("tensor", "T")
                b.wait("scalar", "T", tcv)
                b.plain("scalar", lambda e, o=Hbt.ap()[:, GW * dblk:GW * (dblk + 1)],
                        s=bankB(pb2)[:, 0:GW]: e.activation(o, s, AF.Gelu))
                sck = b.seal("scalar", "S")
                b.wait("tensor", "S", sck)
            for od in (range(8) if p2sub >= 5 else range(0)):
                pb2 = 2 + od % 2
                for K in range(8):
                    lhs = pwT.ap()[:, 1024 * K + 128 * od:1024 * K + 128 * od + 128]
                    rhs = Hbt.ap()[:, GW * K:GW * (K + 1)]
                    o = bankB(pb2)[:, 0:GW]
                    b.plain("tensor", lambda e, o=o, l=lhs, r=rhs, st=(K == 0),
                            sp=(K == 7): e.matmul(o, l, r, start=st, stop=sp))
                tpw = b.seal("tensor", "T")
                b.wait("vector", "T", tpw)
                gsl = G_T.ap()[:, 64 * od + G_CH * g:64 * od + G_CH * (g + 1)] \
                    .unsqueeze(2).broadcast_to([128, G_CH, LHI])
                doff = NCHUNK * M * od + M * G_CH * g
                dst = spec.ap()[:, doff:doff + M * G_CH].rearrange(
                    "p (ch m) -> p ch m", m=M)[:, :, 4:33]
                src = bankB(pb2)[:, 0:GW].rearrange("p (ch m) -> p ch m", m=LHI)
                b.plain("vector", lambda e, o=dst, s=src, gg=gsl:
                        e.tensor_tensor(o, s, gg, op=AluOpType.mult))
                vpw = b.seal("vector", "V")
                b.wait("tensor", "V", vpw)

    b.full_barrier()
    es2.close()
    if phases <= 2:
        return _finish()

    es3 = ExitStack()
    qwT = sb("qwT_sb", (128, 8 * D), BF16, es3)
    kwT = sb("kwT_sb", (128, 8 * D), BF16, es3)
    vwT = sb("vwT_sb", (128, 8 * D), BF16, es3)
    owT = sb("owT_sb", (128, 8 * D), BF16, es3)
    XlTbr = sb("XlTbr", (128, 2048), BF16, es3)
    XlTbi = sb("XlTbi", (128, 2048), BF16, es3)
    QTb = sb("QTb", (128, 2048), BF16, es3)
    KTb = sb("KTb", (128, 2048), BF16, es3)
    Vtok = sb("Vtok", (128, 2048), BF16, es3)
    OTb = sb("OTb", (128, 2048), BF16, es3)
    Ssb = sb("Ssb", (128, 512), F32, es3)
    Pf = sb("Pf", (128, 512), F32, es3)
    Pbf = sb("Pbf", (128, 512), BF16, es3)
    PbT = sb("PbT", (128, 512), BF16, es3)
    smst = sb("smst", (128, 8), F32, es3)
    d1b = sb("d1b", (128, 256), F32, es3)

    ld8(qwT, qwT_h)
    half_ld = ld8(kwT, kwT_h)
    b.wait("gpsimd", "gq", half_ld)
    ld8(vwT, vwT_h)
    aw_ld = ld8(owT, owT_h)
    b.wait("vector", "gq", aw_ld)
    b.plain("vector", lambda e: e.tensor_copy(XlTbr.ap(), XlTr.ap()))
    b.inc("vector", lambda e: e.tensor_copy(XlTbi.ap(), XlTi.ap()), "V")
    b.full_barrier()

    for comp in range(2):
        XlTb = XlTbr if comp == 0 else XlTbi
        XlTf = XlTr if comp == 0 else XlTi
        spec = SpecTr if comp == 0 else SpecTi

        for wT, dstb in ((qwT, QTb), (kwT, KTb)):
            for mtl in range(8):
                pbk = mtl % 4
                slot = 256 * (mtl // 4)
                for K in range(8):
                    lhs = wT.ap()[:, 1024 * K + 128 * mtl:1024 * K + 128 * mtl + 128]
                    rhs = XlTb.ap()[:, 256 * K:256 * K + 256]
                    o = bankA(pbk)[:, slot:slot + 256]
                    b.plain("tensor", lambda e, o=o, l=lhs, r=rhs, st=(K == 0),
                            sp=(K == 7): e.matmul(o, l, r, start=st, stop=sp))
            tq = b.seal("tensor", "T")
            b.wait("scalar", "T", tq)
            for mtl in range(8):
                pbk = mtl % 4
                slot = 256 * (mtl // 4)
                b.plain("scalar", lambda e,
                        o=dstb.ap()[:, 256 * mtl:256 * mtl + 256],
                        s=bankA(pbk)[:, slot:slot + 256]: e.copy(o, s))
            sq = b.seal("scalar", "S")
            b.wait("tensor", "S", sq)
        for mt in range(2):
            for K in range(8):
                lhs = XlTb.ap()[:, 256 * K + 128 * mt:256 * K + 128 * mt + 128]
                for c2 in range(2):
                    rhs = vwT.ap()[:, 1024 * K + 512 * c2:1024 * K + 512 * c2 + 512]
                    o = bankA(2 * mt + c2)
                    b.plain("tensor", lambda e, o=o, l=lhs, r=rhs, st=(K == 0),
                            sp=(K == 7): e.matmul(o, l, r, start=st, stop=sp))
        tv = b.seal("tensor", "T")
        b.wait("scalar", "T", tv)
        for k4 in range(4):
            b.plain("scalar", lambda e, o=Vtok.ap()[:, 512 * k4:512 * k4 + 512],
                    s=bankA(k4): e.copy(o, s))
        sv2 = b.seal("scalar", "S")
        b.wait("tensor", "S", sv2)

        prev_v = 0
        for h in range(16):
            base = 64 * (h % 2)
            tl = h // 2
            if prev_v:
                b.wait("tensor", "V", prev_v)
            for qm in range(2):
                lhs = QTb.ap()[base:base + 64,
                               256 * tl + 128 * qm:256 * tl + 128 * qm + 128]
                rhs = KTb.ap()[base:base + 64, 256 * tl:256 * tl + 256]
                o = bankB(qm)[:, 0:256]
                b.plain("tensor", lambda e, o=o, l=lhs, r=rhs:
                        e.matmul(o, l, r, start=True, stop=True))
            ts2 = b.seal("tensor", "T")
            b.wait("vector", "T", ts2)
            for qm in range(2):
                b.plain("vector", lambda e, o=Ssb.ap()[:, 256 * qm:256 * qm + 256],
                        s=bankB(qm)[:, 0:256],
                        m=maskt.ap()[:, 256 * qm:256 * qm + 256]:
                        e.tensor_tensor(o, s, m, op=AluOpType.add))
            b.plain("vector", lambda e: e.drain())
            for qm in range(2):
                b.plain("vector", lambda e, o=smst.ap()[:, qm:qm + 1],
                        s=Ssb.ap()[:, 256 * qm:256 * qm + 256]:
                        e.reduce_max(o, s, axis=AX.X))
            b.plain("vector", lambda e: e.drain())
            for qm in range(2):
                b.plain("vector", lambda e, o=smst.ap()[:, 2 + qm:3 + qm],
                        s=smst.ap()[:, qm:qm + 1]: e.tensor_scalar_mul(o, s, -0.125))
            vs1 = b.seal("vector", "V")
            prev_v = vs1
            b.wait("scalar", "V", vs1)
            for qm in range(2):
                b.plain("scalar", lambda e, o=Pf.ap()[:, 256 * qm:256 * qm + 256],
                        s=Ssb.ap()[:, 256 * qm:256 * qm + 256],
                        bi=smst.ap()[:, 2 + qm:3 + qm],
                        ac=smst.ap()[:, 4 + qm:5 + qm]:
                        e.activation(o, s, AF.Exp, scale=ccol("smx"), bias=bi,
                                     accum_out=ac))
            se1 = b.seal("scalar", "S")
            b.wait("vector", "S", se1)
            b.plain("vector", lambda e: e.reciprocal(smst.ap()[:, 6:8],
                                                     smst.ap()[:, 4:6]))
            vs2 = b.seal("vector", "V")
            b.wait("scalar", "V", vs2)
            for qm in range(2):
                b.plain("scalar", lambda e, o=Pbf.ap()[:, 256 * qm:256 * qm + 256],
                        s=Pf.ap()[:, 256 * qm:256 * qm + 256],
                        sc2=smst.ap()[:, 6 + qm:7 + qm]:
                        e.activation(o, s, AF.Copy, scale=sc2))
            se2 = b.seal("scalar", "S")
            b.wait("tensor", "S", se2)
            for blk in range(4):
                qm, kb = blk % 2, blk // 2
                lhs = Pbf.ap()[:, 256 * qm + 128 * kb:256 * qm + 128 * kb + 128]
                o = psB.ap()[:, 1024 + 128 * (2 * kb + qm):
                             1024 + 128 * (2 * kb + qm) + 128]
                b.plain("tensor", lambda e, o=o, l=lhs:
                        e.matmul(o, l, I128b.ap(), start=True, stop=True))
            tpt = b.seal("tensor", "T")
            b.wait("scalar", "T", tpt)
            b.plain("scalar", lambda e: e.copy(PbT.ap(), psB.ap()[:, 1024:1536]))
            spt = b.seal("scalar", "S")
            b.wait("tensor", "S", spt)
            for kt in range(2):
                lhs = Vtok.ap()[:, 1024 * kt + 64 * h:1024 * kt + 64 * h + 64]
                rhs = PbT.ap()[:, 256 * kt:256 * kt + 256]
                o = psB.ap()[base:base + 64, 1536:1536 + 256]
                b.plain("tensor", lambda e, o=o, l=lhs, r=rhs, st=(kt == 0),
                        sp=(kt == 1), tp=(0, base):
                        e.matmul(o, l, r, start=st, stop=sp, tile_position=tp))
            to = b.seal("tensor", "T")
            b.wait("scalar", "T", to)
            b.plain("scalar", lambda e, o=OTb.ap()[base:base + 64,
                                                   256 * tl:256 * tl + 256],
                    s=psB.ap()[base:base + 64, 1536:1536 + 256]: e.copy(o, s))
            so = b.seal("scalar", "S")
            b.wait("tensor", "S", so)

        for mtl in range(8):
            pbk = mtl % 2
            for K in range(8):
                lhs = owT.ap()[:, 1024 * K + 128 * mtl:1024 * K + 128 * mtl + 128]
                rhs = OTb.ap()[:, 256 * K:256 * K + 256]
                o = bankA(pbk)[:, 0:256]
                b.plain("tensor", lambda e, o=o, l=lhs, r=rhs, st=(K == 0),
                        sp=(K == 7): e.matmul(o, l, r, start=st, stop=sp))
            tso = b.seal("tensor", "T")
            b.wait("vector", "T", tso)
            b.plain("vector", lambda e, s=bankA(pbk)[:, 0:256],
                    xl=XlTf.ap()[:, 256 * mtl:256 * mtl + 256]:
                    e.tensor_tensor(d1b.ap(), s, xl, op=AluOpType.subtract))
            b.plain("vector", lambda e: e.drain())
            doff = NCHUNK * M * mtl
            dst = spec.ap()[:, doff:doff + NCHUNK * M].rearrange(
                "p (ch m) -> p ch m", m=M)[:, :, 0:4]
            b.plain("vector", lambda e, o=dst,
                    s=d1b.ap().rearrange("p (ch m) -> p ch m", m=4):
                    e.tensor_scalar(o, s, alpha_sg.ap()[:, 0:1], None,
                                    op0=AluOpType.mult))
            vso = b.seal("vector", "V")
            b.wait("tensor", "V", vso)

    b.full_barrier()
    es3.close()
    es23.close()
    if phases <= 3:
        return _finish()

    es4 = ExitStack()
    slotR = sb("slotR", (66, 1024), BF16, es4)
    slotI = sb("slotI", (66, 1024), BF16, es4)
    xnb4 = sb("xnb4", (128, 2048), BF16, es4)
    xp4 = sb("xp4", (128, 2048), F32, es4)
    gp4 = sb("gp4", (128, 2048), F32, es4)
    ub4 = sb("ub4", (128, 1024), F32, es4)
    ob4 = sb("ob4", (128, 2048), F32, es4)

    p4 = {}
    for p in range(NPAIR):
        sl = (p % 2) * 1024
        psem = "p4l%d" % (p % 2)
        if p >= 2:
            b.wait("sync", "V", p4[(p - 2, "u")])
        l1 = ld(xnb4.ap()[:, sl:sl + 1024], xnb_s[128 * p:128 * p + 128, :], psem)
        ld(xp4.ap()[:, sl:sl + 1024], x_h[128 * p:128 * p + 128, :], psem)
        l3 = ld(gp4.ap()[:, sl:sl + 1024], g_s[128 * p:128 * p + 128, :], psem)
        if p >= 1:
            b.wait("tensor", "S", p4[(p - 1, "slots")])
        for comp in range(2):
            spec = SpecTr if comp == 0 else SpecTi
            for dblk in range(8):
                lhs = spec.ap()[:, NCHUNK * M * dblk + 66 * p:
                                NCHUNK * M * dblk + 66 * p + 66]
                bank = 2 * comp + dblk // 4
                o = psA.ap()[0:66, 512 * bank + 128 * (dblk % 4):
                             512 * bank + 128 * (dblk % 4) + 128]
                b.plain("tensor", lambda e, o=o, l=lhs:
                        e.matmul(o, l, I128b.ap(), start=True, stop=True))
        ttr = b.seal("tensor", "T")
        b.wait("scalar", "T", ttr)
        for comp in range(2):
            dstb = slotR if comp == 0 else slotI
            for hb2 in range(2):
                bank = 2 * comp + hb2
                b.plain("scalar", lambda e,
                        o=dstb.ap()[0:66, 512 * hb2:512 * hb2 + 512],
                        s=psA.ap()[0:66, 512 * bank:512 * bank + 512]:
                        e.copy(o, s))
        p4[(p, "slots")] = b.seal("scalar", "S")
        b.wait("tensor", "S", p4[(p, "slots")])
        b.wait("tensor", psem, l3)
        if p >= 1:
            b.wait("tensor", "V", p4[(p - 1, "u")])
        for c2 in range(2):
            rhs = xnb4.ap()[:, sl + 512 * c2:sl + 512 * c2 + 512]
            b.plain("tensor", lambda e, o=bankB(c2), r=rhs:
                    e.matmul(o, I128b.ap(), r, start=True, stop=False))
        for c2 in range(2):
            rhs = slotR.ap()[0:66, 512 * c2:512 * c2 + 512]
            b.plain("tensor", lambda e, o=bankB(c2), r=rhs:
                    e.matmul(o, Wr2.ap(), r, start=False, stop=False))
        for c2 in range(2):
            rhs = slotI.ap()[0:66, 512 * c2:512 * c2 + 512]
            b.plain("tensor", lambda e, o=bankB(c2), r=rhs:
                    e.matmul(o, Wi2.ap(), r, start=False, stop=True))
        ty = b.seal("tensor", "T")
        b.wait("vector", "T", ty)
        b.wait("vector", psem, l3)
        if p >= 2:
            b.wait("vector", "p4s%d" % (p % 2), p4[(p - 2, "done")])
        b.plain("vector", lambda e, s=sl: e.tensor_tensor(
            ub4.ap(), psB.ap()[:, 0:1024], gp4.ap()[:, s:s + 1024],
            op=AluOpType.mult))
        b.plain("vector", lambda e: e.drain())
        b.plain("vector", lambda e, s=sl: e.tensor_tensor(
            ob4.ap()[:, s:s + 1024], ub4.ap(), xp4.ap()[:, s:s + 1024],
            op=AluOpType.add))
        p4[(p, "u")] = b.seal("vector", "V")
        b.wait("sync", "V", p4[(p, "u")])
        p4[(p, "done")] = b.inc("sync", lambda e, s=sl, p=p: e.dma_start(
            out=out_h[128 * p:128 * p + 128, :], in_=ob4.ap()[:, s:s + 1024]),
            "p4s%d" % (p % 2), 16)

    es4.close()
    esM.close()
    return _finish()


def kernel(L0_attn_hidden_states, L0_attn_q_w, L0_attn_k_w, L0_attn_v_w,
           L0_attn_o_w, L0_attn_dw_r, L0_attn_pw_r, L0_attn_dw_i, L0_attn_pw_i,
           L0_attn_gate_w, L0_attn_alpha_raw, L0_ffn_w_fc, L0_ffn_b_fc,
           L0_ffn_w_proj, L0_ffn_b_proj):
    x = np.ascontiguousarray(np.asarray(L0_attn_hidden_states, dtype=np.float32))
    qw = np.asarray(L0_attn_q_w, np.float32)
    kw = np.asarray(L0_attn_k_w, np.float32)
    vw = np.asarray(L0_attn_v_w, np.float32)
    ow = np.asarray(L0_attn_o_w, np.float32)
    gw = np.asarray(L0_attn_gate_w, np.float32)
    pwr = np.asarray(L0_attn_pw_r, np.float32)
    pwi = np.asarray(L0_attn_pw_i, np.float32)
    dwr = np.ascontiguousarray(np.asarray(L0_attn_dw_r, np.float32)[:, 0, :])
    dwi = np.ascontiguousarray(np.asarray(L0_attn_dw_i, np.float32)[:, 0, :])
    alpha = np.asarray(L0_attn_alpha_raw, np.float32)
    wfc = np.ascontiguousarray(np.asarray(L0_ffn_w_fc, np.float32))
    bfc = np.asarray(L0_ffn_b_fc, np.float32)
    wpj = np.ascontiguousarray(np.asarray(L0_ffn_w_proj, np.float32))
    bpj = np.asarray(L0_ffn_b_proj, np.float32)

    kw_rep = np.repeat(kw.reshape(4, 64, D), 4, axis=0).reshape(D, D)
    vw_rep = np.repeat(vw.reshape(4, 64, D), 4, axis=0).reshape(D, D)
    consts = make_consts()
    shared = {
        "qwT": np.ascontiguousarray(qw.T),
        "kwT": np.ascontiguousarray(kw_rep.T),
        "vwT": np.ascontiguousarray(vw_rep.T),
        "owT": np.ascontiguousarray(ow.T),
        "gwT": np.ascontiguousarray(gw.T),
        "pwrT": np.ascontiguousarray(pwr.T),
        "pwiT": np.ascontiguousarray(pwi.T),
        "dwr": dwr, "dwi": dwi,
        "alpha128": np.ascontiguousarray(np.repeat(alpha.reshape(1, 1), 128, 0)),
        "bfc": np.ascontiguousarray(bfc.reshape(1, HID)),
        "bproj": np.ascontiguousarray(bpj.reshape(1, D)),
        "wfc": wfc, "wproj": wpj,
        **consts,
    }

    nc = build_program()
    in_maps = [dict(shared, x=x[bb]) for bb in range(8)]
    res = run_bass_kernel_spmd(nc, in_maps, core_ids=list(range(8)))
    return np.stack([res.results[i]["out"] for i in range(8)], axis=0)


# version 07
# speedup vs baseline: 1.5344x; 1.0064x over previous
import numpy as np
import ml_dtypes
import concourse.bass as bass
import concourse.mybir as mybir
from concourse.bass_utils import run_bass_kernel_spmd
from concourse.alu_op_type import AluOpType
from contextlib import ExitStack

F32 = mybir.dt.float32
BF16 = mybir.dt.bfloat16
I32 = mybir.dt.int32
AF = mybir.ActivationFunctionType
AX = mybir.AxisListType

LN2 = float(np.log(2.0))
N_TOK = 4096
D = 1024
NTILES = 32
C = 64
M = 33
KLOW = 4
LHI = 29
KS = 7
NCHUNK = 64
NPAIR = 32
HID = 4096
G_CH = 8
NGRP = NCHUNK // G_CH
GW = G_CH * LHI

ENGS = ("sync", "gpsimd", "tensor", "vector", "scalar")


def make_consts():
    c = np.arange(C)
    m = np.arange(M)
    ang = 2 * np.pi * np.outer(c, m) / C
    Fc = np.cos(ang).astype(np.float32)
    Fs = (-np.sin(ang)).astype(np.float32)
    F64 = np.zeros((128, 66), np.float32)
    F64[0:64, 0:33] = Fc
    F64[0:64, 33:66] = Fs
    F64[64:128] = F64[0:64]
    Flow = np.zeros((128, 8), np.float32)
    Flow[0:64, 0:4] = Fc[:, 0:4]
    Flow[0:64, 4:8] = Fs[:, 0:4]
    Flow[64:128] = Flow[0:64]

    mm_ = np.arange(M)[:, None]
    cc_ = np.arange(C)[None, :]
    s2 = np.where((mm_ >= 1) & (mm_ <= 31), 2.0, 1.0)
    wr = (np.cos(2 * np.pi * mm_ * cc_ / C) / C * s2).astype(np.float32)
    wi = (-np.sin(2 * np.pi * mm_ * cc_ / C) / C * s2).astype(np.float32)
    wi[0] = 0.0
    wi[32] = 0.0
    Wr2 = np.zeros((66, 128), np.float32)
    Wi2 = np.zeros((66, 128), np.float32)
    Wr2[0:33, 0:64] = wr
    Wr2[33:66, 64:128] = wr
    Wi2[0:33, 0:64] = wi
    Wi2[33:66, 64:128] = wi

    cid = np.arange(256) // 4
    mask8 = np.where(cid[:, None] >= cid[None, :], 0.0, -8.0e9).astype(np.float32)

    bf = ml_dtypes.bfloat16
    return {
        "F64b": F64.astype(bf),
        "FlowB": Flow.astype(bf),
        "Wr2": Wr2.astype(bf),
        "Wi2": Wi2.astype(bf),
        "I128b": np.eye(128, dtype=bf),
        "ones2": np.ones((2, 128), dtype=bf),
        "maskc": mask8,
    }


class B:
    def __init__(self):
        self.ops = {e: [] for e in ENGS}
        self.c = {}
        self.sem = {}

    def add_sem(self, name, handle):
        self.sem[name] = handle
        self.c[name] = 0

    def inc(self, eng, make, sem, n=1):
        h = self.sem[sem]
        self.ops[eng].append(lambda e, make=make, h=h, n=n: make(e).then_inc(h, n))
        self.c[sem] += n
        return self.c[sem]

    def plain(self, eng, make):
        self.ops[eng].append(make)

    def seal(self, eng, sem, n=1):
        h = self.sem[sem]
        last = self.ops[eng].pop()
        self.ops[eng].append(lambda e, last=last, h=h, n=n:
                             last(e).then_inc(h, n))
        self.c[sem] += n
        return self.c[sem]

    def wait(self, eng, sem, val=None):
        h = self.sem[sem]
        v = self.c[sem] if val is None else val
        if v > 0:
            self.ops[eng].append(lambda e, h=h, v=v: e.wait_ge(h, v))
        return v

    def full_barrier(self):
        vals = dict(self.c)
        for e in ENGS:
            for s, v in vals.items():
                self.wait(e, s, v)


def build_program(phases=4, p2sub=9):
    nc = bass.Bass("TRN2", target_bir_lowering=False,
                   dynamic_dma_scratch_size=16384)

    def din(name, shape, dtype):
        return nc.dram_tensor(name, list(shape), dtype, kind="ExternalInput").ap()

    x_h = din("x", (N_TOK, D), F32)
    qwT_h = din("qwT", (D, D), F32)
    kwT_h = din("kwT", (D, D), F32)
    vwT_h = din("vwT", (D, D), F32)
    owT_h = din("owT", (D, D), F32)
    gwT_h = din("gwT", (D, D), F32)
    pwrT_h = din("pwrT", (D, D), F32)
    pwiT_h = din("pwiT", (D, D), F32)
    dwr_h = din("dwr", (D, KS), F32)
    dwi_h = din("dwi", (D, KS), F32)
    alpha_h = din("alpha128", (128, 1), F32)
    bfc_h = din("bfc", (1, HID), F32)
    bpj_h = din("bproj", (1, D), F32)
    wfc_h = din("wfc", (HID, D), F32)
    wpj_h = din("wproj", (D, HID), F32)
    F64b_h = din("F64b", (128, 66), BF16)
    FlowB_h = din("FlowB", (128, 8), BF16)
    Wr2_h = din("Wr2", (66, 128), BF16)
    Wi2_h = din("Wi2", (66, 128), BF16)
    I128b_h = din("I128b", (128, 128), BF16)
    ones2_h = din("ones2", (2, 128), BF16)
    maskc_h = din("maskc", (256, 256), F32)

    out_h = nc.dram_tensor("out", [N_TOK, D], F32, kind="ExternalOutput").ap()
    xnb_s = nc.dram_tensor("xnb_s", [N_TOK, D], BF16, kind="Internal").ap()
    g_s = nc.dram_tensor("g_s", [N_TOK, D], F32, kind="Internal").ap()

    esg = ExitStack()

    def sb(name, shape, dtype, stack):
        return stack.enter_context(nc.sbuf_tensor(name, list(shape), dtype))

    cst = sb("cst", (128, 16), F32, esg)
    I128b = sb("I128b_sb", (128, 128), BF16, esg)
    F64b = sb("F64b_sb", (128, 66), BF16, esg)
    FlowB = sb("FlowB_sb", (128, 8), BF16, esg)
    Wr2 = sb("Wr2_sb", (66, 128), BF16, esg)
    Wi2 = sb("Wi2_sb", (66, 128), BF16, esg)
    ones2 = sb("ones2_sb", (2, 128), BF16, esg)
    maskt = sb("maskt", (128, 512), F32, esg)
    alpha_t = sb("alpha_t", (128, 1), F32, esg)
    alpha_sg = sb("alpha_sg", (128, 1), F32, esg)
    bspl_fc = sb("bspl_fc", (2, HID), BF16, esg)
    bspl_pj = sb("bspl_pj", (2, D), BF16, esg)
    dwr_t = sb("dwr_t", (128, 8 * KS), F32, esg)
    dwi_t = sb("dwi_t", (128, 8 * KS), F32, esg)

    psA = esg.enter_context(nc.psum_tensor("psA", [128, 2048], F32))
    psB = esg.enter_context(nc.psum_tensor("psB", [128, 2048], F32))

    b = B()
    for s in ("cst", "gq", "T", "V", "S", "wst0", "wst1", "x0", "x1",
              "xnbst", "gst", "p2x0", "p2x1", "p4l0", "p4l1", "p4s0", "p4s1", "bld", "bspl"):
        b.add_sem(s, esg.enter_context(nc.semaphore("sem_" + s)))

    def bankA(k, w=512):
        return psA.ap()[:, 512 * k:512 * k + w]

    def bankB(k, w=512):
        return psB.ap()[:, 512 * k:512 * k + w]


    def _finish():
        for s in b.c:
            if s not in ("T", "V", "S"):
                b.wait("sync", s)
        with nc.Block() as block:
            @block.sync
            def _(eng):
                for fn in b.ops["sync"]:
                    fn(eng)

            @block.gpsimd
            def _(eng):
                for fn in b.ops["gpsimd"]:
                    fn(eng)

            @block.tensor
            def _(eng):
                for fn in b.ops["tensor"]:
                    fn(eng)

            @block.vector
            def _(eng):
                for fn in b.ops["vector"]:
                    fn(eng)

            @block.scalar
            def _(eng):
                for fn in b.ops["scalar"]:
                    fn(eng)
        return nc

    def ld(dst, src, sem="cst"):
        return b.inc("sync", lambda e, d=dst, s=src: e.dma_start(out=d, in_=s),
                     sem, 16)

    def ld8(dst_sb, src_h):
        for k in range(8):
            b.inc("gpsimd", lambda e, d=dst_sb.ap()[:, 1024 * k:1024 * k + 1024],
                  s=src_h[128 * k:128 * k + 128, :]: e.dma_start(out=d, in_=s),
                  "gq", 16)
        return b.c["gq"]

    ld(I128b.ap(), I128b_h)
    ld(F64b.ap(), F64b_h)
    ld(FlowB.ap(), FlowB_h)
    ld(Wr2.ap(), Wr2_h)
    ld(Wi2.ap(), Wi2_h)
    ld(ones2.ap(), ones2_h)
    ld(maskt.ap()[0:128, 0:256], maskc_h[0:128, :])
    ld(maskt.ap()[0:128, 256:512], maskc_h[128:256, :])
    ld(alpha_t.ap(), alpha_h)
    for k in range(8):
        ld(dwr_t.ap()[:, KS * k:KS * k + KS], dwr_h[128 * k:128 * k + 128, :])
        ld(dwi_t.ap()[:, KS * k:KS * k + KS], dwi_h[128 * k:128 * k + 128, :])
    consts_loaded = b.c["cst"]

    CST = {"lnscale": 0, "eps": 1, "nln2": 2, "p3ln2": 3, "pln2": 4, "n3ln2": 5,
           "smx": 6, "invln2": 7, "mhalf": 8}
    cvals = [1.0 / 1024.0, 1e-5, -LN2, 3 * LN2, LN2, -3 * LN2, 0.125, 1.0 / LN2,
             -0.5]
    for col in range(9):
        b.plain("vector", lambda e, c=col, v=cvals[col]:
                e.memset(cst.ap()[:, c:c + 1], v))
    b.seal("vector", "V")

    def ccol(name):
        i = CST[name]
        return cst.ap()[:, i:i + 1]

    b.wait("scalar", "cst", consts_loaded)
    b.wait("scalar", "V")
    b.inc("scalar", lambda e: e.activation(alpha_sg.ap(), alpha_t.ap(), AF.Sigmoid),
          "S")
    es01 = ExitStack()
    WfcqT = sb("WfcqT", (128, 8 * HID), BF16, es01)
    WpjqT = sb("WpjqT", (128, 32 * D), BF16, es01)

    tb = sb("tb", (128, 2048), F32, es01)
    ib = sb("ib", (128, 2048), I32, es01)
    amx = sb("amx", (128, 96), F32, es01)
    yq = sb("yq", (128, 96), F32, es01)
    e32q = sb("e32q", (128, 96), I32, es01)
    rsq = sb("rsq", (128, 96), F32, es01)
    scq = sb("scq", (128, 96), F32, es01)
    es_stg = ExitStack()
    wstg = sb("wstg", (128, 2048), F32, es_stg)
    wqb = sb("wqb", (128, 2048), BF16, es_stg)
    bst_f = sb("bst_f", (1, 1024), F32, es_stg)
    bst_hi = sb("bst_hi", (1, 1024), BF16, es_stg)
    bst_lo = sb("bst_lo", (1, 1024), BF16, es_stg)

    for bi, (src_hbm, dstt, n_ch) in enumerate(((bfc_h, bspl_fc, 4),
                                                (bpj_h, bspl_pj, 1))):
        for cchunk in range(n_ch):
            bldv = ld(bst_f.ap(), src_hbm[0:1, 1024 * cchunk:1024 * cchunk + 1024],
                      "bld")
            b.wait("vector", "bld", bldv)
            b.plain("vector", lambda e: e.tensor_copy(bst_hi.ap(), bst_f.ap()))
            b.plain("vector", lambda e: e.drain())
            b.plain("vector", lambda e: e.tensor_tensor(
                bst_lo.ap(), bst_f.ap(), bst_hi.ap(), op=AluOpType.subtract))
            vsp = b.seal("vector", "V")
            b.wait("sync", "V", vsp)
            for r, srcv in ((0, bst_hi), (1, bst_lo)):
                b.inc("sync", lambda e, s=srcv, d=dstt, r=r, c=cchunk:
                      e.dma_start(out=d.ap()[r:r + 1, 1024 * c:1024 * c + 1024],
                                  in_=s.ap()[0:1, :]), "bspl", 16)
            b.wait("vector", "bspl")

    def q_views(src_ap, nblk, dst_ap, acol, tcol):
        W = 32 * nblk
        rs = rsq.ap()[:, acol:acol + nblk]
        sc = scq.ap()[:, acol:acol + nblk]
        return {
            "a": amx.ap()[:, acol:acol + nblk],
            "y": yq.ap()[:, acol:acol + nblk],
            "e32": e32q.ap()[:, acol:acol + nblk],
            "rs": rs, "sc": sc,
            "src3": src_ap.rearrange("p (nb k) -> p nb k", k=32),
            "tw": tb.ap()[:, tcol:tcol + W],
            "iw": ib.ap()[:, tcol:tcol + W],
            "t3": tb.ap()[:, tcol:tcol + W].rearrange("p (nb k) -> p nb k", k=32),
            "i3": ib.ap()[:, tcol:tcol + W].rearrange("p (nb k) -> p nb k", k=32),
            "dst3": dst_ap.rearrange("p (nb k) -> p nb k", k=32),
            "rs3": rs.unsqueeze(2).broadcast_to([128, nblk, 32]),
            "sc3": sc.unsqueeze(2).broadcast_to([128, nblk, 32]),
        }

    def q_amax(qv):
        b.plain("vector", lambda e: e.tensor_reduce(
            qv["a"], qv["src3"], axis=AX.X, op=AluOpType.max,
            apply_absolute_value=True))
        qv["vA"] = b.seal("vector", "V")

    def q_scales(qv):
        b.plain("vector", lambda e: e.drain())
        b.plain("vector", lambda e: e.tensor_scalar_max(qv["a"], qv["a"], 1e-37))
        b.plain("vector", lambda e: e.drain())
        b.plain("vector", lambda e: e.tensor_scalar(
            qv["e32"], qv["a"].bitcast(I32), 0x7F800000, None,
            op0=AluOpType.bitwise_and))
        b.plain("vector", lambda e: e.drain())
        b.plain("vector", lambda e: e.tensor_scalar_mul(
            qv["sc"], qv["e32"].bitcast(F32), 0.125))
        b.plain("vector", lambda e: e.drain())
        b.plain("vector", lambda e: e.reciprocal(qv["rs"], qv["sc"]))
        b.plain("vector", lambda e: e.drain())
        qv["sB"] = None

    def q_apply(qv):
        b.plain("vector", lambda e: e.tensor_tensor(
            qv["t3"], qv["src3"], qv["rs3"], op=AluOpType.mult))
        b.plain("vector", lambda e: e.drain())
        b.plain("vector", lambda e: e.tensor_copy(qv["iw"], qv["tw"]))
        b.plain("vector", lambda e: e.drain())
        b.plain("vector", lambda e: e.tensor_tensor(
            qv["dst3"], qv["i3"], qv["sc3"], op=AluOpType.mult))
        qv["done"] = b.seal("vector", "V")

    def t_transpose8(src_bf_ap):
        for blk in range(8):
            bank = blk // 4
            o = psB.ap()[:, 512 * bank + 128 * (blk % 4):
                         512 * bank + 128 * (blk % 4) + 128]
            sblk = src_bf_ap[:, 128 * blk:128 * blk + 128]
            b.plain("tensor", lambda e, o=o, l=sblk:
                    e.matmul(o, l, I128b.ap(), start=True, stop=True))
        return b.seal("tensor", "T")

    def s_evac8(dst_views, tT):
        b.wait("scalar", "T", tT)
        for blk in range(8):
            bank = blk // 4
            s = psB.ap()[:, 512 * bank + 128 * (blk % 4):
                         512 * bank + 128 * (blk % 4) + 128]
            b.plain("scalar", lambda e, o=dst_views[blk], s=s: e.copy(o, s))
        return b.seal("scalar", "S")

    def p0_src(u):
        if u < 32:
            return wfc_h[128 * u:128 * u + 128, :]
        t8, qv_ = (u - 32) // 4, (u - 32) % 4
        return wpj_h[128 * t8:128 * t8 + 128, 1024 * qv_:1024 * qv_ + 1024]

    def p0_dsts(u):
        if u < 32:
            return [WfcqT.ap()[:, HID * j + 128 * u:HID * j + 128 * u + 128]
                    for j in range(8)]
        t8, qv_ = (u - 32) // 4, (u - 32) % 4
        return [WpjqT.ap()[:, D * (8 * qv_ + j) + 128 * t8:
                           D * (8 * qv_ + j) + 128 * t8 + 128] for j in range(8)]

    NU = 64
    uq = {}
    uld = {}
    uld[0] = ld(wstg.ap()[:, 0:1024], p0_src(0), "wst0")
    uld[1] = ld(wstg.ap()[:, 1024:2048], p0_src(1), "wst1")
    for u in range(NU):
        par = u % 2
        qv = q_views(wstg.ap()[:, 1024 * par:1024 * par + 1024], 32,
                     wqb.ap()[:, 1024 * par:1024 * par + 1024], 32 * par, 0)
        uq[u] = qv
        b.wait("vector", "wst%d" % par, uld[u])
        if u >= 2:
            b.wait("vector", "T", uq[u - 2]["tr"])
        q_amax(qv)
        q_scales(qv)
        if u >= 1:
            pq = uq[u - 1]
            q_apply(pq)
            b.wait("tensor", "V", pq["done"])
            if u >= 2:
                b.wait("tensor", "S", uq[u - 2]["te"])
            pq["tr"] = t_transpose8(wqb.ap()[:, 1024 * ((u - 1) % 2):
                                             1024 * ((u - 1) % 2) + 1024])
            pq["te"] = s_evac8(p0_dsts(u - 1), pq["tr"])
            if u + 1 < NU:
                b.wait("sync", "V", pq["done"])
                uld[u + 1] = ld(wstg.ap()[:, 1024 * ((u + 1) % 2):
                                          1024 * ((u + 1) % 2) + 1024],
                                p0_src(u + 1), "wst%d" % ((u + 1) % 2))
    pq = uq[NU - 1]
    q_apply(pq)
    b.wait("tensor", "V", pq["done"])
    b.wait("tensor", "S", uq[NU - 2]["te"])
    pq["tr"] = t_transpose8(wqb.ap()[:, 1024 * ((NU - 1) % 2):
                                     1024 * ((NU - 1) % 2) + 1024])
    pq["te"] = s_evac8(p0_dsts(NU - 1), pq["tr"])

    es_stg.close()
    if phases <= 0:
        b.full_barrier()
        return _finish()

    xt = sb("xt", (128, 2048), F32, es01)
    xn_t = sb("xn_t", (128, 2048), F32, es01)
    xnb_t = sb("xnb_t", (128, 1024), BF16, es01)
    xq_t = sb("xq_t", (128, 2048), BF16, es01)
    xqT_t = sb("xqT_t", (128, 2048), BF16, es01)
    hb_t = sb("hb_t", (128, 2048), F32, es01)
    hq_t = sb("hq_t", (128, 2048), BF16, es01)
    hqT_t = sb("hqT_t", (128, 1024), BF16, es01)
    g_t = sb("g_t", (128, 1024), F32, es01)
    lnst = sb("lnst", (128, 16), F32, es01)

    b.full_barrier()

    chk = {}

    def emit_prep_v1(j):
        sl = (j % 2) * 1024
        xap = xt.ap()[:, sl:sl + 1024]
        lo = 8 * (j % 2)
        b.wait("vector", "x%d" % (j % 2), chk[(j, "xl")])
        b.plain("vector", lambda e, x=xap: e.reduce_sum(
            lnst.ap()[:, lo:lo + 1], x, axis=AX.X))
        b.plain("vector", lambda e: e.drain())
        b.plain("vector", lambda e: e.tensor_scalar_mul(
            lnst.ap()[:, lo + 1:lo + 2], lnst.ap()[:, lo:lo + 1], -1.0 / 1024.0))
        chk[(j, "ln1")] = b.seal("vector", "V")

    def emit_prep_s(j):
        sl = (j % 2) * 1024
        xap = xt.ap()[:, sl:sl + 1024]
        lo = 8 * (j % 2)
        b.wait("scalar", "V", chk[(j, "ln1")])
        b.plain("scalar", lambda e, x=xap, s=sl: e.activation(
            xn_t.ap()[:, s:s + 1024], x, AF.Square,
            bias=lnst.ap()[:, lo + 1:lo + 2], accum_out=lnst.ap()[:, lo + 2:lo + 3]))
        b.seal("scalar", "S")
        b.wait("scalar", "S")
        b.plain("scalar", lambda e: e.activation(
            lnst.ap()[:, lo + 3:lo + 4], lnst.ap()[:, lo + 2:lo + 3], AF.Ln,
            scale=ccol("lnscale"), bias=ccol("eps")))
        b.seal("scalar", "S")
        b.wait("scalar", "S")
        b.plain("scalar", lambda e: e.activation(
            lnst.ap()[:, lo + 4:lo + 5], lnst.ap()[:, lo + 3:lo + 4], AF.Exp,
            scale=ccol("mhalf")))
        chk[(j, "rstd")] = b.seal("scalar", "S")

    def emit_prep_v2(j):
        sl = (j % 2) * 1024
        xap = xt.ap()[:, sl:sl + 1024]
        lo = 8 * (j % 2)
        b.wait("vector", "S", chk[(j, "rstd")])
        b.plain("vector", lambda e: e.tensor_tensor(
            lnst.ap()[:, lo + 5:lo + 6], lnst.ap()[:, lo + 1:lo + 2],
            lnst.ap()[:, lo + 4:lo + 5], op=AluOpType.mult))
        b.plain("vector", lambda e: e.drain())
        b.plain("vector", lambda e, x=xap, s=sl: e.tensor_scalar(
            xn_t.ap()[:, s:s + 1024], x, lnst.ap()[:, lo + 4:lo + 5],
            lnst.ap()[:, lo + 5:lo + 6], op0=AluOpType.mult, op1=AluOpType.add))
        b.plain("vector", lambda e: e.drain())
        if (j - 1, "xnbst") in chk:
            b.wait("vector", "xnbst", chk[(j - 1, "xnbst")])
        b.plain("vector", lambda e, s=sl: e.tensor_copy(
            xnb_t.ap(), xn_t.ap()[:, s:s + 1024]))
        chk[(j, "xn")] = b.seal("vector", "V")

    chk[(0, "xl")] = ld(xt.ap()[:, 0:1024], x_h[0:128, :], "x0")
    chk[(1, "xl")] = ld(xt.ap()[:, 1024:2048], x_h[128:256, :], "x1")
    emit_prep_v1(0)
    emit_prep_s(0)
    emit_prep_v2(0)
    b.wait("sync", "V", chk[(0, "xn")])
    chk[(0, "xnbst")] = b.inc("sync", lambda e: e.dma_start(
        out=xnb_s[0:128, :], in_=xnb_t.ap()), "xnbst", 16)
    qx0 = q_views(xn_t.ap()[:, 0:1024], 32, xq_t.ap()[:, 0:1024], 0, 1024)
    chk[(0, "qx")] = qx0
    q_amax(qx0)
    q_scales(qx0)
    q_apply(qx0)
    b.wait("tensor", "V", qx0["done"])
    b.wait("tensor", "S", uq[NU - 1]["te"])
    chk[(0, "xqT")] = t_transpose8(xq_t.ap()[:, 0:1024])
    chk[(0, "xqTe")] = s_evac8(
        [xqT_t.ap()[:, 128 * j:128 * j + 128] for j in range(8)], chk[(0, "xqT")])

    for i in range(NTILES):
        nx = i + 1

        def hmm(w):
            pb = 2 * (w % 2)
            if w >= 2:
                b.wait("tensor", "S", chk[(i, w - 2, "g")])
            elif (i - 1, w + 2, "g") in chk:
                b.wait("tensor", "S", chk[(i - 1, w + 2, "g")])
            if w == 0:
                b.wait("tensor", "S", chk[(i, "xqTe")])
            for K in range(8):
                lhs = xqT_t.ap()[:, ((i % 2) * 1024) + 128 * K:
                                 ((i % 2) * 1024) + 128 * K + 128]
                for c2 in range(2):
                    rhs = WfcqT.ap()[:, HID * K + 1024 * w + 512 * c2:
                                     HID * K + 1024 * w + 512 * c2 + 512]
                    b.plain("tensor", lambda e, o=bankA(pb + c2), l=lhs, r=rhs,
                            st=(K == 0): e.matmul(o, l, r, start=st, stop=False))
            for c2 in range(2):
                rhs = bspl_fc.ap()[:, 1024 * w + 512 * c2:1024 * w + 512 * c2 + 512]
                b.plain("tensor", lambda e, o=bankA(pb + c2), r=rhs:
                        e.matmul(o, ones2.ap(), r, start=False, stop=True))
            chk[(i, w, "hm")] = b.seal("tensor", "T")

        def gelu(w):
            pb = 2 * (w % 2)
            hs = (w % 2) * 1024
            b.wait("scalar", "T", chk[(i, w, "hm")])
            if (i, w - 2, "ap") in chk:
                b.wait("scalar", "V", chk[(i, w - 2, "ap")])
            elif (i - 1, w + 2, "ap") in chk:
                b.wait("scalar", "V", chk[(i - 1, w + 2, "ap")])
            for c2 in range(2):
                b.plain("scalar", lambda e,
                        o=hb_t.ap()[:, hs + 512 * c2:hs + 512 * c2 + 512],
                        s=bankA(pb + c2): e.activation(o, s, AF.Gelu))
            chk[(i, w, "g")] = b.seal("scalar", "S")

        def mk_qh(w):
            hs = (w % 2) * 1024
            return q_views(hb_t.ap()[:, hs:hs + 1024], 32,
                           hq_t.ap()[:, hs:hs + 1024], 32 + 32 * (w % 2), 0)

        def t_tr(w):
            b.wait("tensor", "V", chk[(i, w, "ap")])
            chk[(i, w, "tr")] = t_transpose8(hq_t.ap()[:, (w % 2) * 1024:
                                                       (w % 2) * 1024 + 1024])

        def s_te(w):
            chk[(i, w, "te")] = s_evac8(
                [hqT_t.ap()[:, 128 * j:128 * j + 128] for j in range(8)],
                chk[(i, w, "tr")])

        def t_f(w):
            b.wait("tensor", "S", chk[(i, w, "te")])
            if w == 0 and (i - 1, "sil") in chk:
                b.wait("tensor", "S", chk[(i - 1, "sil")])
            for K in range(8):
                lhs = hqT_t.ap()[:, 128 * K:128 * K + 128]
                for c2 in range(2):
                    rhs = WpjqT.ap()[:, D * (8 * w + K) + 512 * c2:
                                     D * (8 * w + K) + 512 * c2 + 512]
                    st = (w == 0 and K == 0)
                    b.plain("tensor", lambda e, o=bankB(2 + c2), l=lhs, r=rhs,
                            st=st: e.matmul(o, l, r, start=st, stop=False))
            if w == 3:
                for c2 in range(2):
                    rhs = bspl_pj.ap()[:, 512 * c2:512 * c2 + 512]
                    b.plain("tensor", lambda e, o=bankB(2 + c2), r=rhs:
                            e.matmul(o, ones2.ap(), r, start=False, stop=True))
            chk[(i, w, "f")] = b.seal("tensor", "T")

        def v_amax(w, qh):
            b.wait("vector", "S", chk[(i, w, "g")])
            q_amax(qh)
            chk[(i, w, "am")] = qh["vA"]

        def v_apply(w, qh):
            if (i, w - 2, "tr") in chk:
                b.wait("vector", "T", chk[(i, w - 2, "tr")])
            elif (i - 1, w + 2, "tr") in chk:
                b.wait("vector", "T", chk[(i - 1, w + 2, "tr")])
            q_apply(qh)
            chk[(i, w, "ap")] = qh["done"]

        qh = {}
        hmm(0)
        hmm(1)
        if nx < NTILES and nx + 1 < NTILES:
            b.wait("sync", "V", chk[(i, "xn")])
            chk[(nx + 1, "xl")] = ld(
                xt.ap()[:, ((nx + 1) % 2) * 1024:((nx + 1) % 2) * 1024 + 1024],
                x_h[128 * (nx + 1):128 * (nx + 1) + 128, :],
                "x%d" % ((nx + 1) % 2))
        gelu(0)
        if nx < NTILES:
            emit_prep_v1(nx)
        qh[0] = mk_qh(0)
        v_amax(0, qh[0])
        q_scales(qh[0])
        if nx < NTILES:
            emit_prep_s(nx)
        hmm(2)
        gelu(1)
        if nx < NTILES:
            emit_prep_v2(nx)
            b.wait("sync", "V", chk[(nx, "xn")])
            chk[(nx, "xnbst")] = b.inc("sync", lambda e, j=nx: e.dma_start(
                out=xnb_s[128 * j:128 * j + 128, :], in_=xnb_t.ap()), "xnbst", 16)
        qh[1] = mk_qh(1)
        v_amax(1, qh[1])
        q_scales(qh[1])
        v_apply(0, qh[0])
        t_tr(0)
        s_te(0)
        t_f(0)
        hmm(3)
        gelu(2)
        qh[2] = mk_qh(2)
        v_amax(2, qh[2])
        q_scales(qh[2])
        v_apply(1, qh[1])
        t_tr(1)
        s_te(1)
        t_f(1)
        gelu(3)
        qh[3] = mk_qh(3)
        v_amax(3, qh[3])
        q_scales(qh[3])
        v_apply(2, qh[2])
        t_tr(2)
        s_te(2)
        t_f(2)
        v_apply(3, qh[3])
        t_tr(3)
        s_te(3)
        t_f(3)
        if nx < NTILES:
            qx = q_views(xn_t.ap()[:, ((nx % 2) * 1024):((nx % 2) * 1024) + 1024],
                         32, xq_t.ap()[:, ((nx % 2) * 1024):((nx % 2) * 1024) + 1024],
                         0, 1024)
            q_amax(qx)
            q_scales(qx)
            if (i - 1, "xqT") in chk:
                b.wait("vector", "T", chk[(i - 1, "xqT")])
            q_apply(qx)
            b.wait("tensor", "V", qx["done"])
            b.wait("tensor", "S", chk[(i, 3, "te")])
            chk[(nx, "xqT")] = t_transpose8(
                xq_t.ap()[:, ((nx % 2) * 1024):((nx % 2) * 1024) + 1024])
            chk[(nx, "xqTe")] = s_evac8(
                [xqT_t.ap()[:, ((nx % 2) * 1024) + 128 * j:
                            ((nx % 2) * 1024) + 128 * j + 128] for j in range(8)],
                chk[(nx, "xqT")])
        b.wait("scalar", "T", chk[(i, 3, "f")])
        if (i - 1, "gst") in chk:
            b.wait("scalar", "gst", chk[(i - 1, "gst")])
        for c2 in range(2):
            b.plain("scalar", lambda e, o=g_t.ap()[:, 512 * c2:512 * c2 + 512],
                    s=bankB(2 + c2): e.activation(o, s, AF.Silu))
        chk[(i, "sil")] = b.seal("scalar", "S")
        b.wait("sync", "S", chk[(i, "sil")])
        chk[(i, "gst")] = b.inc("sync", lambda e, j=i: e.dma_start(
            out=g_s[128 * j:128 * j + 128, :], in_=g_t.ap()), "gst", 16)

    b.full_barrier()
    es01.close()
    if phases <= 1:
        return _finish()

    esM = ExitStack()
    SpecTr = sb("SpecTr", (128, 8 * NCHUNK * M), BF16, esM)
    SpecTi = sb("SpecTi", (128, 8 * NCHUNK * M), BF16, esM)

    es23 = ExitStack()
    XlTr = sb("XlTr", (128, 8 * 256), F32, es23)
    XlTi = sb("XlTi", (128, 8 * 256), F32, es23)

    es2 = ExitStack()
    GmT = sb("GmT", (128, 8 * 64), BF16, es2)
    G_T = sb("G_T", (128, 8 * 64), F32, es2)
    cpr = sb("cpr", (128, 8 * G_CH * 35), BF16, es2)
    cpi = sb("cpi", (128, 8 * G_CH * 35), BF16, es2)
    Hbt = sb("Hbt", (128, 8 * GW), BF16, es2)
    dwdg = sb("dwdg", (128, 2 * 7 * 8 * 128), BF16, es2)
    pwrT = sb("pwrT_sb", (128, 8 * D), BF16, es2)
    pwiT = sb("pwiT_sb", (128, 8 * D), BF16, es2)
    xnbp = sb("xnbp", (128, 2048), BF16, es2)

    def dwdg_ap(comp, j, dblk):
        off = 128 * (dblk + 8 * (j + 7 * comp))
        return dwdg.ap()[:, off:off + 128]

    for comp, dwt in ((0, dwr_t), (1, dwi_t)):
        for j in range(KS):
            for dblk in range(8):
                col = dwt.ap()[:, KS * dblk + j:KS * dblk + j + 1]
                b.plain("vector", lambda e, o=dwdg_ap(comp, j, dblk), c=col:
                        e.tensor_scalar(o, I128b.ap(), c, None,
                                        op0=AluOpType.mult))
    b.seal("vector", "V")

    gw_ld = ld8(pwrT, gwT_h)
    if p2sub <= 0:
        b.full_barrier()
        return _finish()

    pa_done = {}
    for p in range(NPAIR):
        xslot = (p % 2) * 1024
        if p >= 2:
            b.wait("sync", "V", pa_done[p - 2])
        lxp = ld(xnbp.ap()[:, xslot:xslot + 1024],
                 xnb_s[128 * p:128 * p + 128, :], "p2x%d" % (p % 2))
        b.wait("tensor", "p2x%d" % (p % 2), lxp)
        if p >= 1:
            b.wait("tensor", "V", pa_done[p - 1])
        for half in range(2):
            ch = 2 * p + half
            base = 64 * half
            pb = (ch % 2) * 2
            for dblk in range(8):
                lhs = xnbp.ap()[base:base + 64,
                                xslot + 128 * dblk:xslot + 128 * dblk + 128]
                rhs = F64b.ap()[base:base + 64, :]
                bk = pb + dblk // 4
                o = psA.ap()[:, 512 * bk + 128 * (dblk % 4):
                             512 * bk + 128 * (dblk % 4) + 66]
                b.plain("tensor", lambda e, o=o, l=lhs, r=rhs:
                        e.matmul(o, l, r, start=True, stop=True))
        tpa = b.seal("tensor", "T")
        b.wait("vector", "T", tpa)
        for half in range(2):
            ch = 2 * p + half
            pb = (ch % 2) * 2
            dr = XlTr.ap().rearrange("p (db t) -> p db t", t=256)
            di = XlTi.ap().rearrange("p (db t) -> p db t", t=256)
            dg = GmT.ap().rearrange("p (db c) -> p db c", c=64)
            for hb2 in range(2):
                src = psA.ap()[:, 512 * (pb + hb2):512 * (pb + hb2) + 512]
                src3 = src.rearrange("p (s w) -> p s w", w=128)
                b.plain("vector", lambda e,
                        o=dr[:, 4 * hb2:4 * hb2 + 4, 4 * ch:4 * ch + 4],
                        s=src3[:, :, 0:4]: e.tensor_copy(o, s))
                b.plain("vector", lambda e,
                        o=di[:, 4 * hb2:4 * hb2 + 4, 4 * ch:4 * ch + 4],
                        s=src3[:, :, 33:37]: e.tensor_copy(o, s))
                b.plain("vector", lambda e,
                        o=dg[:, 4 * hb2:4 * hb2 + 4, ch:ch + 1],
                        s=src3[:, :, 0:1]: e.tensor_scalar_mul(o, s, 1.0 / 64.0))
        pa_done[p] = b.seal("vector", "V")

    if p2sub <= 1:
        b.full_barrier()
        return _finish()

    b.wait("tensor", "V")
    b.wait("tensor", "gq", gw_ld)
    for mtl in range(8):
        for K in range(8):
            lhs = pwrT.ap()[:, 1024 * K + 128 * mtl:1024 * K + 128 * mtl + 128]
            rhs = GmT.ap()[:, 64 * K:64 * K + 64]
            o = psB.ap()[:, 64 * mtl:64 * mtl + 64]
            b.plain("tensor", lambda e, o=o, l=lhs, r=rhs, st=(K == 0),
                    sp=(K == 7): e.matmul(o, l, r, start=st, stop=sp))
    tg = b.seal("tensor", "T")
    b.wait("scalar", "T", tg)
    sg2 = b.inc("scalar", lambda e: e.activation(G_T.ap(), psB.ap()[:, 0:512],
                                                 AF.Sigmoid), "S")
    pw_ld = 0
    if p2sub >= 3:
        b.wait("gpsimd", "S", sg2)
        ld8(pwrT, pwrT_h)
        pw_ld = ld8(pwiT, pwiT_h)

    if p2sub <= 2:
        b.full_barrier()
        return _finish()

    for buf in (cpr, cpi):
        pad = buf.ap().rearrange("p (q w) -> p q w", w=35)[:, :, 0:6]
        b.plain("vector", lambda e, o=pad: e.memset(o, 0.0))
    vz = b.seal("vector", "V")

    b.wait("tensor", "V", vz)
    b.wait("tensor", "gq", pw_ld)
    prev_evac = pa_done[NPAIR - 1]
    grp_last = {}
    for g in range(NGRP):
        pair_evac = {}
        for pp in range(4):
            p = 4 * g + pp
            xslot = (p % 2) * 1024
            free_v = pair_evac.get(pp - 2, prev_evac)
            b.wait("sync", "V", free_v)
            lxp = ld(xnbp.ap()[:, xslot:xslot + 1024],
                     xnb_s[128 * p:128 * p + 128, :], "p2x%d" % (p % 2))
            b.wait("tensor", "p2x%d" % (p % 2), lxp)
            b.wait("tensor", "V", pair_evac.get(pp - 1, prev_evac))
            for half in range(2):
                ch = 2 * p + half
                gc = ch - G_CH * g
                base = 64 * half
                pb = (ch % 2) * 2
                for dblk in range(8):
                    lhs = xnbp.ap()[base:base + 64,
                                    xslot + 128 * dblk:xslot + 128 * dblk + 128]
                    rhs = F64b.ap()[base:base + 64, :]
                    bk = pb + dblk // 4
                    o = psA.ap()[:, 512 * bk + 128 * (dblk % 4):
                                 512 * bk + 128 * (dblk % 4) + 66]
                    b.plain("tensor", lambda e, o=o, l=lhs, r=rhs:
                            e.matmul(o, l, r, start=True, stop=True))
                trf = b.seal("tensor", "T")
                b.wait("vector", "T", trf)
                for hb2 in range(2):
                    src = psA.ap()[:, 512 * (pb + hb2):512 * (pb + hb2) + 512]
                    src3 = src.rearrange("p (s w) -> p s w", w=128)
                    for comp, buf in ((0, cpr), (1, cpi)):
                        co = 4 if comp == 0 else 37
                        dst = buf.ap().rearrange(
                            "p (db g2 w) -> p db g2 w", w=35, g2=G_CH)[
                            :, 4 * hb2:4 * hb2 + 4, gc:gc + 1, 6:35]
                        s4 = src3[:, :, co:co + LHI].unsqueeze(2)
                        b.plain("vector", lambda e, o=dst, s=s4:
                                e.tensor_copy(o, s))
            pair_evac[pp] = b.seal("vector", "V")
        prev_evac = pair_evac[3]

        for comp in (range(2) if p2sub >= 4 else range(0)):
            cpad = cpr if comp == 0 else cpi
            pwT = pwrT if comp == 0 else pwiT
            spec = SpecTr if comp == 0 else SpecTi
            b.wait("tensor", "V", prev_evac)
            for dblk in range(8):
                pb2 = dblk % 2
                for j in range(KS):
                    lhs = dwdg_ap(comp, j, dblk)
                    base_col = G_CH * 35 * dblk
                    rhs = cpad.ap()[:, base_col:base_col + G_CH * 35].rearrange(
                        "p (g2 w) -> p g2 w", w=35)[:, :, j:j + LHI]
                    o = bankB(pb2)[:, 0:GW]
                    b.plain("tensor", lambda e, o=o, l=lhs, r=rhs, st=(j == 0),
                            sp=(j == KS - 1): e.matmul(o, l, r, start=st, stop=sp))
                tcv = b.seal("tensor", "T")
                b.wait("scalar", "T", tcv)
                b.plain("scalar", lambda e, o=Hbt.ap()[:, GW * dblk:GW * (dblk + 1)],
                        s=bankB(pb2)[:, 0:GW]: e.activation(o, s, AF.Gelu))
                sck = b.seal("scalar", "S")
                b.wait("tensor", "S", sck)
            for od in (range(8) if p2sub >= 5 else range(0)):
                pb2 = 2 + od % 2
                for K in range(8):
                    lhs = pwT.ap()[:, 1024 * K + 128 * od:1024 * K + 128 * od + 128]
                    rhs = Hbt.ap()[:, GW * K:GW * (K + 1)]
                    o = bankB(pb2)[:, 0:GW]
                    b.plain("tensor", lambda e, o=o, l=lhs, r=rhs, st=(K == 0),
                            sp=(K == 7): e.matmul(o, l, r, start=st, stop=sp))
                tpw = b.seal("tensor", "T")
                b.wait("vector", "T", tpw)
                gsl = G_T.ap()[:, 64 * od + G_CH * g:64 * od + G_CH * (g + 1)] \
                    .unsqueeze(2).broadcast_to([128, G_CH, LHI])
                doff = NCHUNK * M * od + M * G_CH * g
                dst = spec.ap()[:, doff:doff + M * G_CH].rearrange(
                    "p (ch m) -> p ch m", m=M)[:, :, 4:33]
                src = bankB(pb2)[:, 0:GW].rearrange("p (ch m) -> p ch m", m=LHI)
                b.plain("vector", lambda e, o=dst, s=src, gg=gsl:
                        e.tensor_tensor(o, s, gg, op=AluOpType.mult))
                vpw = b.seal("vector", "V")
                b.wait("tensor", "V", vpw)

    b.full_barrier()
    es2.close()
    if phases <= 2:
        return _finish()

    es3 = ExitStack()
    qwT = sb("qwT_sb", (128, 8 * D), BF16, es3)
    kwT = sb("kwT_sb", (128, 8 * D), BF16, es3)
    vwT = sb("vwT_sb", (128, 8 * D), BF16, es3)
    owT = sb("owT_sb", (128, 8 * D), BF16, es3)
    XlTbr = sb("XlTbr", (128, 2048), BF16, es3)
    XlTbi = sb("XlTbi", (128, 2048), BF16, es3)
    QTb = sb("QTb", (128, 2048), BF16, es3)
    KTb = sb("KTb", (128, 2048), BF16, es3)
    Vtok = sb("Vtok", (128, 2048), BF16, es3)
    OTb = sb("OTb", (128, 2048), BF16, es3)
    Ssb = sb("Ssb", (128, 512), F32, es3)
    Pf = sb("Pf", (128, 512), F32, es3)
    Pbf = sb("Pbf", (128, 512), BF16, es3)
    PbT = sb("PbT", (128, 512), BF16, es3)
    smst = sb("smst", (128, 8), F32, es3)
    d1b = sb("d1b", (128, 256), F32, es3)

    ld8(qwT, qwT_h)
    half_ld = ld8(kwT, kwT_h)
    b.wait("gpsimd", "gq", half_ld)
    ld8(vwT, vwT_h)
    aw_ld = ld8(owT, owT_h)
    b.wait("vector", "gq", aw_ld)
    b.plain("vector", lambda e: e.tensor_copy(XlTbr.ap(), XlTr.ap()))
    b.inc("vector", lambda e: e.tensor_copy(XlTbi.ap(), XlTi.ap()), "V")
    b.full_barrier()

    for comp in range(2):
        XlTb = XlTbr if comp == 0 else XlTbi
        XlTf = XlTr if comp == 0 else XlTi
        spec = SpecTr if comp == 0 else SpecTi

        for wT, dstb in ((qwT, QTb), (kwT, KTb)):
            for mtl in range(8):
                pbk = mtl % 4
                slot = 256 * (mtl // 4)
                for K in range(8):
                    lhs = wT.ap()[:, 1024 * K + 128 * mtl:1024 * K + 128 * mtl + 128]
                    rhs = XlTb.ap()[:, 256 * K:256 * K + 256]
                    o = bankA(pbk)[:, slot:slot + 256]
                    b.plain("tensor", lambda e, o=o, l=lhs, r=rhs, st=(K == 0),
                            sp=(K == 7): e.matmul(o, l, r, start=st, stop=sp))
            tq = b.seal("tensor", "T")
            b.wait("scalar", "T", tq)
            for mtl in range(8):
                pbk = mtl % 4
                slot = 256 * (mtl // 4)
                b.plain("scalar", lambda e,
                        o=dstb.ap()[:, 256 * mtl:256 * mtl + 256],
                        s=bankA(pbk)[:, slot:slot + 256]: e.copy(o, s))
            sq = b.seal("scalar", "S")
            b.wait("tensor", "S", sq)
        for mt in range(2):
            for K in range(8):
                lhs = XlTb.ap()[:, 256 * K + 128 * mt:256 * K + 128 * mt + 128]
                for c2 in range(2):
                    rhs = vwT.ap()[:, 1024 * K + 512 * c2:1024 * K + 512 * c2 + 512]
                    o = bankA(2 * mt + c2)
                    b.plain("tensor", lambda e, o=o, l=lhs, r=rhs, st=(K == 0),
                            sp=(K == 7): e.matmul(o, l, r, start=st, stop=sp))
        tv = b.seal("tensor", "T")
        b.wait("scalar", "T", tv)
        for k4 in range(4):
            b.plain("scalar", lambda e, o=Vtok.ap()[:, 512 * k4:512 * k4 + 512],
                    s=bankA(k4): e.copy(o, s))
        sv2 = b.seal("scalar", "S")
        b.wait("tensor", "S", sv2)

        prev_v = 0
        for h in range(16):
            base = 64 * (h % 2)
            tl = h // 2
            if prev_v:
                b.wait("tensor", "V", prev_v)
            for qm in range(2):
                lhs = QTb.ap()[base:base + 64,
                               256 * tl + 128 * qm:256 * tl + 128 * qm + 128]
                rhs = KTb.ap()[base:base + 64, 256 * tl:256 * tl + 256]
                o = bankB(qm)[:, 0:256]
                b.plain("tensor", lambda e, o=o, l=lhs, r=rhs:
                        e.matmul(o, l, r, start=True, stop=True))
            ts2 = b.seal("tensor", "T")
            b.wait("vector", "T", ts2)
            for qm in range(2):
                b.plain("vector", lambda e, o=Ssb.ap()[:, 256 * qm:256 * qm + 256],
                        s=bankB(qm)[:, 0:256],
                        m=maskt.ap()[:, 256 * qm:256 * qm + 256]:
                        e.tensor_tensor(o, s, m, op=AluOpType.add))
            b.plain("vector", lambda e: e.drain())
            for qm in range(2):
                b.plain("vector", lambda e, o=smst.ap()[:, qm:qm + 1],
                        s=Ssb.ap()[:, 256 * qm:256 * qm + 256]:
                        e.reduce_max(o, s, axis=AX.X))
            b.plain("vector", lambda e: e.drain())
            for qm in range(2):
                b.plain("vector", lambda e, o=smst.ap()[:, 2 + qm:3 + qm],
                        s=smst.ap()[:, qm:qm + 1]: e.tensor_scalar_mul(o, s, -0.125))
            vs1 = b.seal("vector", "V")
            prev_v = vs1
            b.wait("scalar", "V", vs1)
            for qm in range(2):
                b.plain("scalar", lambda e, o=Pf.ap()[:, 256 * qm:256 * qm + 256],
                        s=Ssb.ap()[:, 256 * qm:256 * qm + 256],
                        bi=smst.ap()[:, 2 + qm:3 + qm],
                        ac=smst.ap()[:, 4 + qm:5 + qm]:
                        e.activation(o, s, AF.Exp, scale=ccol("smx"), bias=bi,
                                     accum_out=ac))
            se1 = b.seal("scalar", "S")
            b.wait("vector", "S", se1)
            b.plain("vector", lambda e: e.reciprocal(smst.ap()[:, 6:8],
                                                     smst.ap()[:, 4:6]))
            vs2 = b.seal("vector", "V")
            b.wait("scalar", "V", vs2)
            for qm in range(2):
                b.plain("scalar", lambda e, o=Pbf.ap()[:, 256 * qm:256 * qm + 256],
                        s=Pf.ap()[:, 256 * qm:256 * qm + 256],
                        sc2=smst.ap()[:, 6 + qm:7 + qm]:
                        e.activation(o, s, AF.Copy, scale=sc2))
            se2 = b.seal("scalar", "S")
            b.wait("tensor", "S", se2)
            for blk in range(4):
                qm, kb = blk % 2, blk // 2
                lhs = Pbf.ap()[:, 256 * qm + 128 * kb:256 * qm + 128 * kb + 128]
                o = psB.ap()[:, 1024 + 128 * (2 * kb + qm):
                             1024 + 128 * (2 * kb + qm) + 128]
                b.plain("tensor", lambda e, o=o, l=lhs:
                        e.matmul(o, l, I128b.ap(), start=True, stop=True))
            tpt = b.seal("tensor", "T")
            b.wait("scalar", "T", tpt)
            b.plain("scalar", lambda e: e.copy(PbT.ap(), psB.ap()[:, 1024:1536]))
            spt = b.seal("scalar", "S")
            b.wait("tensor", "S", spt)
            for kt in range(2):
                lhs = Vtok.ap()[:, 1024 * kt + 64 * h:1024 * kt + 64 * h + 64]
                rhs = PbT.ap()[:, 256 * kt:256 * kt + 256]
                o = psB.ap()[base:base + 64, 1536:1536 + 256]
                b.plain("tensor", lambda e, o=o, l=lhs, r=rhs, st=(kt == 0),
                        sp=(kt == 1), tp=(0, base):
                        e.matmul(o, l, r, start=st, stop=sp, tile_position=tp))
            to = b.seal("tensor", "T")
            b.wait("scalar", "T", to)
            b.plain("scalar", lambda e, o=OTb.ap()[base:base + 64,
                                                   256 * tl:256 * tl + 256],
                    s=psB.ap()[base:base + 64, 1536:1536 + 256]: e.copy(o, s))
            so = b.seal("scalar", "S")
            b.wait("tensor", "S", so)

        for mtl in range(8):
            pbk = mtl % 2
            for K in range(8):
                lhs = owT.ap()[:, 1024 * K + 128 * mtl:1024 * K + 128 * mtl + 128]
                rhs = OTb.ap()[:, 256 * K:256 * K + 256]
                o = bankA(pbk)[:, 0:256]
                b.plain("tensor", lambda e, o=o, l=lhs, r=rhs, st=(K == 0),
                        sp=(K == 7): e.matmul(o, l, r, start=st, stop=sp))
            tso = b.seal("tensor", "T")
            b.wait("vector", "T", tso)
            b.plain("vector", lambda e, s=bankA(pbk)[:, 0:256],
                    xl=XlTf.ap()[:, 256 * mtl:256 * mtl + 256]:
                    e.tensor_tensor(d1b.ap(), s, xl, op=AluOpType.subtract))
            b.plain("vector", lambda e: e.drain())
            doff = NCHUNK * M * mtl
            dst = spec.ap()[:, doff:doff + NCHUNK * M].rearrange(
                "p (ch m) -> p ch m", m=M)[:, :, 0:4]
            b.plain("vector", lambda e, o=dst,
                    s=d1b.ap().rearrange("p (ch m) -> p ch m", m=4):
                    e.tensor_scalar(o, s, alpha_sg.ap()[:, 0:1], None,
                                    op0=AluOpType.mult))
            vso = b.seal("vector", "V")
            b.wait("tensor", "V", vso)

    b.full_barrier()
    es3.close()
    es23.close()
    if phases <= 3:
        return _finish()

    es4 = ExitStack()
    slotR = sb("slotR", (66, 1024), BF16, es4)
    slotI = sb("slotI", (66, 1024), BF16, es4)
    xnb4 = sb("xnb4", (128, 2048), BF16, es4)
    xp4 = sb("xp4", (128, 2048), F32, es4)
    gp4 = sb("gp4", (128, 2048), F32, es4)
    ub4 = sb("ub4", (128, 1024), F32, es4)
    ob4 = sb("ob4", (128, 2048), F32, es4)

    p4 = {}
    for p in range(NPAIR):
        sl = (p % 2) * 1024
        psem = "p4l%d" % (p % 2)
        if p >= 2:
            b.wait("sync", "V", p4[(p - 2, "u")])
        l1 = ld(xnb4.ap()[:, sl:sl + 1024], xnb_s[128 * p:128 * p + 128, :], psem)
        ld(xp4.ap()[:, sl:sl + 1024], x_h[128 * p:128 * p + 128, :], psem)
        l3 = ld(gp4.ap()[:, sl:sl + 1024], g_s[128 * p:128 * p + 128, :], psem)
        if p >= 1:
            b.wait("tensor", "S", p4[(p - 1, "slots")])
        for comp in range(2):
            spec = SpecTr if comp == 0 else SpecTi
            for dblk in range(8):
                lhs = spec.ap()[:, NCHUNK * M * dblk + 66 * p:
                                NCHUNK * M * dblk + 66 * p + 66]
                bank = 2 * comp + dblk // 4
                o = psA.ap()[0:66, 512 * bank + 128 * (dblk % 4):
                             512 * bank + 128 * (dblk % 4) + 128]
                b.plain("tensor", lambda e, o=o, l=lhs:
                        e.matmul(o, l, I128b.ap(), start=True, stop=True))
        ttr = b.seal("tensor", "T")
        b.wait("scalar", "T", ttr)
        for comp in range(2):
            dstb = slotR if comp == 0 else slotI
            for hb2 in range(2):
                bank = 2 * comp + hb2
                b.plain("scalar", lambda e,
                        o=dstb.ap()[0:66, 512 * hb2:512 * hb2 + 512],
                        s=psA.ap()[0:66, 512 * bank:512 * bank + 512]:
                        e.copy(o, s))
        p4[(p, "slots")] = b.seal("scalar", "S")
        b.wait("tensor", "S", p4[(p, "slots")])
        b.wait("tensor", psem, l3)
        if p >= 1:
            b.wait("tensor", "V", p4[(p - 1, "u")])
        for c2 in range(2):
            rhs = xnb4.ap()[:, sl + 512 * c2:sl + 512 * c2 + 512]
            b.plain("tensor", lambda e, o=bankB(c2), r=rhs:
                    e.matmul(o, I128b.ap(), r, start=True, stop=False))
        for c2 in range(2):
            rhs = slotR.ap()[0:66, 512 * c2:512 * c2 + 512]
            b.plain("tensor", lambda e, o=bankB(c2), r=rhs:
                    e.matmul(o, Wr2.ap(), r, start=False, stop=False))
        for c2 in range(2):
            rhs = slotI.ap()[0:66, 512 * c2:512 * c2 + 512]
            b.plain("tensor", lambda e, o=bankB(c2), r=rhs:
                    e.matmul(o, Wi2.ap(), r, start=False, stop=True))
        ty = b.seal("tensor", "T")
        b.wait("vector", "T", ty)
        b.wait("vector", psem, l3)
        if p >= 2:
            b.wait("vector", "p4s%d" % (p % 2), p4[(p - 2, "done")])
        b.plain("vector", lambda e, s=sl: e.tensor_tensor(
            ub4.ap(), psB.ap()[:, 0:1024], gp4.ap()[:, s:s + 1024],
            op=AluOpType.mult))
        b.plain("vector", lambda e: e.drain())
        b.plain("vector", lambda e, s=sl: e.tensor_tensor(
            ob4.ap()[:, s:s + 1024], ub4.ap(), xp4.ap()[:, s:s + 1024],
            op=AluOpType.add))
        p4[(p, "u")] = b.seal("vector", "V")
        b.wait("sync", "V", p4[(p, "u")])
        p4[(p, "done")] = b.inc("sync", lambda e, s=sl, p=p: e.dma_start(
            out=out_h[128 * p:128 * p + 128, :], in_=ob4.ap()[:, s:s + 1024]),
            "p4s%d" % (p % 2), 16)

    es4.close()
    esM.close()
    return _finish()


def kernel(L0_attn_hidden_states, L0_attn_q_w, L0_attn_k_w, L0_attn_v_w,
           L0_attn_o_w, L0_attn_dw_r, L0_attn_pw_r, L0_attn_dw_i, L0_attn_pw_i,
           L0_attn_gate_w, L0_attn_alpha_raw, L0_ffn_w_fc, L0_ffn_b_fc,
           L0_ffn_w_proj, L0_ffn_b_proj):
    x = np.ascontiguousarray(np.asarray(L0_attn_hidden_states, dtype=np.float32))
    qw = np.asarray(L0_attn_q_w, np.float32)
    kw = np.asarray(L0_attn_k_w, np.float32)
    vw = np.asarray(L0_attn_v_w, np.float32)
    ow = np.asarray(L0_attn_o_w, np.float32)
    gw = np.asarray(L0_attn_gate_w, np.float32)
    pwr = np.asarray(L0_attn_pw_r, np.float32)
    pwi = np.asarray(L0_attn_pw_i, np.float32)
    dwr = np.ascontiguousarray(np.asarray(L0_attn_dw_r, np.float32)[:, 0, :])
    dwi = np.ascontiguousarray(np.asarray(L0_attn_dw_i, np.float32)[:, 0, :])
    alpha = np.asarray(L0_attn_alpha_raw, np.float32)
    wfc = np.ascontiguousarray(np.asarray(L0_ffn_w_fc, np.float32))
    bfc = np.asarray(L0_ffn_b_fc, np.float32)
    wpj = np.ascontiguousarray(np.asarray(L0_ffn_w_proj, np.float32))
    bpj = np.asarray(L0_ffn_b_proj, np.float32)

    kw_rep = np.repeat(kw.reshape(4, 64, D), 4, axis=0).reshape(D, D)
    vw_rep = np.repeat(vw.reshape(4, 64, D), 4, axis=0).reshape(D, D)
    consts = make_consts()
    shared = {
        "qwT": np.ascontiguousarray(qw.T),
        "kwT": np.ascontiguousarray(kw_rep.T),
        "vwT": np.ascontiguousarray(vw_rep.T),
        "owT": np.ascontiguousarray(ow.T),
        "gwT": np.ascontiguousarray(gw.T),
        "pwrT": np.ascontiguousarray(pwr.T),
        "pwiT": np.ascontiguousarray(pwi.T),
        "dwr": dwr, "dwi": dwi,
        "alpha128": np.ascontiguousarray(np.repeat(alpha.reshape(1, 1), 128, 0)),
        "bfc": np.ascontiguousarray(bfc.reshape(1, HID)),
        "bproj": np.ascontiguousarray(bpj.reshape(1, D)),
        "wfc": wfc, "wproj": wpj,
        **consts,
    }

    nc = build_program()
    in_maps = [dict(shared, x=x[bb]) for bb in range(8)]
    res = run_bass_kernel_spmd(nc, in_maps, core_ids=list(range(8)))
    return np.stack([res.results[i]["out"] for i in range(8)], axis=0)


# version 08
# speedup vs baseline: 1.5973x; 1.0410x over previous
import numpy as np
import ml_dtypes
import concourse.bass as bass
import concourse.mybir as mybir
from concourse.bass_utils import run_bass_kernel_spmd
from concourse.alu_op_type import AluOpType
from contextlib import ExitStack

F32 = mybir.dt.float32
BF16 = mybir.dt.bfloat16
I32 = mybir.dt.int32
AF = mybir.ActivationFunctionType
AX = mybir.AxisListType

LN2 = float(np.log(2.0))
N_TOK = 4096
D = 1024
NTILES = 32
C = 64
M = 33
KLOW = 4
LHI = 29
KS = 7
NCHUNK = 64
NPAIR = 32
HID = 4096
G_CH = 8
NGRP = NCHUNK // G_CH
GW = G_CH * LHI

ENGS = ("sync", "gpsimd", "tensor", "vector", "scalar")


def make_consts():
    c = np.arange(C)
    m = np.arange(M)
    ang = 2 * np.pi * np.outer(c, m) / C
    Fc = np.cos(ang).astype(np.float32)
    Fs = (-np.sin(ang)).astype(np.float32)
    F64 = np.zeros((128, 66), np.float32)
    F64[0:64, 0:33] = Fc
    F64[0:64, 33:66] = Fs
    F64[64:128] = F64[0:64]
    Flow = np.zeros((128, 8), np.float32)
    Flow[0:64, 0:4] = Fc[:, 0:4]
    Flow[0:64, 4:8] = Fs[:, 0:4]
    Flow[64:128] = Flow[0:64]

    mm_ = np.arange(M)[:, None]
    cc_ = np.arange(C)[None, :]
    s2 = np.where((mm_ >= 1) & (mm_ <= 31), 2.0, 1.0)
    wr = (np.cos(2 * np.pi * mm_ * cc_ / C) / C * s2).astype(np.float32)
    wi = (-np.sin(2 * np.pi * mm_ * cc_ / C) / C * s2).astype(np.float32)
    wi[0] = 0.0
    wi[32] = 0.0
    Wr2 = np.zeros((66, 128), np.float32)
    Wi2 = np.zeros((66, 128), np.float32)
    Wr2[0:33, 0:64] = wr
    Wr2[33:66, 64:128] = wr
    Wi2[0:33, 0:64] = wi
    Wi2[33:66, 64:128] = wi

    cid = np.arange(256) // 4
    mask8 = np.where(cid[:, None] >= cid[None, :], 0.0, -8.0e9).astype(np.float32)

    bf = ml_dtypes.bfloat16
    return {
        "F64b": F64.astype(bf),
        "FlowB": Flow.astype(bf),
        "Wr2": Wr2.astype(bf),
        "Wi2": Wi2.astype(bf),
        "I128b": np.eye(128, dtype=bf),
        "ones2": np.ones((2, 128), dtype=bf),
        "maskc": mask8,
    }


class B:
    def __init__(self):
        self.ops = {e: [] for e in ENGS}
        self.c = {}
        self.sem = {}

    def add_sem(self, name, handle):
        self.sem[name] = handle
        self.c[name] = 0

    def inc(self, eng, make, sem, n=1):
        h = self.sem[sem]
        self.ops[eng].append(lambda e, make=make, h=h, n=n: make(e).then_inc(h, n))
        self.c[sem] += n
        return self.c[sem]

    def plain(self, eng, make):
        self.ops[eng].append(make)

    def seal(self, eng, sem, n=1):
        h = self.sem[sem]
        last = self.ops[eng].pop()
        self.ops[eng].append(lambda e, last=last, h=h, n=n:
                             last(e).then_inc(h, n))
        self.c[sem] += n
        return self.c[sem]

    def wait(self, eng, sem, val=None):
        h = self.sem[sem]
        v = self.c[sem] if val is None else val
        if v > 0:
            self.ops[eng].append(lambda e, h=h, v=v: e.wait_ge(h, v))
        return v

    def full_barrier(self):
        vals = dict(self.c)
        for e in ENGS:
            for s, v in vals.items():
                self.wait(e, s, v)


def build_program(phases=4, p2sub=9):
    nc = bass.Bass("TRN2", target_bir_lowering=False,
                   dynamic_dma_scratch_size=16384)

    def din(name, shape, dtype):
        return nc.dram_tensor(name, list(shape), dtype, kind="ExternalInput").ap()

    x_h = din("x", (N_TOK, D), F32)
    qwT_h = din("qwT", (D, D), F32)
    kwT_h = din("kwT", (D, D), F32)
    vwT_h = din("vwT", (D, D), F32)
    owT_h = din("owT", (D, D), F32)
    gwT_h = din("gwT", (D, D), F32)
    pwrT_h = din("pwrT", (D, D), F32)
    pwiT_h = din("pwiT", (D, D), F32)
    dwr_h = din("dwr", (D, KS), F32)
    dwi_h = din("dwi", (D, KS), F32)
    alpha_h = din("alpha128", (128, 1), F32)
    bfc_h = din("bfc", (1, HID), F32)
    bpj_h = din("bproj", (1, D), F32)
    wfc_h = din("wfc", (HID, D), F32)
    wpj_h = din("wproj", (D, HID), F32)
    F64b_h = din("F64b", (128, 66), BF16)
    FlowB_h = din("FlowB", (128, 8), BF16)
    Wr2_h = din("Wr2", (66, 128), BF16)
    Wi2_h = din("Wi2", (66, 128), BF16)
    I128b_h = din("I128b", (128, 128), BF16)
    ones2_h = din("ones2", (2, 128), BF16)
    maskc_h = din("maskc", (256, 256), F32)

    out_h = nc.dram_tensor("out", [N_TOK, D], F32, kind="ExternalOutput").ap()
    xnb_s = nc.dram_tensor("xnb_s", [N_TOK, D], BF16, kind="Internal").ap()
    g_s = nc.dram_tensor("g_s", [N_TOK, D], F32, kind="Internal").ap()

    esg = ExitStack()

    def sb(name, shape, dtype, stack):
        return stack.enter_context(nc.sbuf_tensor(name, list(shape), dtype))

    cst = sb("cst", (128, 16), F32, esg)
    I128b = sb("I128b_sb", (128, 128), BF16, esg)
    F64b = sb("F64b_sb", (128, 66), BF16, esg)
    FlowB = sb("FlowB_sb", (128, 8), BF16, esg)
    Wr2 = sb("Wr2_sb", (66, 128), BF16, esg)
    Wi2 = sb("Wi2_sb", (66, 128), BF16, esg)
    ones2 = sb("ones2_sb", (2, 128), BF16, esg)
    maskt = sb("maskt", (128, 512), F32, esg)
    alpha_t = sb("alpha_t", (128, 1), F32, esg)
    alpha_sg = sb("alpha_sg", (128, 1), F32, esg)
    bspl_fc = sb("bspl_fc", (2, HID), BF16, esg)
    bspl_pj = sb("bspl_pj", (2, D), BF16, esg)
    dwr_t = sb("dwr_t", (128, 8 * KS), F32, esg)
    dwi_t = sb("dwi_t", (128, 8 * KS), F32, esg)

    psA = esg.enter_context(nc.psum_tensor("psA", [128, 2048], F32))
    psB = esg.enter_context(nc.psum_tensor("psB", [128, 2048], F32))

    b = B()
    for s in ("cst", "gq", "T", "V", "S", "wst0", "wst1", "x0", "x1",
              "xnbst", "gst", "p2x0", "p2x1", "p4l0", "p4l1", "p4s0", "p4s1", "bld", "bspl"):
        b.add_sem(s, esg.enter_context(nc.semaphore("sem_" + s)))

    def bankA(k, w=512):
        return psA.ap()[:, 512 * k:512 * k + w]

    def bankB(k, w=512):
        return psB.ap()[:, 512 * k:512 * k + w]


    def _finish():
        for s in b.c:
            if s not in ("T", "V", "S"):
                b.wait("sync", s)
        with nc.Block() as block:
            @block.sync
            def _(eng):
                for fn in b.ops["sync"]:
                    fn(eng)

            @block.gpsimd
            def _(eng):
                for fn in b.ops["gpsimd"]:
                    fn(eng)

            @block.tensor
            def _(eng):
                for fn in b.ops["tensor"]:
                    fn(eng)

            @block.vector
            def _(eng):
                for fn in b.ops["vector"]:
                    fn(eng)

            @block.scalar
            def _(eng):
                for fn in b.ops["scalar"]:
                    fn(eng)
        return nc

    def ld(dst, src, sem="cst"):
        return b.inc("sync", lambda e, d=dst, s=src: e.dma_start(out=d, in_=s),
                     sem, 16)

    def ld8(dst_sb, src_h):
        for k in range(8):
            b.inc("gpsimd", lambda e, d=dst_sb.ap()[:, 1024 * k:1024 * k + 1024],
                  s=src_h[128 * k:128 * k + 128, :]: e.dma_start(out=d, in_=s),
                  "gq", 16)
        return b.c["gq"]

    ld(I128b.ap(), I128b_h)
    ld(F64b.ap(), F64b_h)
    ld(FlowB.ap(), FlowB_h)
    ld(Wr2.ap(), Wr2_h)
    ld(Wi2.ap(), Wi2_h)
    ld(ones2.ap(), ones2_h)
    ld(maskt.ap()[0:128, 0:256], maskc_h[0:128, :])
    ld(maskt.ap()[0:128, 256:512], maskc_h[128:256, :])
    ld(alpha_t.ap(), alpha_h)
    for k in range(8):
        ld(dwr_t.ap()[:, KS * k:KS * k + KS], dwr_h[128 * k:128 * k + 128, :])
        ld(dwi_t.ap()[:, KS * k:KS * k + KS], dwi_h[128 * k:128 * k + 128, :])
    consts_loaded = b.c["cst"]

    CST = {"lnscale": 0, "eps": 1, "nln2": 2, "p3ln2": 3, "pln2": 4, "n3ln2": 5,
           "smx": 6, "invln2": 7, "mhalf": 8}
    cvals = [1.0 / 1024.0, 1e-5, -LN2, 3 * LN2, LN2, -3 * LN2, 0.125, 1.0 / LN2,
             -0.5]
    for col in range(9):
        b.plain("vector", lambda e, c=col, v=cvals[col]:
                e.memset(cst.ap()[:, c:c + 1], v))
    b.seal("vector", "V")

    def ccol(name):
        i = CST[name]
        return cst.ap()[:, i:i + 1]

    b.wait("scalar", "cst", consts_loaded)
    b.wait("scalar", "V")
    b.inc("scalar", lambda e: e.activation(alpha_sg.ap(), alpha_t.ap(), AF.Sigmoid),
          "S")
    es01 = ExitStack()
    WfcqT = sb("WfcqT", (128, 8 * HID), BF16, es01)
    WpjqT = sb("WpjqT", (128, 32 * D), BF16, es01)

    tb = sb("tb", (128, 2048), F32, es01)
    ib = sb("ib", (128, 2048), I32, es01)
    amx = sb("amx", (128, 96), F32, es01)
    yq = sb("yq", (128, 96), F32, es01)
    e32q = sb("e32q", (128, 96), I32, es01)
    rsq = sb("rsq", (128, 96), F32, es01)
    scq = sb("scq", (128, 96), F32, es01)
    es_stg = ExitStack()
    wstg = sb("wstg", (128, 2048), F32, es_stg)
    wqb = sb("wqb", (128, 2048), BF16, es_stg)
    bst_f = sb("bst_f", (1, 1024), F32, es_stg)
    bst_hi = sb("bst_hi", (1, 1024), BF16, es_stg)
    bst_lo = sb("bst_lo", (1, 1024), BF16, es_stg)

    for bi, (src_hbm, dstt, n_ch) in enumerate(((bfc_h, bspl_fc, 4),
                                                (bpj_h, bspl_pj, 1))):
        for cchunk in range(n_ch):
            bldv = ld(bst_f.ap(), src_hbm[0:1, 1024 * cchunk:1024 * cchunk + 1024],
                      "bld")
            b.wait("vector", "bld", bldv)
            b.plain("vector", lambda e: e.tensor_copy(bst_hi.ap(), bst_f.ap()))
            b.plain("vector", lambda e: e.drain())
            b.plain("vector", lambda e: e.tensor_tensor(
                bst_lo.ap(), bst_f.ap(), bst_hi.ap(), op=AluOpType.subtract))
            vsp = b.seal("vector", "V")
            b.wait("sync", "V", vsp)
            for r, srcv in ((0, bst_hi), (1, bst_lo)):
                b.inc("sync", lambda e, s=srcv, d=dstt, r=r, c=cchunk:
                      e.dma_start(out=d.ap()[r:r + 1, 1024 * c:1024 * c + 1024],
                                  in_=s.ap()[0:1, :]), "bspl", 16)
            b.wait("vector", "bspl")

    def q_views(src_ap, nblk, dst_ap, acol, tcol):
        W = 32 * nblk
        rs = rsq.ap()[:, acol:acol + nblk]
        sc = scq.ap()[:, acol:acol + nblk]
        return {
            "a": amx.ap()[:, acol:acol + nblk],
            "y": yq.ap()[:, acol:acol + nblk],
            "e32": e32q.ap()[:, acol:acol + nblk],
            "rs": rs, "sc": sc,
            "src3": src_ap.rearrange("p (nb k) -> p nb k", k=32),
            "tw": tb.ap()[:, tcol:tcol + W],
            "iw": ib.ap()[:, tcol:tcol + W],
            "t3": tb.ap()[:, tcol:tcol + W].rearrange("p (nb k) -> p nb k", k=32),
            "i3": ib.ap()[:, tcol:tcol + W].rearrange("p (nb k) -> p nb k", k=32),
            "dst3": dst_ap.rearrange("p (nb k) -> p nb k", k=32),
            "rs3": rs.unsqueeze(2).broadcast_to([128, nblk, 32]),
            "sc3": sc.unsqueeze(2).broadcast_to([128, nblk, 32]),
        }

    def q_amax(qv):
        b.plain("vector", lambda e: e.tensor_reduce(
            qv["a"], qv["src3"], axis=AX.X, op=AluOpType.max,
            apply_absolute_value=True))
        qv["vA"] = b.seal("vector", "V")

    def q_scales(qv):
        b.plain("vector", lambda e: e.drain())
        b.plain("vector", lambda e: e.tensor_scalar_max(qv["a"], qv["a"], 1e-37))
        b.plain("vector", lambda e: e.drain())
        b.plain("vector", lambda e: e.tensor_scalar(
            qv["e32"], qv["a"].bitcast(I32), 0x7F800000, None,
            op0=AluOpType.bitwise_and))
        b.plain("vector", lambda e: e.drain())
        b.plain("vector", lambda e: e.tensor_scalar_mul(
            qv["sc"], qv["e32"].bitcast(F32), 0.125))
        b.plain("vector", lambda e: e.drain())
        b.plain("vector", lambda e: e.reciprocal(qv["rs"], qv["sc"]))
        b.plain("vector", lambda e: e.drain())
        qv["sB"] = None

    def q_apply(qv):
        b.plain("vector", lambda e: e.tensor_tensor(
            qv["t3"], qv["src3"], qv["rs3"], op=AluOpType.mult))
        b.plain("vector", lambda e: e.drain())
        b.plain("vector", lambda e: e.tensor_copy(qv["iw"], qv["tw"]))
        b.plain("vector", lambda e: e.drain())
        b.plain("vector", lambda e: e.tensor_tensor(
            qv["dst3"], qv["i3"], qv["sc3"], op=AluOpType.mult))
        qv["done"] = b.seal("vector", "V")

    def t_transpose8(src_bf_ap):
        for blk in range(8):
            bank = blk // 4
            o = psB.ap()[:, 512 * bank + 128 * (blk % 4):
                         512 * bank + 128 * (blk % 4) + 128]
            sblk = src_bf_ap[:, 128 * blk:128 * blk + 128]
            b.plain("tensor", lambda e, o=o, l=sblk:
                    e.matmul(o, l, I128b.ap(), start=True, stop=True))
        return b.seal("tensor", "T")

    def s_evac8(dst_views, tT):
        b.wait("scalar", "T", tT)
        for blk in range(8):
            bank = blk // 4
            s = psB.ap()[:, 512 * bank + 128 * (blk % 4):
                         512 * bank + 128 * (blk % 4) + 128]
            b.plain("scalar", lambda e, o=dst_views[blk], s=s: e.copy(o, s))
        return b.seal("scalar", "S")

    def p0_src(u):
        if u < 32:
            return wfc_h[128 * u:128 * u + 128, :]
        t8, qv_ = (u - 32) // 4, (u - 32) % 4
        return wpj_h[128 * t8:128 * t8 + 128, 1024 * qv_:1024 * qv_ + 1024]

    def p0_dsts(u):
        if u < 32:
            return [WfcqT.ap()[:, HID * j + 128 * u:HID * j + 128 * u + 128]
                    for j in range(8)]
        t8, qv_ = (u - 32) // 4, (u - 32) % 4
        return [WpjqT.ap()[:, D * (8 * qv_ + j) + 128 * t8:
                           D * (8 * qv_ + j) + 128 * t8 + 128] for j in range(8)]

    NU = 64
    uq = {}
    uld = {}
    uld[0] = ld(wstg.ap()[:, 0:1024], p0_src(0), "wst0")
    uld[1] = ld(wstg.ap()[:, 1024:2048], p0_src(1), "wst1")
    for u in range(NU):
        par = u % 2
        qv = q_views(wstg.ap()[:, 1024 * par:1024 * par + 1024], 32,
                     wqb.ap()[:, 1024 * par:1024 * par + 1024], 32 * par, 0)
        uq[u] = qv
        b.wait("vector", "wst%d" % par, uld[u])
        if u >= 2:
            b.wait("vector", "T", uq[u - 2]["tr"])
        q_amax(qv)
        q_scales(qv)
        if u >= 1:
            pq = uq[u - 1]
            q_apply(pq)
            b.wait("tensor", "V", pq["done"])
            if u >= 2:
                b.wait("tensor", "S", uq[u - 2]["te"])
            pq["tr"] = t_transpose8(wqb.ap()[:, 1024 * ((u - 1) % 2):
                                             1024 * ((u - 1) % 2) + 1024])
            pq["te"] = s_evac8(p0_dsts(u - 1), pq["tr"])
            if u + 1 < NU:
                b.wait("sync", "V", pq["done"])
                uld[u + 1] = ld(wstg.ap()[:, 1024 * ((u + 1) % 2):
                                          1024 * ((u + 1) % 2) + 1024],
                                p0_src(u + 1), "wst%d" % ((u + 1) % 2))
    pq = uq[NU - 1]
    q_apply(pq)
    b.wait("tensor", "V", pq["done"])
    b.wait("tensor", "S", uq[NU - 2]["te"])
    pq["tr"] = t_transpose8(wqb.ap()[:, 1024 * ((NU - 1) % 2):
                                     1024 * ((NU - 1) % 2) + 1024])
    pq["te"] = s_evac8(p0_dsts(NU - 1), pq["tr"])

    es_stg.close()
    if phases <= 0:
        b.full_barrier()
        return _finish()

    xt = sb("xt", (128, 2048), F32, es01)
    xn_t = sb("xn_t", (128, 2048), F32, es01)
    xnb_t = sb("xnb_t", (128, 1024), BF16, es01)
    xq_t = sb("xq_t", (128, 2048), BF16, es01)
    xqT_t = sb("xqT_t", (128, 2048), BF16, es01)
    hb_t = sb("hb_t", (128, 2048), F32, es01)
    hq_t = sb("hq_t", (128, 2048), BF16, es01)
    hqT_t = sb("hqT_t", (128, 1024), BF16, es01)
    g_t = sb("g_t", (128, 1024), F32, es01)
    lnst = sb("lnst", (128, 16), F32, es01)

    b.full_barrier()

    chk = {}

    def emit_prep_v1(j):
        sl = (j % 2) * 1024
        xap = xt.ap()[:, sl:sl + 1024]
        lo = 8 * (j % 2)
        b.wait("vector", "x%d" % (j % 2), chk[(j, "xl")])
        b.plain("vector", lambda e, x=xap: e.reduce_sum(
            lnst.ap()[:, lo:lo + 1], x, axis=AX.X))
        b.plain("vector", lambda e: e.drain())
        b.plain("vector", lambda e: e.tensor_scalar_mul(
            lnst.ap()[:, lo + 1:lo + 2], lnst.ap()[:, lo:lo + 1], -1.0 / 1024.0))
        chk[(j, "ln1")] = b.seal("vector", "V")

    def emit_prep_s(j):
        sl = (j % 2) * 1024
        xap = xt.ap()[:, sl:sl + 1024]
        lo = 8 * (j % 2)
        b.wait("scalar", "V", chk[(j, "ln1")])
        b.plain("scalar", lambda e, x=xap, s=sl: e.activation(
            xn_t.ap()[:, s:s + 1024], x, AF.Square,
            bias=lnst.ap()[:, lo + 1:lo + 2], accum_out=lnst.ap()[:, lo + 2:lo + 3]))
        b.seal("scalar", "S")
        b.wait("scalar", "S")
        b.plain("scalar", lambda e: e.activation(
            lnst.ap()[:, lo + 3:lo + 4], lnst.ap()[:, lo + 2:lo + 3], AF.Ln,
            scale=ccol("lnscale"), bias=ccol("eps")))
        b.seal("scalar", "S")
        b.wait("scalar", "S")
        b.plain("scalar", lambda e: e.activation(
            lnst.ap()[:, lo + 4:lo + 5], lnst.ap()[:, lo + 3:lo + 4], AF.Exp,
            scale=ccol("mhalf")))
        chk[(j, "rstd")] = b.seal("scalar", "S")

    def emit_prep_v2(j):
        sl = (j % 2) * 1024
        xap = xt.ap()[:, sl:sl + 1024]
        lo = 8 * (j % 2)
        b.wait("vector", "S", chk[(j, "rstd")])
        b.plain("vector", lambda e: e.tensor_tensor(
            lnst.ap()[:, lo + 5:lo + 6], lnst.ap()[:, lo + 1:lo + 2],
            lnst.ap()[:, lo + 4:lo + 5], op=AluOpType.mult))
        b.plain("vector", lambda e: e.drain())
        b.plain("vector", lambda e, x=xap, s=sl: e.tensor_scalar(
            xn_t.ap()[:, s:s + 1024], x, lnst.ap()[:, lo + 4:lo + 5],
            lnst.ap()[:, lo + 5:lo + 6], op0=AluOpType.mult, op1=AluOpType.add))
        b.plain("vector", lambda e: e.drain())
        if (j - 1, "xnbst") in chk:
            b.wait("vector", "xnbst", chk[(j - 1, "xnbst")])
        b.plain("vector", lambda e, s=sl: e.tensor_copy(
            xnb_t.ap(), xn_t.ap()[:, s:s + 1024]))
        chk[(j, "xn")] = b.seal("vector", "V")

    chk[(0, "xl")] = ld(xt.ap()[:, 0:1024], x_h[0:128, :], "x0")
    chk[(1, "xl")] = ld(xt.ap()[:, 1024:2048], x_h[128:256, :], "x1")
    emit_prep_v1(0)
    emit_prep_s(0)
    emit_prep_v2(0)
    b.wait("sync", "V", chk[(0, "xn")])
    chk[(0, "xnbst")] = b.inc("sync", lambda e: e.dma_start(
        out=xnb_s[0:128, :], in_=xnb_t.ap()), "xnbst", 16)
    qx0 = q_views(xn_t.ap()[:, 0:1024], 32, xq_t.ap()[:, 0:1024], 0, 1024)
    chk[(0, "qx")] = qx0
    q_amax(qx0)
    q_scales(qx0)
    q_apply(qx0)
    b.wait("tensor", "V", qx0["done"])
    b.wait("tensor", "S", uq[NU - 1]["te"])
    chk[(0, "xqT")] = t_transpose8(xq_t.ap()[:, 0:1024])
    chk[(0, "xqTe")] = s_evac8(
        [xqT_t.ap()[:, 128 * j:128 * j + 128] for j in range(8)], chk[(0, "xqT")])

    for i in range(NTILES):
        nx = i + 1

        def hmm(w):
            pb = 2 * (w % 2)
            if w >= 2:
                b.wait("tensor", "S", chk[(i, w - 2, "g")])
            elif (i - 1, w + 2, "g") in chk:
                b.wait("tensor", "S", chk[(i - 1, w + 2, "g")])
            if w == 0:
                b.wait("tensor", "S", chk[(i, "xqTe")])
            for K in range(8):
                lhs = xqT_t.ap()[:, ((i % 2) * 1024) + 128 * K:
                                 ((i % 2) * 1024) + 128 * K + 128]
                for c2 in range(2):
                    rhs = WfcqT.ap()[:, HID * K + 1024 * w + 512 * c2:
                                     HID * K + 1024 * w + 512 * c2 + 512]
                    b.plain("tensor", lambda e, o=bankA(pb + c2), l=lhs, r=rhs,
                            st=(K == 0): e.matmul(o, l, r, start=st, stop=False))
            for c2 in range(2):
                rhs = bspl_fc.ap()[:, 1024 * w + 512 * c2:1024 * w + 512 * c2 + 512]
                b.plain("tensor", lambda e, o=bankA(pb + c2), r=rhs:
                        e.matmul(o, ones2.ap(), r, start=False, stop=True))
            chk[(i, w, "hm")] = b.seal("tensor", "T")

        def gelu(w):
            pb = 2 * (w % 2)
            hs = (w % 2) * 1024
            b.wait("scalar", "T", chk[(i, w, "hm")])
            if (i, w - 2, "ap") in chk:
                b.wait("scalar", "V", chk[(i, w - 2, "ap")])
            elif (i - 1, w + 2, "ap") in chk:
                b.wait("scalar", "V", chk[(i - 1, w + 2, "ap")])
            for c2 in range(2):
                b.plain("scalar", lambda e,
                        o=hb_t.ap()[:, hs + 512 * c2:hs + 512 * c2 + 512],
                        s=bankA(pb + c2): e.activation(o, s, AF.Gelu))
            chk[(i, w, "g")] = b.seal("scalar", "S")

        def mk_qh(w):
            hs = (w % 2) * 1024
            return q_views(hb_t.ap()[:, hs:hs + 1024], 32,
                           hq_t.ap()[:, hs:hs + 1024], 32 + 32 * (w % 2), 0)

        def t_tr(w):
            b.wait("tensor", "V", chk[(i, w, "ap")])
            chk[(i, w, "tr")] = t_transpose8(hq_t.ap()[:, (w % 2) * 1024:
                                                       (w % 2) * 1024 + 1024])

        def s_te(w):
            chk[(i, w, "te")] = s_evac8(
                [hqT_t.ap()[:, 128 * j:128 * j + 128] for j in range(8)],
                chk[(i, w, "tr")])

        def t_f(w):
            b.wait("tensor", "S", chk[(i, w, "te")])
            if w == 0 and (i - 1, "sil") in chk:
                b.wait("tensor", "S", chk[(i - 1, "sil")])
            for K in range(8):
                lhs = hqT_t.ap()[:, 128 * K:128 * K + 128]
                for c2 in range(2):
                    rhs = WpjqT.ap()[:, D * (8 * w + K) + 512 * c2:
                                     D * (8 * w + K) + 512 * c2 + 512]
                    st = (w == 0 and K == 0)
                    b.plain("tensor", lambda e, o=bankB(2 + c2), l=lhs, r=rhs,
                            st=st: e.matmul(o, l, r, start=st, stop=False))
            if w == 3:
                for c2 in range(2):
                    rhs = bspl_pj.ap()[:, 512 * c2:512 * c2 + 512]
                    b.plain("tensor", lambda e, o=bankB(2 + c2), r=rhs:
                            e.matmul(o, ones2.ap(), r, start=False, stop=True))
            chk[(i, w, "f")] = b.seal("tensor", "T")

        def v_amax(w, qh):
            b.wait("vector", "S", chk[(i, w, "g")])
            q_amax(qh)
            chk[(i, w, "am")] = qh["vA"]

        def v_apply(w, qh):
            if (i, w - 2, "tr") in chk:
                b.wait("vector", "T", chk[(i, w - 2, "tr")])
            elif (i - 1, w + 2, "tr") in chk:
                b.wait("vector", "T", chk[(i - 1, w + 2, "tr")])
            q_apply(qh)
            chk[(i, w, "ap")] = qh["done"]

        qh = {}
        hmm(0)
        hmm(1)
        if nx < NTILES and nx + 1 < NTILES:
            b.wait("sync", "V", chk[(i, "xn")])
            chk[(nx + 1, "xl")] = ld(
                xt.ap()[:, ((nx + 1) % 2) * 1024:((nx + 1) % 2) * 1024 + 1024],
                x_h[128 * (nx + 1):128 * (nx + 1) + 128, :],
                "x%d" % ((nx + 1) % 2))
        gelu(0)
        if nx < NTILES:
            emit_prep_v1(nx)
        qh[0] = mk_qh(0)
        v_amax(0, qh[0])
        q_scales(qh[0])
        if nx < NTILES:
            emit_prep_s(nx)
        hmm(2)
        gelu(1)
        if nx < NTILES:
            emit_prep_v2(nx)
            b.wait("sync", "V", chk[(nx, "xn")])
            chk[(nx, "xnbst")] = b.inc("sync", lambda e, j=nx: e.dma_start(
                out=xnb_s[128 * j:128 * j + 128, :], in_=xnb_t.ap()), "xnbst", 16)
        qh[1] = mk_qh(1)
        v_amax(1, qh[1])
        q_scales(qh[1])
        v_apply(0, qh[0])
        t_tr(0)
        s_te(0)
        t_f(0)
        hmm(3)
        gelu(2)
        qh[2] = mk_qh(2)
        v_amax(2, qh[2])
        q_scales(qh[2])
        v_apply(1, qh[1])
        t_tr(1)
        s_te(1)
        t_f(1)
        gelu(3)
        qh[3] = mk_qh(3)
        v_amax(3, qh[3])
        q_scales(qh[3])
        v_apply(2, qh[2])
        t_tr(2)
        s_te(2)
        t_f(2)
        v_apply(3, qh[3])
        t_tr(3)
        s_te(3)
        t_f(3)
        if nx < NTILES:
            qx = q_views(xn_t.ap()[:, ((nx % 2) * 1024):((nx % 2) * 1024) + 1024],
                         32, xq_t.ap()[:, ((nx % 2) * 1024):((nx % 2) * 1024) + 1024],
                         0, 1024)
            q_amax(qx)
            q_scales(qx)
            if (i - 1, "xqT") in chk:
                b.wait("vector", "T", chk[(i - 1, "xqT")])
            q_apply(qx)
            b.wait("tensor", "V", qx["done"])
            b.wait("tensor", "S", chk[(i, 3, "te")])
            chk[(nx, "xqT")] = t_transpose8(
                xq_t.ap()[:, ((nx % 2) * 1024):((nx % 2) * 1024) + 1024])
            chk[(nx, "xqTe")] = s_evac8(
                [xqT_t.ap()[:, ((nx % 2) * 1024) + 128 * j:
                            ((nx % 2) * 1024) + 128 * j + 128] for j in range(8)],
                chk[(nx, "xqT")])
        b.wait("scalar", "T", chk[(i, 3, "f")])
        if (i - 1, "gst") in chk:
            b.wait("scalar", "gst", chk[(i - 1, "gst")])
        for c2 in range(2):
            b.plain("scalar", lambda e, o=g_t.ap()[:, 512 * c2:512 * c2 + 512],
                    s=bankB(2 + c2): e.activation(o, s, AF.Silu))
        chk[(i, "sil")] = b.seal("scalar", "S")
        b.wait("sync", "S", chk[(i, "sil")])
        chk[(i, "gst")] = b.inc("sync", lambda e, j=i: e.dma_start(
            out=g_s[128 * j:128 * j + 128, :], in_=g_t.ap()), "gst", 16)

    b.full_barrier()
    es01.close()
    if phases <= 1:
        return _finish()

    esM = ExitStack()
    SpecTr = sb("SpecTr", (128, 8 * NCHUNK * M), BF16, esM)
    SpecTi = sb("SpecTi", (128, 8 * NCHUNK * M), BF16, esM)

    es23 = ExitStack()
    XlTr = sb("XlTr", (128, 8 * 256), F32, es23)
    XlTi = sb("XlTi", (128, 8 * 256), F32, es23)

    es2 = ExitStack()
    GmT = sb("GmT", (128, 8 * 64), BF16, es2)
    G_T = sb("G_T", (128, 8 * 64), F32, es2)
    cpr = sb("cpr", (128, 8 * G_CH * 35), BF16, es2)
    cpi = sb("cpi", (128, 8 * G_CH * 35), BF16, es2)
    Hbt = sb("Hbt", (128, 8 * GW), BF16, es2)
    dwdg = sb("dwdg", (128, 2 * 7 * 8 * 128), BF16, es2)
    pwrT = sb("pwrT_sb", (128, 8 * D), BF16, es2)
    pwiT = sb("pwiT_sb", (128, 8 * D), BF16, es2)
    xnbp = sb("xnbp", (128, 2048), BF16, es2)

    def dwdg_ap(comp, j, dblk):
        off = 128 * (dblk + 8 * (j + 7 * comp))
        return dwdg.ap()[:, off:off + 128]

    for comp, dwt in ((0, dwr_t), (1, dwi_t)):
        for j in range(KS):
            for dblk in range(8):
                col = dwt.ap()[:, KS * dblk + j:KS * dblk + j + 1]
                b.plain("vector", lambda e, o=dwdg_ap(comp, j, dblk), c=col:
                        e.tensor_scalar(o, I128b.ap(), c, None,
                                        op0=AluOpType.mult))
    b.seal("vector", "V")

    gw_ld = ld8(pwrT, gwT_h)
    if p2sub <= 0:
        b.full_barrier()
        return _finish()

    pa_done = {}
    for p in range(NPAIR):
        xslot = (p % 2) * 1024
        if p >= 2:
            b.wait("sync", "V", pa_done[p - 2])
        lxp = ld(xnbp.ap()[:, xslot:xslot + 1024],
                 xnb_s[128 * p:128 * p + 128, :], "p2x%d" % (p % 2))
        b.wait("tensor", "p2x%d" % (p % 2), lxp)
        if p >= 1:
            b.wait("tensor", "V", pa_done[p - 1])
        for half in range(2):
            ch = 2 * p + half
            base = 64 * half
            pb = (ch % 2) * 2
            for dblk in range(8):
                lhs = xnbp.ap()[base:base + 64,
                                xslot + 128 * dblk:xslot + 128 * dblk + 128]
                rhs = F64b.ap()[base:base + 64, :]
                bk = pb + dblk // 4
                o = psA.ap()[:, 512 * bk + 128 * (dblk % 4):
                             512 * bk + 128 * (dblk % 4) + 66]
                b.plain("tensor", lambda e, o=o, l=lhs, r=rhs:
                        e.matmul(o, l, r, start=True, stop=True))
        tpa = b.seal("tensor", "T")
        b.wait("vector", "T", tpa)
        for half in range(2):
            ch = 2 * p + half
            pb = (ch % 2) * 2
            dr = XlTr.ap().rearrange("p (db t) -> p db t", t=256)
            di = XlTi.ap().rearrange("p (db t) -> p db t", t=256)
            dg = GmT.ap().rearrange("p (db c) -> p db c", c=64)
            for hb2 in range(2):
                src = psA.ap()[:, 512 * (pb + hb2):512 * (pb + hb2) + 512]
                src3 = src.rearrange("p (s w) -> p s w", w=128)
                b.plain("vector", lambda e,
                        o=dr[:, 4 * hb2:4 * hb2 + 4, 4 * ch:4 * ch + 4],
                        s=src3[:, :, 0:4]: e.tensor_copy(o, s))
                b.plain("vector", lambda e,
                        o=di[:, 4 * hb2:4 * hb2 + 4, 4 * ch:4 * ch + 4],
                        s=src3[:, :, 33:37]: e.tensor_copy(o, s))
                b.plain("vector", lambda e,
                        o=dg[:, 4 * hb2:4 * hb2 + 4, ch:ch + 1],
                        s=src3[:, :, 0:1]: e.tensor_scalar_mul(o, s, 1.0 / 64.0))
        pa_done[p] = b.seal("vector", "V")

    if p2sub <= 1:
        b.full_barrier()
        return _finish()

    b.wait("tensor", "V")
    b.wait("tensor", "gq", gw_ld)
    for mtl in range(8):
        for K in range(8):
            lhs = pwrT.ap()[:, 1024 * K + 128 * mtl:1024 * K + 128 * mtl + 128]
            rhs = GmT.ap()[:, 64 * K:64 * K + 64]
            o = psB.ap()[:, 64 * mtl:64 * mtl + 64]
            b.plain("tensor", lambda e, o=o, l=lhs, r=rhs, st=(K == 0),
                    sp=(K == 7): e.matmul(o, l, r, start=st, stop=sp))
    tg = b.seal("tensor", "T")
    b.wait("scalar", "T", tg)
    sg2 = b.inc("scalar", lambda e: e.activation(G_T.ap(), psB.ap()[:, 0:512],
                                                 AF.Sigmoid), "S")
    pw_ld = 0
    if p2sub >= 3:
        b.wait("gpsimd", "S", sg2)
        ld8(pwrT, pwrT_h)
        pw_ld = ld8(pwiT, pwiT_h)

    if p2sub <= 2:
        b.full_barrier()
        return _finish()

    for buf in (cpr, cpi):
        pad = buf.ap().rearrange("p (q w) -> p q w", w=35)[:, :, 0:6]
        b.plain("vector", lambda e, o=pad: e.memset(o, 0.0))
    vz = b.seal("vector", "V")

    b.wait("tensor", "V", vz)
    b.wait("tensor", "gq", pw_ld)
    prev_evac = pa_done[NPAIR - 1]
    grp_last = {}
    for g in range(NGRP):
        pair_evac = {}
        for pp in range(4):
            p = 4 * g + pp
            xslot = (p % 2) * 1024
            free_v = pair_evac.get(pp - 2, prev_evac)
            b.wait("sync", "V", free_v)
            lxp = ld(xnbp.ap()[:, xslot:xslot + 1024],
                     xnb_s[128 * p:128 * p + 128, :], "p2x%d" % (p % 2))
            b.wait("tensor", "p2x%d" % (p % 2), lxp)
            b.wait("tensor", "V", pair_evac.get(pp - 1, prev_evac))
            for half in range(2):
                ch = 2 * p + half
                gc = ch - G_CH * g
                base = 64 * half
                pb = (ch % 2) * 2
                for dblk in range(8):
                    lhs = xnbp.ap()[base:base + 64,
                                    xslot + 128 * dblk:xslot + 128 * dblk + 128]
                    rhs = F64b.ap()[base:base + 64, :]
                    bk = pb + dblk // 4
                    o = psA.ap()[:, 512 * bk + 128 * (dblk % 4):
                                 512 * bk + 128 * (dblk % 4) + 66]
                    b.plain("tensor", lambda e, o=o, l=lhs, r=rhs:
                            e.matmul(o, l, r, start=True, stop=True))
                trf = b.seal("tensor", "T")
                b.wait("vector", "T", trf)
                for hb2 in range(2):
                    src = psA.ap()[:, 512 * (pb + hb2):512 * (pb + hb2) + 512]
                    src3 = src.rearrange("p (s w) -> p s w", w=128)
                    for comp, buf in ((0, cpr), (1, cpi)):
                        co = 4 if comp == 0 else 37
                        dst = buf.ap().rearrange(
                            "p (db g2 w) -> p db g2 w", w=35, g2=G_CH)[
                            :, 4 * hb2:4 * hb2 + 4, gc:gc + 1, 6:35]
                        s4 = src3[:, :, co:co + LHI].unsqueeze(2)
                        b.plain("vector", lambda e, o=dst, s=s4:
                                e.tensor_copy(o, s))
            pair_evac[pp] = b.seal("vector", "V")
        prev_evac = pair_evac[3]

        conv_g = {}
        pw_v = {}
        for comp in (range(2) if p2sub >= 4 else range(0)):
            cpad = cpr if comp == 0 else cpi
            pwT = pwrT if comp == 0 else pwiT
            spec = SpecTr if comp == 0 else SpecTi
            b.wait("tensor", "V", prev_evac)
            for dblk in range(8):
                pb2 = dblk % 4
                if (comp, dblk - 4) in conv_g:
                    b.wait("tensor", "S", conv_g[(comp, dblk - 4)])
                elif (comp - 1, dblk + 4) in conv_g:
                    b.wait("tensor", "S", conv_g[(comp - 1, dblk + 4)])
                for j in range(KS):
                    lhs = dwdg_ap(comp, j, dblk)
                    base_col = G_CH * 35 * dblk
                    rhs = cpad.ap()[:, base_col:base_col + G_CH * 35].rearrange(
                        "p (g2 w) -> p g2 w", w=35)[:, :, j:j + LHI]
                    o = bankA(pb2)[:, 0:GW]
                    b.plain("tensor", lambda e, o=o, l=lhs, r=rhs, st=(j == 0),
                            sp=(j == KS - 1): e.matmul(o, l, r, start=st, stop=sp))
                tcv = b.seal("tensor", "T")
                b.wait("scalar", "T", tcv)
                b.plain("scalar", lambda e, o=Hbt.ap()[:, GW * dblk:GW * (dblk + 1)],
                        s=bankA(pb2)[:, 0:GW]: e.activation(o, s, AF.Gelu))
                conv_g[(comp, dblk)] = b.seal("scalar", "S")
            b.wait("tensor", "S", conv_g.get((comp, 7), 0))
            for od in (range(8) if p2sub >= 5 else range(0)):
                pb2 = od % 4
                if (comp, od - 4) in pw_v:
                    b.wait("tensor", "V", pw_v[(comp, od - 4)])
                elif (comp - 1, od + 4) in pw_v:
                    b.wait("tensor", "V", pw_v[(comp - 1, od + 4)])
                for K in range(8):
                    lhs = pwT.ap()[:, 1024 * K + 128 * od:1024 * K + 128 * od + 128]
                    rhs = Hbt.ap()[:, GW * K:GW * (K + 1)]
                    o = bankB(pb2)[:, 0:GW]
                    b.plain("tensor", lambda e, o=o, l=lhs, r=rhs, st=(K == 0),
                            sp=(K == 7): e.matmul(o, l, r, start=st, stop=sp))
                tpw = b.seal("tensor", "T")
                b.wait("vector", "T", tpw)
                gsl = G_T.ap()[:, 64 * od + G_CH * g:64 * od + G_CH * (g + 1)] \
                    .unsqueeze(2).broadcast_to([128, G_CH, LHI])
                doff = NCHUNK * M * od + M * G_CH * g
                dst = spec.ap()[:, doff:doff + M * G_CH].rearrange(
                    "p (ch m) -> p ch m", m=M)[:, :, 4:33]
                src = bankB(pb2)[:, 0:GW].rearrange("p (ch m) -> p ch m", m=LHI)
                b.plain("vector", lambda e, o=dst, s=src, gg=gsl:
                        e.tensor_tensor(o, s, gg, op=AluOpType.mult))
                pw_v[(comp, od)] = b.seal("vector", "V")
            if (comp, 7) in pw_v:
                b.wait("tensor", "V", pw_v[(comp, 7)])

    b.full_barrier()
    es2.close()
    if phases <= 2:
        return _finish()

    es3 = ExitStack()
    qwT = sb("qwT_sb", (128, 8 * D), BF16, es3)
    kwT = sb("kwT_sb", (128, 8 * D), BF16, es3)
    vwT = sb("vwT_sb", (128, 8 * D), BF16, es3)
    owT = sb("owT_sb", (128, 8 * D), BF16, es3)
    XlTbr = sb("XlTbr", (128, 2048), BF16, es3)
    XlTbi = sb("XlTbi", (128, 2048), BF16, es3)
    QTb = sb("QTb", (128, 2048), BF16, es3)
    KTb = sb("KTb", (128, 2048), BF16, es3)
    Vtok = sb("Vtok", (128, 2048), BF16, es3)
    OTb = sb("OTb", (128, 2048), BF16, es3)
    Ssb = sb("Ssb", (128, 512), F32, es3)
    Pf = sb("Pf", (128, 512), F32, es3)
    Pbf = sb("Pbf", (128, 512), BF16, es3)
    PbT = sb("PbT", (128, 512), BF16, es3)
    smst = sb("smst", (128, 8), F32, es3)
    d1b = sb("d1b", (128, 256), F32, es3)

    ld8(qwT, qwT_h)
    half_ld = ld8(kwT, kwT_h)
    b.wait("gpsimd", "gq", half_ld)
    ld8(vwT, vwT_h)
    aw_ld = ld8(owT, owT_h)
    b.wait("vector", "gq", aw_ld)
    b.plain("vector", lambda e: e.tensor_copy(XlTbr.ap(), XlTr.ap()))
    b.inc("vector", lambda e: e.tensor_copy(XlTbi.ap(), XlTi.ap()), "V")
    b.full_barrier()

    for comp in range(2):
        XlTb = XlTbr if comp == 0 else XlTbi
        XlTf = XlTr if comp == 0 else XlTi
        spec = SpecTr if comp == 0 else SpecTi

        for wT, dstb in ((qwT, QTb), (kwT, KTb)):
            for mtl in range(8):
                pbk = mtl % 4
                slot = 256 * (mtl // 4)
                for K in range(8):
                    lhs = wT.ap()[:, 1024 * K + 128 * mtl:1024 * K + 128 * mtl + 128]
                    rhs = XlTb.ap()[:, 256 * K:256 * K + 256]
                    o = bankA(pbk)[:, slot:slot + 256]
                    b.plain("tensor", lambda e, o=o, l=lhs, r=rhs, st=(K == 0),
                            sp=(K == 7): e.matmul(o, l, r, start=st, stop=sp))
            tq = b.seal("tensor", "T")
            b.wait("scalar", "T", tq)
            for mtl in range(8):
                pbk = mtl % 4
                slot = 256 * (mtl // 4)
                b.plain("scalar", lambda e,
                        o=dstb.ap()[:, 256 * mtl:256 * mtl + 256],
                        s=bankA(pbk)[:, slot:slot + 256]: e.copy(o, s))
            sq = b.seal("scalar", "S")
            b.wait("tensor", "S", sq)
        for mt in range(2):
            for K in range(8):
                lhs = XlTb.ap()[:, 256 * K + 128 * mt:256 * K + 128 * mt + 128]
                for c2 in range(2):
                    rhs = vwT.ap()[:, 1024 * K + 512 * c2:1024 * K + 512 * c2 + 512]
                    o = bankA(2 * mt + c2)
                    b.plain("tensor", lambda e, o=o, l=lhs, r=rhs, st=(K == 0),
                            sp=(K == 7): e.matmul(o, l, r, start=st, stop=sp))
        tv = b.seal("tensor", "T")
        b.wait("scalar", "T", tv)
        for k4 in range(4):
            b.plain("scalar", lambda e, o=Vtok.ap()[:, 512 * k4:512 * k4 + 512],
                    s=bankA(k4): e.copy(o, s))
        sv2 = b.seal("scalar", "S")
        b.wait("tensor", "S", sv2)

        prev_v = 0
        for h in range(16):
            base = 64 * (h % 2)
            tl = h // 2
            if prev_v:
                b.wait("tensor", "V", prev_v)
            for qm in range(2):
                lhs = QTb.ap()[base:base + 64,
                               256 * tl + 128 * qm:256 * tl + 128 * qm + 128]
                rhs = KTb.ap()[base:base + 64, 256 * tl:256 * tl + 256]
                o = bankB(qm)[:, 0:256]
                b.plain("tensor", lambda e, o=o, l=lhs, r=rhs:
                        e.matmul(o, l, r, start=True, stop=True))
            ts2 = b.seal("tensor", "T")
            b.wait("vector", "T", ts2)
            for qm in range(2):
                b.plain("vector", lambda e, o=Ssb.ap()[:, 256 * qm:256 * qm + 256],
                        s=bankB(qm)[:, 0:256],
                        m=maskt.ap()[:, 256 * qm:256 * qm + 256]:
                        e.tensor_tensor(o, s, m, op=AluOpType.add))
            b.plain("vector", lambda e: e.drain())
            for qm in range(2):
                b.plain("vector", lambda e, o=smst.ap()[:, qm:qm + 1],
                        s=Ssb.ap()[:, 256 * qm:256 * qm + 256]:
                        e.reduce_max(o, s, axis=AX.X))
            b.plain("vector", lambda e: e.drain())
            for qm in range(2):
                b.plain("vector", lambda e, o=smst.ap()[:, 2 + qm:3 + qm],
                        s=smst.ap()[:, qm:qm + 1]: e.tensor_scalar_mul(o, s, -0.125))
            vs1 = b.seal("vector", "V")
            prev_v = vs1
            b.wait("scalar", "V", vs1)
            for qm in range(2):
                b.plain("scalar", lambda e, o=Pf.ap()[:, 256 * qm:256 * qm + 256],
                        s=Ssb.ap()[:, 256 * qm:256 * qm + 256],
                        bi=smst.ap()[:, 2 + qm:3 + qm],
                        ac=smst.ap()[:, 4 + qm:5 + qm]:
                        e.activation(o, s, AF.Exp, scale=ccol("smx"), bias=bi,
                                     accum_out=ac))
            se1 = b.seal("scalar", "S")
            b.wait("vector", "S", se1)
            b.plain("vector", lambda e: e.reciprocal(smst.ap()[:, 6:8],
                                                     smst.ap()[:, 4:6]))
            vs2 = b.seal("vector", "V")
            b.wait("scalar", "V", vs2)
            for qm in range(2):
                b.plain("scalar", lambda e, o=Pbf.ap()[:, 256 * qm:256 * qm + 256],
                        s=Pf.ap()[:, 256 * qm:256 * qm + 256],
                        sc2=smst.ap()[:, 6 + qm:7 + qm]:
                        e.activation(o, s, AF.Copy, scale=sc2))
            se2 = b.seal("scalar", "S")
            b.wait("tensor", "S", se2)
            for blk in range(4):
                qm, kb = blk % 2, blk // 2
                lhs = Pbf.ap()[:, 256 * qm + 128 * kb:256 * qm + 128 * kb + 128]
                o = psB.ap()[:, 1024 + 128 * (2 * kb + qm):
                             1024 + 128 * (2 * kb + qm) + 128]
                b.plain("tensor", lambda e, o=o, l=lhs:
                        e.matmul(o, l, I128b.ap(), start=True, stop=True))
            tpt = b.seal("tensor", "T")
            b.wait("scalar", "T", tpt)
            b.plain("scalar", lambda e: e.copy(PbT.ap(), psB.ap()[:, 1024:1536]))
            spt = b.seal("scalar", "S")
            b.wait("tensor", "S", spt)
            for kt in range(2):
                lhs = Vtok.ap()[:, 1024 * kt + 64 * h:1024 * kt + 64 * h + 64]
                rhs = PbT.ap()[:, 256 * kt:256 * kt + 256]
                o = psB.ap()[base:base + 64, 1536:1536 + 256]
                b.plain("tensor", lambda e, o=o, l=lhs, r=rhs, st=(kt == 0),
                        sp=(kt == 1), tp=(0, base):
                        e.matmul(o, l, r, start=st, stop=sp, tile_position=tp))
            to = b.seal("tensor", "T")
            b.wait("scalar", "T", to)
            b.plain("scalar", lambda e, o=OTb.ap()[base:base + 64,
                                                   256 * tl:256 * tl + 256],
                    s=psB.ap()[base:base + 64, 1536:1536 + 256]: e.copy(o, s))
            so = b.seal("scalar", "S")
            b.wait("tensor", "S", so)

        for mtl in range(8):
            pbk = mtl % 2
            for K in range(8):
                lhs = owT.ap()[:, 1024 * K + 128 * mtl:1024 * K + 128 * mtl + 128]
                rhs = OTb.ap()[:, 256 * K:256 * K + 256]
                o = bankA(pbk)[:, 0:256]
                b.plain("tensor", lambda e, o=o, l=lhs, r=rhs, st=(K == 0),
                        sp=(K == 7): e.matmul(o, l, r, start=st, stop=sp))
            tso = b.seal("tensor", "T")
            b.wait("vector", "T", tso)
            b.plain("vector", lambda e, s=bankA(pbk)[:, 0:256],
                    xl=XlTf.ap()[:, 256 * mtl:256 * mtl + 256]:
                    e.tensor_tensor(d1b.ap(), s, xl, op=AluOpType.subtract))
            b.plain("vector", lambda e: e.drain())
            doff = NCHUNK * M * mtl
            dst = spec.ap()[:, doff:doff + NCHUNK * M].rearrange(
                "p (ch m) -> p ch m", m=M)[:, :, 0:4]
            b.plain("vector", lambda e, o=dst,
                    s=d1b.ap().rearrange("p (ch m) -> p ch m", m=4):
                    e.tensor_scalar(o, s, alpha_sg.ap()[:, 0:1], None,
                                    op0=AluOpType.mult))
            vso = b.seal("vector", "V")
            b.wait("tensor", "V", vso)

    b.full_barrier()
    es3.close()
    es23.close()
    if phases <= 3:
        return _finish()

    es4 = ExitStack()
    slotR = sb("slotR", (66, 1024), BF16, es4)
    slotI = sb("slotI", (66, 1024), BF16, es4)
    xnb4 = sb("xnb4", (128, 2048), BF16, es4)
    xp4 = sb("xp4", (128, 2048), F32, es4)
    gp4 = sb("gp4", (128, 2048), F32, es4)
    ub4 = sb("ub4", (128, 1024), F32, es4)
    ob4 = sb("ob4", (128, 2048), F32, es4)

    p4 = {}
    for p in range(NPAIR):
        sl = (p % 2) * 1024
        psem = "p4l%d" % (p % 2)
        if p >= 2:
            b.wait("sync", "V", p4[(p - 2, "u")])
        l1 = ld(xnb4.ap()[:, sl:sl + 1024], xnb_s[128 * p:128 * p + 128, :], psem)
        ld(xp4.ap()[:, sl:sl + 1024], x_h[128 * p:128 * p + 128, :], psem)
        l3 = ld(gp4.ap()[:, sl:sl + 1024], g_s[128 * p:128 * p + 128, :], psem)
        if p >= 1:
            b.wait("tensor", "S", p4[(p - 1, "slots")])
        for comp in range(2):
            spec = SpecTr if comp == 0 else SpecTi
            for dblk in range(8):
                lhs = spec.ap()[:, NCHUNK * M * dblk + 66 * p:
                                NCHUNK * M * dblk + 66 * p + 66]
                bank = 2 * comp + dblk // 4
                o = psA.ap()[0:66, 512 * bank + 128 * (dblk % 4):
                             512 * bank + 128 * (dblk % 4) + 128]
                b.plain("tensor", lambda e, o=o, l=lhs:
                        e.matmul(o, l, I128b.ap(), start=True, stop=True))
        ttr = b.seal("tensor", "T")
        b.wait("scalar", "T", ttr)
        for comp in range(2):
            dstb = slotR if comp == 0 else slotI
            for hb2 in range(2):
                bank = 2 * comp + hb2
                b.plain("scalar", lambda e,
                        o=dstb.ap()[0:66, 512 * hb2:512 * hb2 + 512],
                        s=psA.ap()[0:66, 512 * bank:512 * bank + 512]:
                        e.copy(o, s))
        p4[(p, "slots")] = b.seal("scalar", "S")
        b.wait("tensor", "S", p4[(p, "slots")])
        b.wait("tensor", psem, l3)
        if p >= 1:
            b.wait("tensor", "V", p4[(p - 1, "u")])
        for c2 in range(2):
            rhs = xnb4.ap()[:, sl + 512 * c2:sl + 512 * c2 + 512]
            b.plain("tensor", lambda e, o=bankB(c2), r=rhs:
                    e.matmul(o, I128b.ap(), r, start=True, stop=False))
        for c2 in range(2):
            rhs = slotR.ap()[0:66, 512 * c2:512 * c2 + 512]
            b.plain("tensor", lambda e, o=bankB(c2), r=rhs:
                    e.matmul(o, Wr2.ap(), r, start=False, stop=False))
        for c2 in range(2):
            rhs = slotI.ap()[0:66, 512 * c2:512 * c2 + 512]
            b.plain("tensor", lambda e, o=bankB(c2), r=rhs:
                    e.matmul(o, Wi2.ap(), r, start=False, stop=True))
        ty = b.seal("tensor", "T")
        b.wait("vector", "T", ty)
        b.wait("vector", psem, l3)
        if p >= 2:
            b.wait("vector", "p4s%d" % (p % 2), p4[(p - 2, "done")])
        b.plain("vector", lambda e, s=sl: e.tensor_tensor(
            ub4.ap(), psB.ap()[:, 0:1024], gp4.ap()[:, s:s + 1024],
            op=AluOpType.mult))
        b.plain("vector", lambda e: e.drain())
        b.plain("vector", lambda e, s=sl: e.tensor_tensor(
            ob4.ap()[:, s:s + 1024], ub4.ap(), xp4.ap()[:, s:s + 1024],
            op=AluOpType.add))
        p4[(p, "u")] = b.seal("vector", "V")
        b.wait("sync", "V", p4[(p, "u")])
        p4[(p, "done")] = b.inc("sync", lambda e, s=sl, p=p: e.dma_start(
            out=out_h[128 * p:128 * p + 128, :], in_=ob4.ap()[:, s:s + 1024]),
            "p4s%d" % (p % 2), 16)

    es4.close()
    esM.close()
    return _finish()


def kernel(L0_attn_hidden_states, L0_attn_q_w, L0_attn_k_w, L0_attn_v_w,
           L0_attn_o_w, L0_attn_dw_r, L0_attn_pw_r, L0_attn_dw_i, L0_attn_pw_i,
           L0_attn_gate_w, L0_attn_alpha_raw, L0_ffn_w_fc, L0_ffn_b_fc,
           L0_ffn_w_proj, L0_ffn_b_proj):
    x = np.ascontiguousarray(np.asarray(L0_attn_hidden_states, dtype=np.float32))
    qw = np.asarray(L0_attn_q_w, np.float32)
    kw = np.asarray(L0_attn_k_w, np.float32)
    vw = np.asarray(L0_attn_v_w, np.float32)
    ow = np.asarray(L0_attn_o_w, np.float32)
    gw = np.asarray(L0_attn_gate_w, np.float32)
    pwr = np.asarray(L0_attn_pw_r, np.float32)
    pwi = np.asarray(L0_attn_pw_i, np.float32)
    dwr = np.ascontiguousarray(np.asarray(L0_attn_dw_r, np.float32)[:, 0, :])
    dwi = np.ascontiguousarray(np.asarray(L0_attn_dw_i, np.float32)[:, 0, :])
    alpha = np.asarray(L0_attn_alpha_raw, np.float32)
    wfc = np.ascontiguousarray(np.asarray(L0_ffn_w_fc, np.float32))
    bfc = np.asarray(L0_ffn_b_fc, np.float32)
    wpj = np.ascontiguousarray(np.asarray(L0_ffn_w_proj, np.float32))
    bpj = np.asarray(L0_ffn_b_proj, np.float32)

    kw_rep = np.repeat(kw.reshape(4, 64, D), 4, axis=0).reshape(D, D)
    vw_rep = np.repeat(vw.reshape(4, 64, D), 4, axis=0).reshape(D, D)
    consts = make_consts()
    shared = {
        "qwT": np.ascontiguousarray(qw.T),
        "kwT": np.ascontiguousarray(kw_rep.T),
        "vwT": np.ascontiguousarray(vw_rep.T),
        "owT": np.ascontiguousarray(ow.T),
        "gwT": np.ascontiguousarray(gw.T),
        "pwrT": np.ascontiguousarray(pwr.T),
        "pwiT": np.ascontiguousarray(pwi.T),
        "dwr": dwr, "dwi": dwi,
        "alpha128": np.ascontiguousarray(np.repeat(alpha.reshape(1, 1), 128, 0)),
        "bfc": np.ascontiguousarray(bfc.reshape(1, HID)),
        "bproj": np.ascontiguousarray(bpj.reshape(1, D)),
        "wfc": wfc, "wproj": wpj,
        **consts,
    }

    nc = build_program()
    in_maps = [dict(shared, x=x[bb]) for bb in range(8)]
    res = run_bass_kernel_spmd(nc, in_maps, core_ids=list(range(8)))
    return np.stack([res.results[i]["out"] for i in range(8)], axis=0)
